# version 07
# baseline (speedup 1.0000x reference)
import math
from contextlib import ExitStack
import numpy as np
import ml_dtypes
import concourse.bass as bass
import concourse.mybir as mybir

F32 = mybir.dt.float32
BF16 = mybir.dt.bfloat16
I32 = mybir.dt.int32
U32 = mybir.dt.uint32
AF = mybir.ActivationFunctionType
ALU = mybir.AluOpType
AX = mybir.AxisListType
BF16NP = ml_dtypes.bfloat16

D = 2048
S = 4096
H = 16
HD = 128
II = 8192
NT = S // 128
NDC = D // 128
EPS = 1e-5
MC = 30.0
RSQ_HD = 1.0 / math.sqrt(HD)
CLIP = -120.0

SLOPES = [2.0 ** (-(i + 1) / 2.0) for i in range(H)]
TRUNC = 120.0

def head_window_tiles(h):
    w = TRUNC / SLOPES[h] + 512.0
    return min(int(math.ceil(w / 128.0)), NT)

BIASVEC_HEADS = set(h for h in range(H) if SLOPES[h] * 512.0 <= 32.0)


class Sem:
    __slots__ = ("h", "val")
    def __init__(self, h):
        self.h = h
        self.val = 0


class P:
    def __init__(self, nc, ctx):
        self.nc = nc
        self.ctx = ctx
        self._scopes = [ctx]
        self._sems = {}
    def sem(self, name):
        if name not in self._sems:
            self._sems[name] = Sem(self.ctx.enter_context(self.nc.semaphore(name)))
        return self._sems[name]
    def sbuf(self, name, shape, dtype):
        return self._scopes[-1].enter_context(self.nc.sbuf_tensor(name, shape, dtype))
    def psum(self, name, shape, dtype=F32):
        return self._scopes[-1].enter_context(self.nc.psum_tensor(name, shape, dtype))
    from contextlib import contextmanager as _cm
    @_cm
    def scope(self):
        es = ExitStack()
        self._scopes.append(es)
        try:
            yield es
        finally:
            self._scopes.pop()
            es.close()
    def push_scope(self):
        self._scopes.append(ExitStack())
    def pop_scope(self):
        es = self._scopes.pop()
        es.close()



def _gate(eng, ready):
    for s_, v_ in ready:
        eng.wait_ge(s_.h, v_)


def emit_const_load(p, cb, dram):
    nc = p.nc
    s = p.sem("const_ld")
    with nc.Block("constld") as blk:
        @blk.sync
        def _(sy):
            for name, tile in cb.items():
                sy.dma_start(out=tile[:], in_=dram[name].ap()).then_inc(s.h, 16)
                s.val += 16
            sy.wait_ge(s.h, s.val)


def emit_ln(p, T, src_tm, dst_fm, cb, tag, dst_tm=None, ready=(), stag="LN",
            rs_dst=None):
    nc = p.nc
    p.push_scope()
    n = T // 128
    ld = [p.sem(stag + "ld0"), p.sem(stag + "ld1")]
    acs = p.sem(stag + "ac")
    vvr = p.sem(stag + "vv")
    art = p.sem(stag + "ar")
    vnm = p.sem(stag + "vn")
    vrs = p.sem(stag + "vr")
    ttp = p.sem(stag + "tt")
    vcp = p.sem(stag + "vc")
    sst = p.sem(stag + "ss")
    vsw = p.sem(stag + "sw")

    xb = [p.sbuf(tag + f"xb{k}", [128, 2048], F32) for k in range(4)]
    ln = [p.sbuf(tag + f"ln{k}", [128, 2048], BF16) for k in range(4)]
    dmp = p.sbuf(tag + "dmp", [128, 2048], F32)
    st = p.sbuf(tag + "st", [128, 32], F32)
    rsb = p.sbuf(tag + "rsb", [128, 4], F32)
    fm = [p.sbuf(tag + f"fm{k}", [128, 2048], BF16) for k in range(4)]
    tp = [p.psum(tag + f"tp{k}", [128, 1024], BF16) for k in range(2)]
    ident, miscf = cb["ident"], cb["miscf"]

    n_store = 16 + (1 if dst_tm is not None else 0) + (1 if rs_dst is not None else 0)

    with nc.Block(tag) as blk:
        @blk.sync
        def _(sy):
            _gate(sy, ready)
            for i in range(n):
                if i >= 4:
                    sy.wait_ge(vnm.h, vnm.val + i - 3)
                    sy.wait_ge(acs.h, acs.val + 2 * (i - 3))
                sy.dma_start(out=xb[i % 4][:], in_=src_tm[i * 128:(i + 1) * 128, :]) \
                    .then_inc(ld[i % 2].h, 16)

        @blk.scalar
        def _(sc):
            _gate(sc, ready)
            def sums(i):
                sl = i % 4
                sc.wait_ge(ld[i % 2].h, ld[i % 2].val + 16 * (i // 2 + 1))
                if i >= 4:
                    sc.wait_ge(vvr.h, vvr.val + i - 3)
                sc.activation(out=dmp[:], in_=xb[sl][:], func=AF.Identity,
                              accum_out=st[:, 8 * sl:8 * sl + 1]).then_inc(acs.h, 1)
                sc.activation(out=dmp[:], in_=xb[sl][:], func=AF.Square,
                              accum_out=st[:, 8 * sl + 2:8 * sl + 3]).then_inc(acs.h, 1)
            def rstd(j):
                sl = j % 4
                sc.wait_ge(vvr.h, vvr.val + j + 1)
                sc.activation(out=st[:, 8 * sl + 5:8 * sl + 6], in_=st[:, 8 * sl + 4:8 * sl + 5],
                              func=AF.Ln, bias=miscf[:, 0:1])
                sc.activation(out=st[:, 8 * sl + 6:8 * sl + 7], in_=st[:, 8 * sl + 5:8 * sl + 6],
                              func=AF.Exp, scale=-0.5).then_inc(art.h, 1)
            for i in range(n):
                sums(i)
                if i >= 1:
                    rstd(i - 1)
            rstd(n - 1)

        @blk.vector
        def _(v):
            _gate(v, ready)
            def stats(i):
                sl = i % 4
                v.wait_ge(acs.h, acs.val + 2 * (i + 1))
                if i >= 4:
                    v.wait_ge(art.h, art.val + i - 3)
                    v.wait_ge(vnm.h, vnm.val + i - 3)
                v.tensor_scalar_mul(st[:, 8 * sl + 3:8 * sl + 4], st[:, 8 * sl:8 * sl + 1], 1.0 / D)
                v.tensor_scalar_mul(st[:, 8 * sl + 4:8 * sl + 5], st[:, 8 * sl + 2:8 * sl + 3], 1.0 / D) \
                    .then_inc(vsw.h, 1)
                vsw.val += 1
                v.wait_ge(vsw.h, vsw.val)
                v.tensor_mul(st[:, 8 * sl + 7:8 * sl + 8], st[:, 8 * sl + 3:8 * sl + 4],
                             st[:, 8 * sl + 3:8 * sl + 4]).then_inc(vsw.h, 1)
                vsw.val += 1
                v.wait_ge(vsw.h, vsw.val)
                v.tensor_sub(st[:, 8 * sl + 4:8 * sl + 5], st[:, 8 * sl + 4:8 * sl + 5],
                             st[:, 8 * sl + 7:8 * sl + 8]).then_inc(vvr.h, 1)
            def norm(j):
                sl = j % 4
                v.wait_ge(art.h, art.val + j + 1)
                if j >= 4:
                    v.wait_ge(ttp.h, ttp.val + 16 * (j - 3))
                    v.wait_ge(sst.h, sst.val + 16 * n_store * (j - 3))
                v.tensor_scalar(out=ln[sl][:], in0=xb[sl][:],
                                scalar1=st[:, 8 * sl + 3:8 * sl + 4],
                                scalar2=st[:, 8 * sl + 6:8 * sl + 7],
                                op0=ALU.subtract, op1=ALU.mult).then_inc(vnm.h, 1)
                if rs_dst is not None:
                    v.reduce_sum(rsb[:, sl:sl + 1], ln[sl][:], axis=AX.X).then_inc(vrs.h, 1)
            def copies(j):
                sl = j % 4
                for b in range(2):
                    gb = 2 * j + b
                    v.wait_ge(ttp.h, ttp.val + 8 * (gb + 1))
                    v.tensor_copy(out=fm[sl][:, 1024 * b:1024 * (b + 1)],
                                  in_=tp[gb % 2][:]).then_inc(vcp.h, 1)
            for i in range(n):
                stats(i)
                if i >= 1:
                    norm(i - 1)
                if i >= 3:
                    copies(i - 3)
            norm(n - 1)
            copies(n - 3)
            copies(n - 2)
            copies(n - 1)

        @blk.tensor
        def _(t):
            _gate(t, ready)
            for i in range(n):
                sl = i % 4
                t.wait_ge(vnm.h, vnm.val + i + 1)
                for k in range(16):
                    m = 16 * i + k
                    gb = m // 8
                    if k % 8 == 0 and gb >= 2:
                        t.wait_ge(vcp.h, vcp.val + gb - 1)
                    t.transpose(tp[gb % 2][:, 128 * (k % 8):128 * (k % 8 + 1)],
                                ln[sl][:, 128 * k:128 * (k + 1)], ident[:]) \
                        .then_inc(ttp.h, 1)

        @blk.gpsimd
        def _(g):
            _gate(g, ready)
            for i in range(n):
                sl = i % 4
                g.wait_ge(vcp.h, vcp.val + 2 * (i + 1))
                for k in range(16):
                    g.dma_start(out=dst_fm[128 * k:128 * (k + 1), 128 * i:128 * (i + 1)],
                                in_=fm[sl][:, 128 * k:128 * (k + 1)]).then_inc(sst.h, 16)
                if dst_tm is not None:
                    g.dma_start(out=dst_tm[128 * i:128 * (i + 1), :], in_=ln[sl][:]) \
                        .then_inc(sst.h, 16)
                if rs_dst is not None:
                    g.wait_ge(vrs.h, vrs.val + i + 1)
                    g.dma_start(out=rs_dst[128 * i:128 * (i + 1), 0:1],
                                in_=rsb[:, sl:sl + 1]).then_inc(sst.h, 16)
            g.wait_ge(sst.h, sst.val + 16 * n_store * n)

    p.pop_scope()
    ld[0].val += 16 * ((n + 1) // 2)
    ld[1].val += 16 * (n // 2)
    acs.val += 2 * n
    vvr.val += n
    art.val += n
    vnm.val += n
    if rs_dst is not None:
        vrs.val += n
    ttp.val += 16 * n
    vcp.val += 2 * n
    sst.val += 16 * n_store * n
    sems = [ld[0], ld[1], acs, vvr, art, vnm, vrs, ttp, vcp, sst]
    return [(s, s.val) for s in sems]



def emit_brows(p, dram_t, tag):
    nc = p.nc
    rows = [p.sbuf(tag + f"br{m}", [1, D], BF16) for m in range(4)]
    stg = p.sbuf(tag + "brs", [1, D], F32)
    s1 = p.sem("BRl")
    s2 = p.sem("BRc")
    with nc.Block(tag + "brows") as blk:
        @blk.sync
        def _(sy):
            for m in range(4):
                sy.wait_ge(s2.h, s2.val + m)
                sy.dma_start(out=stg[:], in_=dram_t.ap()[m:m + 1, :]).then_inc(s1.h, 16)
                s1.val += 16
                sy.wait_ge(s1.h, s1.val)
        @blk.vector
        def _(v):
            for m in range(4):
                v.wait_ge(s1.h, s1.val - 16 * (4 - m) + 16 * 1)
                v.tensor_copy(out=rows[m][:], in_=stg[:]).then_inc(s2.h, 1)
    s2.val += 4
    return rows, [(s1, s1.val), (s2, s2.val)]


class WeightPrep:
    def __init__(self, p, dram, scr, cb):
        self.p = p
        self.ld = p.sem("wp_ld")
        self.dq = p.sem("wp_dq")
        self.st = p.sem("wp_st")
        self.raw = [p.sbuf(f"wp_raw{k}", [128, 2048], BF16) for k in range(2)]
        self.deq = [p.sbuf(f"wp_deq{k}", [128, 2048], BF16) for k in range(2)]
        self.chunks = []
        self.done = 0
        self.pending = []
        self.dqn = 0
        for name in ("gate", "up"):
            src = dram[name + "_w"].ap()
            dst = scr[name + "_scr"].ap()
            for r in range(II // 128):
                self.chunks.append(("dq2", src[128 * r:128 * (r + 1), :], None,
                                    dst[128 * r:128 * (r + 1), :]))
        src = dram["down_w"].ap()
        dst = scr["down_scr"].ap()
        for r in range(D // 128):
            for cc in range(4):
                self.chunks.append(("dq2", src[128 * r:128 * (r + 1), 2048 * cc:2048 * (cc + 1)],
                                    None,
                                    dst[128 * r:128 * (r + 1), 2048 * cc:2048 * (cc + 1)]))
        self.n_dq2 = len(self.chunks)
        for mi, name in enumerate(("wb_q", "wb_k", "wb_v", "wb_o")):
            src = dram[name].ap()
            dst = scr["wb_scr"].ap()
            for r in range(D // 128):
                self.chunks.append(("c", src[128 * r:128 * (r + 1), :], None,
                                    dst[2048 * mi + 128 * r:2048 * mi + 128 * (r + 1), :]))
        self.total = len(self.chunks)
        self.n_gateup = 2 * (II // 128)
        self.n_down = self.n_gateup + 4 * (D // 128)
        self.n_wb = self.n_down + 4 * (D // 128)

    def emit(self, g, n):
        for _ in range(n):
            if self.done >= self.total:
                return
            kind, src, scale, dst = self.chunks[self.done]
            k = self.done % 2
            if self.done >= 2:
                g.wait_ge(self.st.h, 16 * (self.done - 1))
                if self.done - 2 < self.n_dq2:
                    g.wait_ge(self.dq.h, min(self.done - 1, self.n_dq2))
            g.dma_start(out=self.raw[k][:], in_=src).then_inc(self.ld.h, 16)
            if kind == "dq2":
                self.pending.append(self.done)
                self.dqn += 1
                g.wait_ge(self.dq.h, self.dqn)
                g.dma_start(out=dst, in_=self.deq[k][:]).then_inc(self.st.h, 16)
            else:
                g.wait_ge(self.ld.h, 16 * (self.done + 1))
                g.dma_start(out=dst, in_=self.raw[k][:]).then_inc(self.st.h, 16)
            self.done += 1
        self.ld.val = 16 * self.done
        self.st.val = 16 * self.done

    def emit_dve_n(self, v, n):
        for _ in range(n):
            if not self.pending:
                return
            i = self.pending.pop(0)
            k = i % 2
            v.wait_ge(self.ld.h, 16 * (i + 1))
            if i >= 2:
                v.wait_ge(self.st.h, 16 * (i - 1))
            v.tensor_scalar(out=self.deq[k][:], in0=self.raw[k][:],
                            scalar1=128.0, scalar2=1.0,
                            op0=ALU.subtract, op1=ALU.mult).then_inc(self.dq.h, 1)

    def emit_dve_rest(self, v):
        self.emit_dve_n(v, len(self.pending))

    def emit_rest(self, g):
        self.emit(g, self.total - self.done)


def emit_qkv(p, T, g, mode, wdram, wb_scr_ap, xln_fm, q_tm, bufs, cb, prep, tag, ready=(), stag="QK"):
    nc = p.nc
    p.push_scope()
    ntb = T // 512
    Qh, Kh, Vt = bufs["Qh"], bufs["Kh"], bufs["Vt"]
    xsb = bufs["xsb"]
    wqs, wks, wvs = bufs["wqs"], bufs["wks"], bufs["wvs"]
    brows = bufs["brow_a" if mode == "a" else "brow_b"]
    onesb = cb["onesb"]

    wld = p.sem(stag + "wl")
    xld = [p.sem(stag + "xl0"), p.sem(stag + "xl1")]
    peg = p.sem(stag + "pg")
    eva = p.sem(stag + "ea")
    evv = p.sem(stag + "ev")
    qst = p.sem(stag + "qs")

    qps = [p.psum(tag + "qps0", [128, 512 if mode == "a" else 256]),
           p.psum(tag + "qps1", [128, 512 if mode == "a" else 256])]
    kps = [p.psum(tag + "kps0", [128, 512]), p.psum(tag + "kps1", [128, 512])]
    vps = [p.psum(tag + "vps0", [128, 256]), p.psum(tag + "vps1", [128, 256])]
    qsb = [p.sbuf(tag + f"qsb{k}", [128, 256], BF16) for k in range(2)] if mode == "b" else None

    groups = []
    for tb in range(ntb):
        if mode == "a":
            groups += [("qa", tb, hh) for hh in range(2)]
        else:
            groups += [("qb", tb, tt) for tt in range(4)]
        groups += [("k", tb, hh) for hh in range(2)]
        groups += [("v", tb, tt) for tt in range(4)]
    gpt = len(groups) // ntb

    def psum_of(kind, idx):
        if kind in ("qa", "qb"):
            return ("q" + str(idx % 2), qps[idx % 2])
        if kind == "k":
            return ("k" + str(idx % 2), kps[idx % 2])
        return ("v" + str(idx % 2), vps[idx % 2])

    evac_eng = {"qa": "a", "k": "a", "qb": "a", "v": "v"}
    eva_cnt, evv_cnt = [], []
    ca = cv = 0
    for (kind, tb, idx) in groups:
        if evac_eng[kind] == "a":
            ca += 1
        else:
            cv += 1
        eva_cnt.append(ca)
        evv_cnt.append(cv)

    n_w_dma = 48

    with nc.Block(tag) as blk:
        @blk.gpsimd
        def _(g_):
            _gate(g_, ready)
            if mode == "a":
                for mi, wn in enumerate(("aw_q", "aw_k", "aw_v")):
                    dstt = (wqs, wks, wvs)[mi]
                    w = wdram[wn].ap()
                    for dc in range(16):
                        g_.dma_start(out=dstt[:, 256 * dc:256 * (dc + 1)],
                                     in_=w[128 * dc:128 * (dc + 1), 256 * g:256 * (g + 1)]) \
                            .then_inc(wld.h, 16)
            prep.emit(g_, 3)

        @blk.sync
        def _(sy):
            _gate(sy, ready)
            if mode == "b":
                sy.wait_ge(prep.st.h, 16 * prep.n_wb)
                for mi in range(3):
                    dstt = (wqs, wks, wvs)[mi]
                    for dc in range(16):
                        sy.dma_start(out=dstt[:, 256 * dc:256 * (dc + 1)],
                                     in_=wb_scr_ap[2048 * mi + 256 * g:2048 * mi + 256 * (g + 1),
                                                   128 * dc:128 * (dc + 1)],
                                     transpose=True).then_inc(wld.h, 16)
            for tb in range(ntb):
                if tb >= 2:
                    sy.wait_ge(peg.h, peg.val + gpt * (tb - 1))
                for dc in range(16):
                    sy.dma_start(out=xsb[tb % 2][:, 512 * dc:512 * (dc + 1)],
                                 in_=xln_fm[128 * dc:128 * (dc + 1), 512 * tb:512 * (tb + 1)]) \
                        .then_inc(xld[tb % 2].h, 16)

        @blk.tensor
        def _(t):
            _gate(t, ready)
            slot_free = {}
            t.wait_ge(wld.h, wld.val + 16 * n_w_dma)
            for i, (kind, tb, idx) in enumerate(groups):
                sl = tb % 2
                if idx == 0 and kind in ("qa", "qb"):
                    t.wait_ge(xld[sl].h, xld[sl].val + 16 * 16 * (tb // 2 + 1))
                sname, ps = psum_of(kind, idx)
                if sname in slot_free:
                    sem, cnt = slot_free[sname]
                    t.wait_ge(sem.h, cnt)
                if kind == "qa":
                    hh = idx
                    for dc in range(16):
                        t.matmul(ps[:], wqs[:, 256 * dc + 128 * hh:256 * dc + 128 * (hh + 1)],
                                 xsb[sl][:, 512 * dc:512 * (dc + 1)],
                                 start=(dc == 0), stop=False)
                    t.matmul(ps[:], brows[0][0:1, 256 * g + 128 * hh:256 * g + 128 * (hh + 1)],
                             onesb[0:1, 0:512], start=False, stop=True).then_inc(peg.h, 1)
                elif kind == "k":
                    hh = idx
                    for dc in range(16):
                        t.matmul(ps[:], wks[:, 256 * dc + 128 * hh:256 * dc + 128 * (hh + 1)],
                                 xsb[sl][:, 512 * dc:512 * (dc + 1)],
                                 start=(dc == 0), stop=False)
                    t.matmul(ps[:], brows[1][0:1, 256 * g + 128 * hh:256 * g + 128 * (hh + 1)],
                             onesb[0:1, 0:512], start=False, stop=True).then_inc(peg.h, 1)
                elif kind == "qb":
                    tt = idx
                    for dc in range(16):
                        t.matmul(ps[:], xsb[sl][:, 512 * dc + 128 * tt:512 * dc + 128 * (tt + 1)],
                                 wqs[:, 256 * dc:256 * (dc + 1)], start=(dc == 0), stop=False)
                    t.matmul(ps[:], onesb[0:1, 0:128],
                             brows[0][0:1, 256 * g:256 * (g + 1)], start=False, stop=True) \
                        .then_inc(peg.h, 1)
                else:
                    tt = idx
                    for dc in range(16):
                        t.matmul(ps[:], xsb[sl][:, 512 * dc + 128 * tt:512 * dc + 128 * (tt + 1)],
                                 wvs[:, 256 * dc:256 * (dc + 1)], start=(dc == 0), stop=False)
                    t.matmul(ps[:], onesb[0:1, 0:128],
                             brows[2][0:1, 256 * g:256 * (g + 1)], start=False, stop=True) \
                        .then_inc(peg.h, 1)
                if evac_eng[kind] == "a":
                    slot_free[sname] = (eva, eva.val + eva_cnt[i])
                else:
                    slot_free[sname] = (evv, evv.val + evv_cnt[i])

        @blk.scalar
        def _(sc):
            _gate(sc, ready)
            for i, (kind, tb, idx) in enumerate(groups):
                if evac_eng[kind] != "a":
                    continue
                sname, ps = psum_of(kind, idx)
                sc.wait_ge(peg.h, peg.val + i + 1)
                if kind == "qa":
                    sc.activation(out=Qh[idx][:, 512 * tb:512 * (tb + 1)], in_=ps[:],
                                  func=AF.Identity, scale=RSQ_HD).then_inc(eva.h, 1)
                elif kind == "k":
                    sc.activation(out=Kh[idx][:, 512 * tb:512 * (tb + 1)], in_=ps[:],
                                  func=AF.Identity).then_inc(eva.h, 1)
                else:
                    qn = 4 * tb + idx
                    if qn >= 2:
                        sc.wait_ge(qst.h, qst.val + 16 * (qn - 1))
                    sc.activation(out=qsb[qn % 2][:], in_=ps[:],
                                  func=AF.Identity, scale=RSQ_HD).then_inc(eva.h, 1)
                    sc.wait_ge(eva.h, eva.val + eva_cnt[i])
                    sc.dma_start(out=q_tm[128 * qn:128 * (qn + 1), :], in_=qsb[qn % 2][:]) \
                        .then_inc(qst.h, 16)

        @blk.vector
        def _(v):
            _gate(v, ready)
            for i, (kind, tb, idx) in enumerate(groups):
                if evac_eng[kind] != "v":
                    continue
                sname, ps = psum_of(kind, idx)
                v.wait_ge(peg.h, peg.val + i + 1)
                kt = 4 * tb + idx
                v.tensor_copy(out=Vt[:, 256 * kt:256 * (kt + 1)], in_=ps[:]).then_inc(evv.h, 1)

            prep.emit_dve_rest(v)
    p.pop_scope()
    wld.val += 16 * n_w_dma
    for tb in range(ntb):
        xld[tb % 2].val += 16 * 16
    peg.val += len(groups)
    eva.val += eva_cnt[-1]
    evv.val += evv_cnt[-1]
    if mode == "b":
        qst.val += 16 * 4 * ntb
    sems = [wld, xld[0], xld[1], peg, eva, evv, qst]
    return [(s, s.val) for s in sems]


def build_inner_sched(mode, g, colmaps):
    qcs = []
    base_s = base_act = base_a = base_av = 0
    qcidx = 0
    nqc = 8 if mode == "a" else 4
    for hh in range(2):
        h = 2 * g + hh
        for c in range(nqc):
            if mode == "a":
                wt = head_window_tiles(h)
                kstart = max(0, 4 * c + 4 - wt)
                ks = list(range(kstart, 4 * c + 4))
                flags = []
                for kt in ks:
                    diag = kt >= 4 * c
                    use_abase = h not in BIASVEC_HEADS
                    dveadd = use_abase or diag
                    mask_e = (kt - 4 * c) if diag else None
                    if h in BIASVEC_HEADS:
                        bcol = colmaps["bv"][(h, 4 * c - kt + 3)]
                    else:
                        bcol = colmaps["be"][(h, kt - 4 * c)]
                    flags.append((dveadd, use_abase, mask_e, bcol))
            else:
                ks = list(range(0, 8 * c + 8))
                flags = []
                for kt in ks:
                    r = kt - 8 * c
                    if r >= 0:
                        flags.append((True, False, r, None))
                    else:
                        flags.append((False, False, None, None))
            cnt_a = []
            ca = 0
            for f in flags:
                if f[0]:
                    ca += 1
                cnt_a.append(ca)
            qcs.append(dict(hh=hh, h=2 * g + hh, c=c, ks=ks, flags=flags,
                            cnt_a=cnt_a, base_s=base_s, base_act=base_act,
                            base_a=base_a, base_av=base_av, qcidx=qcidx))
            base_s += len(ks)
            base_act += len(ks)
            base_a += ca
            base_av += len(ks)
            qcidx += 1
    return qcs, base_s, base_act, base_a, base_av, qcidx


def emit_attn_inner(p, mode, g, bufs, cb, prep, o_scr, tag, ready=(), stag="AI"):
    nc = p.nc
    p.push_scope()
    Qh, Kh, Vt = bufs["Qh"], bufs["Kh"], bufs["Vt"]
    colmaps = bufs.get("colmaps")
    qcs, tot_s, tot_act, tot_a, tot_av, tot_qc = build_inner_sched(mode, g, colmaps)

    pes = p.sem(stag + "ps")
    actp = p.sem(stag + "ap")
    dvea = p.sem(stag + "da")
    peav = p.sem(stag + "av")
    dverc = p.sem(stag + "rc")
    aisw = p.sem(stag + "sw")
    per = p.sem(stag + "pr")
    dveo = p.sem(stag + "do")
    sto = p.sem(stag + "so")

    sps = [p.psum(tag + f"sps{k}", [128, 512]) for k in range(4)]
    sums = p.psum(tag + "sums", [1, 512])
    ops = [p.psum(tag + "ops0", [128, 512]), p.psum(tag + "ops1", [128, 512])]
    rps = p.psum(tag + "rps", [128, 512])
    Tsb = [p.sbuf(tag + f"tsb{k}", [128, 512], F32) for k in range(4)]
    Psb = [p.sbuf(tag + f"psb{k}", [128, 512], BF16) for k in range(4)]
    rcs = p.sbuf(tag + "rcs", [1, 512], F32)
    Rsb = p.sbuf(tag + "rsb", [128, 512], F32)
    Osb = [p.sbuf(tag + f"osb{k}", [128, 512], BF16) for k in range(2)]
    onesb, onesf, miscf = cb["onesb"], cb["onesf"], cb["miscf"]
    abase = bufs.get("abase")
    amask = bufs.get("amask")
    bmask = bufs.get("bmask")
    abt = bufs.get("abt")

    with nc.Block(tag) as blk:
        @blk.tensor
        def _(t):
            _gate(t, ready)
            slot_free = {}
            for q in qcs:
                hh, c, ks, flags = q["hh"], q["c"], q["ks"], q["flags"]
                n = len(ks)
                qsl = 512 * c
                Qsrc = Qh[hh]
                o_sl = "o" + str(q["qcidx"] % 2)

                def s_mm(j):
                    if j >= 4:
                        t.wait_ge(actp.h, actp.val + q["base_act"] + j - 3)
                    t.matmul(sps[j % 4][:], Kh[hh][:, 128 * ks[j]:128 * (ks[j] + 1)],
                             Qsrc[:, qsl:qsl + 512], start=True, stop=True) \
                        .then_inc(pes.h, 1)

                def sums_av(j):
                    t.wait_ge(actp.h, actp.val + q["base_act"] + j + 1)
                    t.matmul(sums[:], onesb[:, 0:1], Psb[j % 4][:],
                             start=(j == 0), stop=(j == n - 1))
                    if j == 0 and o_sl in slot_free:
                        sem, cnt = slot_free[o_sl]
                        t.wait_ge(sem.h, cnt)
                    t.matmul(ops[q["qcidx"] % 2][:],
                             Vt[:, 256 * ks[j] + 128 * hh:256 * ks[j] + 128 * (hh + 1)],
                             Psb[j % 4][:], start=(j == 0), stop=(j == n - 1)) \
                        .then_inc(peav.h, 1)

                s_mm(0)
                s_mm(1) if n > 1 else None
                s_mm(2) if n > 2 else None
                for j in range(3, n):
                    s_mm(j)
                    sums_av(j - 3)
                for j in range(max(0, n - 3), n):
                    sums_av(j)
                slot_free[o_sl] = (dveo, dveo.val + q["qcidx"] + 1)
                t.wait_ge(dverc.h, dverc.val + q["qcidx"] + 1)
                if "r" in slot_free:
                    sem, cnt = slot_free["r"]
                    t.wait_ge(sem.h, cnt)
                t.matmul(rps[:], onesf[0:1, 0:128], rcs[0:1, :], start=True, stop=True) \
                    .then_inc(per.h, 1)
                slot_free["r"] = (dveo, dveo.val + q["qcidx"] + 1)

        @blk.scalar
        def _(sc):
            _gate(sc, ready)
            for q in qcs:
                n = len(q["ks"])
                for j, kt in enumerate(q["ks"]):
                    dveadd, use_abase, mask_e, bcol = q["flags"][j]
                    sc.wait_ge(pes.h, pes.val + q["base_s"] + j + 1)
                    if dveadd:
                        sc.wait_ge(dvea.h, dvea.val + q["base_a"] + q["cnt_a"][j])
                        src = Tsb[j % 4][:]
                    else:
                        src = sps[j % 4][:]
                    if j >= 4:
                        sc.wait_ge(peav.h, peav.val + q["base_av"] + j - 3)
                    if mode == "a":
                        bias_ap = abt[:, bcol:bcol + 1]
                    else:
                        bias_ap = miscf[:, 1:2]
                    sc.activation(out=Psb[j % 4][:], in_=src, func=AF.Exp,
                                  bias=bias_ap).then_inc(actp.h, 1)

        @blk.gpsimd
        def _(g_):
            _gate(g_, ready)
            for q in qcs:
                prep.emit(g_, 2)

        @blk.vector
        def _(v):
            _gate(v, ready)
            tslot_w = {}
            for q in qcs:
                n = len(q["ks"])
                for j, kt in enumerate(q["ks"]):
                    dveadd, use_abase, mask_e, bcol = q["flags"][j]
                    if not dveadd:
                        continue
                    v.wait_ge(pes.h, pes.val + q["base_s"] + j + 1)
                    sl = j % 4
                    if sl in tslot_w:
                        v.wait_ge(actp.h, tslot_w[sl])
                    ops_list = []
                    if use_abase:
                        ops_list.append(abase[:, 512 * q["h"]:512 * (q["h"] + 1)])
                    if mask_e is not None:
                        ops_list.append((amask if mode == "a" else bmask)
                                        [:, 512 * mask_e:512 * (mask_e + 1)])
                    for oi, opnd in enumerate(ops_list):
                        srcap = sps[sl][:] if oi == 0 else Tsb[sl][:]
                        inst = v.tensor_add(Tsb[sl][:], srcap, opnd)
                        if oi == len(ops_list) - 1:
                            inst.then_inc(dvea.h, 1)
                    tslot_w[sl] = actp.val + q["base_act"] + j + 1
                v.wait_ge(peav.h, peav.val + q["base_av"] + n)
                v.reciprocal(rcs[:], sums[:]).then_inc(dverc.h, 1)
                v.wait_ge(per.h, per.val + q["qcidx"] + 1)
                v.tensor_copy(Rsb[:], rps[:]).then_inc(aisw.h, 1)
                aisw.val += 1
                v.wait_ge(aisw.h, aisw.val)
                if q["qcidx"] >= 2:
                    v.wait_ge(sto.h, sto.val + 16 * (q["qcidx"] - 1))
                v.tensor_tensor(Osb[q["qcidx"] % 2][:], ops[q["qcidx"] % 2][:], Rsb[:],
                                op=ALU.mult).then_inc(dveo.h, 1)
                prep.emit_dve_n(v, 2)

        @blk.sync
        def _(sy):
            _gate(sy, ready)
            for q in qcs:
                sy.wait_ge(dveo.h, dveo.val + q["qcidx"] + 1)
                habs = q["h"]
                sy.dma_start(out=o_scr[128 * habs:128 * (habs + 1), 512 * q["c"]:512 * (q["c"] + 1)],
                             in_=Osb[q["qcidx"] % 2][:]).then_inc(sto.h, 16)


    p.pop_scope()
    pes.val += tot_s
    actp.val += tot_act
    dvea.val += tot_a
    peav.val += tot_av
    dverc.val += tot_qc
    per.val += tot_qc
    dveo.val += tot_qc
    sto.val += 16 * tot_qc
    sems = [pes, actp, dvea, peav, dverc, per, dveo, sto]
    return [(s, s.val) for s in sems]


def emit_qgather(p, q_tm, qidx_sb, bufs, cb, qst_sem, qst_val, tag, ready=(), stag="QG"):
    nc = p.nc
    p.push_scope()
    Qh = bufs["Qh"]
    gq = p.sem(stag + "gq")
    tq = p.sem(stag + "tq")
    cq = p.sem(stag + "cq")
    qg = [p.sbuf(tag + f"qg{k}", [128, 256], BF16) for k in range(2)]
    tpq = [p.psum(tag + f"tpq{k}", [128, 128], BF16) for k in range(2)]
    ident = cb["ident"]

    with nc.Block(tag) as blk:
        @blk.gpsimd
        def _(g_):
            _gate(g_, ready)
            g_.wait_ge(qst_sem.h, qst_val)
            from concourse.bass_core_rust import IndirectOffsetOnAxis
            for j in range(16):
                if j >= 2:
                    g_.wait_ge(tq.h, tq.val + 2 * (j - 1))
                g_.indirect_dma_start(
                    out=qg[j % 2][:, :], out_offset=None, in_=q_tm[:, :],
                    in_offset=IndirectOffsetOnAxis(ap=qidx_sb[:, j:j + 1], axis=0)) \
                    .then_inc(gq.h, 16)

        @blk.tensor
        def _(t):
            _gate(t, ready)
            for j in range(16):
                t.wait_ge(gq.h, gq.val + 16 * (j + 1))
                for u in range(2):
                    m = 2 * j + u
                    if m >= 2:
                        t.wait_ge(cq.h, cq.val + m - 1)
                    t.transpose(tpq[m % 2][:], qg[j % 2][:, 128 * u:128 * (u + 1)], ident[:]) \
                        .then_inc(tq.h, 1)

        @blk.vector
        def _(v):
            _gate(v, ready)
            for j in range(16):
                for u in range(2):
                    m = 2 * j + u
                    v.wait_ge(tq.h, tq.val + m + 1)
                    v.tensor_copy(out=Qh[u][:, 128 * j:128 * (j + 1)], in_=tpq[m % 2][:]) \
                        .then_inc(cq.h, 1)

    p.pop_scope()
    gq.val += 16 * 16
    tq.val += 32
    cq.val += 32
    return [(s, s.val) for s in (gq, tq, cq)]


def emit_oproj(p, Tq, mode, wdram, wb_scr_ap, o_scr, res_ap, x_dst, bufs, cb,
               prep, qidx_sb, tag, ready=(), stag="OP"):
    nc = p.nc
    p.push_scope()
    nt = Tq // 128
    wob = bufs["wob"]
    brows = bufs["brow_a" if mode == "a" else "brow_b"]
    onesb = cb["onesb"]

    wld = p.sem(stag + "wl")
    old_ = [p.sem(stag + "ol0"), p.sem(stag + "ol1")]
    xld = [p.sem(stag + "xl0"), p.sem(stag + "xl1")]
    peo = p.sem(stag + "po")
    dve = p.sem(stag + "dv")
    stx = p.sem(stag + "sx")

    otb = [p.sbuf(tag + f"otb{k}", [128, 2048], BF16) for k in range(2)]
    xrb = [p.sbuf(tag + f"xrb{k}", [128, 2048], F32) for k in range(2)]
    xnb = [p.sbuf(tag + f"xnb{k}", [128, 2048], F32) for k in range(2)]
    pso = [p.psum(tag + f"pso{k}", [128, 512]) for k in range(4)]

    with nc.Block(tag) as blk:
        @blk.gpsimd
        def _(g_):
            _gate(g_, ready)
            if mode == "a":
                w = wdram["aw_o"].ap()
                for dc in range(16):
                    g_.dma_start(out=wob[:, 2048 * dc:2048 * (dc + 1)],
                                 in_=w[128 * dc:128 * (dc + 1), :]).then_inc(wld.h, 16)
            if mode == "b":
                from concourse.bass_core_rust import IndirectOffsetOnAxis
                for i in range(nt):
                    if i >= 2:
                        g_.wait_ge(dve.h, dve.val + 4 * (i - 1))
                    g_.indirect_dma_start(
                        out=xrb[i % 2][:, :], out_offset=None, in_=res_ap[:, :],
                        in_offset=IndirectOffsetOnAxis(ap=qidx_sb[:, i:i + 1], axis=0)) \
                        .then_inc(xld[i % 2].h, 16)
            prep.emit(g_, 4)

        @blk.sync
        def _(sy):
            _gate(sy, ready)
            if mode == "b":
                sy.wait_ge(prep.st.h, 16 * prep.n_wb)
                for doc in range(4):
                    for dc in range(16):
                        sy.dma_start(out=wob[:, 2048 * dc + 512 * doc:2048 * dc + 512 * (doc + 1)],
                                     in_=wb_scr_ap[3 * 2048 + 512 * doc:3 * 2048 + 512 * (doc + 1),
                                                   128 * dc:128 * (dc + 1)],
                                     transpose=True).then_inc(wld.h, 16)
            for i in range(nt):
                if i >= 2:
                    sy.wait_ge(peo.h, peo.val + 4 * (i - 1))
                for dc in range(16):
                    sy.dma_start(out=otb[i % 2][:, 128 * dc:128 * (dc + 1)],
                                 in_=o_scr[128 * dc:128 * (dc + 1), 128 * i:128 * (i + 1)]) \
                        .then_inc(old_[i % 2].h, 16)

        @blk.scalar
        def _(sc):
            _gate(sc, ready)
            for i in range(nt):
                if mode == "a":
                    if i >= 2:
                        sc.wait_ge(dve.h, dve.val + 4 * (i - 1))
                    sc.dma_start(out=xrb[i % 2][:], in_=res_ap[128 * i:128 * (i + 1), :]) \
                        .then_inc(xld[i % 2].h, 16)
                if i >= 1:
                    sc.wait_ge(dve.h, dve.val + 4 * i)
                    sc.dma_start(out=x_dst[128 * (i - 1):128 * i, :], in_=xnb[(i - 1) % 2][:]) \
                        .then_inc(stx.h, 16)
            sc.wait_ge(dve.h, dve.val + 4 * nt)
            sc.dma_start(out=x_dst[128 * (nt - 1):128 * nt, :], in_=xnb[(nt - 1) % 2][:]) \
                .then_inc(stx.h, 16)
            sc.wait_ge(stx.h, stx.val + 16 * nt)

        @blk.tensor
        def _(t):
            _gate(t, ready)
            nw = 64 if mode == "b" else 16
            t.wait_ge(wld.h, wld.val + 16 * nw)
            for i in range(nt):
                sl = i % 2
                t.wait_ge(old_[sl].h, old_[sl].val + 16 * 16 * (i // 2 + 1))
                for doc in range(4):
                    if i >= 1:
                        t.wait_ge(dve.h, dve.val + 4 * (i - 1) + doc + 1)
                    for dc in range(16):
                        t.matmul(pso[doc][:], otb[sl][:, 128 * dc:128 * (dc + 1)],
                                 wob[:, 2048 * dc + 512 * doc:2048 * dc + 512 * (doc + 1)],
                                 start=(dc == 0), stop=False)
                    t.matmul(pso[doc][:], onesb[0:1, 0:128],
                             brows[3][0:1, 512 * doc:512 * (doc + 1)],
                             start=False, stop=True).then_inc(peo.h, 1)

        @blk.vector
        def _(v):
            _gate(v, ready)
            for i in range(nt):
                sl = i % 2
                v.wait_ge(xld[sl].h, xld[sl].val + 16 * (i // 2 + 1))
                if i >= 2:
                    v.wait_ge(stx.h, stx.val + 16 * (i - 1))
                for doc in range(4):
                    v.wait_ge(peo.h, peo.val + 4 * i + doc + 1)
                    v.tensor_add(xnb[sl][:, 512 * doc:512 * (doc + 1)], pso[doc][:],
                                 xrb[sl][:, 512 * doc:512 * (doc + 1)]).then_inc(dve.h, 1)

            prep.emit_dve_rest(v)
    p.pop_scope()
    wld.val += 16 * (64 if mode == "b" else 16)
    for i in range(nt):
        old_[i % 2].val += 16 * 16
        xld[i % 2].val += 16
    peo.val += 4 * nt
    dve.val += 4 * nt
    stx.val += 16 * nt
    sems = [wld, old_[0], old_[1], xld[0], xld[1], peo, dve, stx]
    return [(s, s.val) for s in sems]


UPTO_FFH = [False]


def emit_ffn(p, T, scr, xln_fm, xsrc, xdst, cb, prep, tag, ready=(), stag="FF"):
    nc = p.nc
    nH = T // 2048
    all_sems = []
    gate_scr, up_scr, down_scr, h_scr = (scr["gate_scr"].ap(), scr["up_scr"].ap(),
                                         scr["down_scr"].ap(), scr["h_scr"].ap())
    for HH in range(nH):
        tg = stag + "g"
        xhl = p.sem(tg + "xh")
        wldg = [p.sem(tg + "w0"), p.sem(tg + "w1")]
        peg = p.sem(tg + "pg")
        evg = p.sem(tg + "eg")
        evu = p.sem(tg + "eu")
        evh = p.sem(tg + "eh")
        sth = p.sem(tg + "sh")
        with p.scope():
            xh = p.sbuf(f"{tag}{HH}xh", [128, 16 * 2048], BF16)
            wgb = [p.sbuf(f"{tag}{HH}wg{k}", [128, 16 * 512], BF16) for k in range(2)]
            wub = [p.sbuf(f"{tag}{HH}wu{k}", [128, 16 * 512], BF16) for k in range(2)]
            gsb = [p.sbuf(f"{tag}{HH}gs{k}", [128, 2048], BF16) for k in range(2)]
            usb = p.sbuf(f"{tag}{HH}us", [128, 2048], BF16)
            hsb = [p.sbuf(f"{tag}{HH}hs{k}", [128, 2048], BF16) for k in range(2)]
            gps = p.psum(f"{tag}{HH}gps", [128, 2048])
            ups = p.psum(f"{tag}{HH}ups", [128, 2048])

            with nc.Block(f"{tag}{HH}g") as blk:
                @blk.sync
                def _(sy):
                    _gate(sy, ready)
                    if HH == 1:
                        pd = p.sem(stag + "d" + "sd")
                        dd = p.sem(stag + "d" + "dd")
                        sy.wait_ge(pd.h, pd.val)
                        sy.wait_ge(dd.h, dd.val)
                    if HH == 0:
                        sy.wait_ge(prep.st.h, 16 * prep.n_gateup)
                    for dc in range(16):
                        sy.dma_start(out=xh[:, 2048 * dc:2048 * (dc + 1)],
                                     in_=xln_fm[128 * dc:128 * (dc + 1),
                                                2048 * HH:2048 * (HH + 1)]) \
                            .then_inc(xhl.h, 16)
                    for ir in range(16):
                        if ir >= 2:
                            sy.wait_ge(peg.h, peg.val + 8 * (ir - 1))
                        for dc in range(16):
                            sy.dma_start(out=wgb[ir % 2][:, 512 * dc:512 * (dc + 1)],
                                         in_=gate_scr[512 * ir:512 * (ir + 1),
                                                      128 * dc:128 * (dc + 1)],
                                         transpose=True).then_inc(wldg[ir % 2].h, 16)
                        for dc in range(16):
                            sy.dma_start(out=wub[ir % 2][:, 512 * dc:512 * (dc + 1)],
                                         in_=up_scr[512 * ir:512 * (ir + 1),
                                                    128 * dc:128 * (dc + 1)],
                                         transpose=True).then_inc(wldg[ir % 2].h, 16)

                @blk.tensor
                def _(t):
                    _gate(t, ready)
                    t.wait_ge(xhl.h, xhl.val + 16 * 16)
                    for ic in range(64):
                        ir, u = ic // 4, ic % 4
                        if u == 0:
                            t.wait_ge(wldg[ir % 2].h, wldg[ir % 2].val + 16 * 32 * (ir // 2 + 1))
                        if ic >= 1:
                            t.wait_ge(evg.h, evg.val + ic)
                        for dc in range(16):
                            for tb in range(4):
                                inst = t.matmul(gps[:, 512 * tb:512 * (tb + 1)],
                                                wgb[ir % 2][:, 512 * dc + 128 * u:512 * dc + 128 * (u + 1)],
                                                xh[:, 2048 * dc + 512 * tb:2048 * dc + 512 * (tb + 1)],
                                                start=(dc == 0), stop=(dc == 15))
                        inst.then_inc(peg.h, 1)
                        if ic >= 1:
                            t.wait_ge(evu.h, evu.val + ic)
                        for dc in range(16):
                            for tb in range(4):
                                inst = t.matmul(ups[:, 512 * tb:512 * (tb + 1)],
                                                wub[ir % 2][:, 512 * dc + 128 * u:512 * dc + 128 * (u + 1)],
                                                xh[:, 2048 * dc + 512 * tb:2048 * dc + 512 * (tb + 1)],
                                                start=(dc == 0), stop=(dc == 15))
                        inst.then_inc(peg.h, 1)

                @blk.scalar
                def _(sc):
                    _gate(sc, ready)
                    for ic in range(64):
                        sc.wait_ge(peg.h, peg.val + 2 * ic + 1)
                        if ic >= 2:
                            sc.wait_ge(evh.h, evh.val + ic - 1)
                        sc.activation(out=gsb[ic % 2][:], in_=gps[:], func=AF.Silu,
                                      scale=cb["gsc"][:, ic:ic + 1]).then_inc(evg.h, 1)

                @blk.vector
                def _(v):
                    _gate(v, ready)
                    for ic in range(64):
                        v.wait_ge(peg.h, peg.val + 2 * ic + 2)
                        v.tensor_scalar_mul(usb[:], ups[:], cb["usc"][:, ic:ic + 1]) \
                            .then_inc(evu.h, 1)
                        v.wait_ge(evg.h, evg.val + ic + 1)
                        if ic >= 2:
                            v.wait_ge(sth.h, sth.val + 16 * (ic - 1))
                        v.tensor_mul(hsb[ic % 2][:], gsb[ic % 2][:], usb[:]) \
                            .then_inc(evh.h, 1)

                @blk.gpsimd
                def _(g_):
                    _gate(g_, ready)
                    for ic in range(64):
                        g_.wait_ge(evh.h, evh.val + ic + 1)
                        g_.dma_start(out=h_scr[128 * ic:128 * (ic + 1),
                                               2048 * HH:2048 * (HH + 1)],
                                     in_=hsb[ic % 2][:]).then_inc(sth.h, 16)

            xhl.val += 16 * 16
            for ir in range(16):
                wldg[ir % 2].val += 16 * 32
            peg.val += 128
            evg.val += 64
            evu.val += 64
            evh.val += 64
            sth.val += 16 * 64

        if UPTO_FFH[0] and HH == 0:
            break
        td = stag + "d"
        wldd = p.sem(td + "wl")
        hld = [p.sem(td + "h0"), p.sem(td + "h1")]
        xrl = [p.sem(td + "x0"), p.sem(td + "x1")]
        peB = p.sem(td + "pb")
        evA = p.sem(td + "ea")
        peT = p.sem(td + "pt")
        dvD = p.sem(td + "dd")
        stD = p.sem(td + "sd")
        quads = [(HF, tqp, hcp) for HF in range(4) for tqp in range(2)
                 for hcp in range(2)]
        with p.scope():
            wdb = [p.sbuf(f"{tag}{HH}wd{k}", [128, 64 * 512], BF16) for k in range(2)]
            hqb = [p.sbuf(f"{tag}{HH}hq{k}", [128, 8 * 1024], BF16) for k in range(2)]
            fsb = [p.sbuf(f"{tag}{HH}fs{k}", [128, 2048], BF16) for k in range(2)]
            xrb = [p.sbuf(f"{tag}{HH}xr{k}", [128, 1024], F32) for k in range(2)]
            xo4 = [p.sbuf(f"{tag}{HH}xo{k}", [128, 1024], F32) for k in range(2)]
            qd = p.psum(f"{tag}{HH}qd", [128, 2048])
            tps = [p.psum(f"{tag}{HH}tp{k}", [128, 1024], BF16) for k in range(2)]
            gsem = (p.sem(tg + "pg"), p.sem(tg + "eg"), p.sem(tg + "eu"),
                    p.sem(tg + "eh"), p.sem(tg + "sh"))

            with nc.Block(f"{tag}{HH}dn") as blk:
                @blk.sync
                def _(sy):
                    sy.wait_ge(gsem[0].h, gsem[0].val)
                    sy.wait_ge(gsem[3].h, gsem[3].val)
                    sy.wait_ge(gsem[4].h, gsem[4].val)
                    if HH == 0 and tag == "F1":
                        sy.wait_ge(prep.st.h, 16 * prep.n_down)
                    gi = 0
                    for HF in range(4):
                        if HF >= 2:
                            sy.wait_ge(peB.h, peB.val + 32 * (HF - 1))
                        for ic in range(64):
                            sy.dma_start(out=wdb[HF % 2][:, 512 * ic:512 * (ic + 1)],
                                         in_=down_scr[512 * HF:512 * (HF + 1),
                                                      128 * ic:128 * (ic + 1)],
                                         transpose=True).then_inc(wldd.h, 16)
                        for tqp in range(2):
                            for hcp in range(2):
                                for icb in range(8):
                                    if gi >= 2:
                                        sy.wait_ge(peB.h, peB.val + gi - 1)
                                    for k in range(8):
                                        ic = 8 * icb + k
                                        for tqj in range(2):
                                            tq = 2 * tqp + tqj
                                            sy.dma_start(
                                                out=hqb[gi % 2][:, 1024 * k + 512 * tqj:
                                                                1024 * k + 512 * (tqj + 1)],
                                                in_=h_scr[128 * ic:128 * (ic + 1),
                                                          2048 * HH + 512 * tq:
                                                          2048 * HH + 512 * (tq + 1)]) \
                                                .then_inc(hld[gi % 2].h, 16)
                                    gi += 1

                @blk.tensor
                def _(t):
                    t.wait_ge(gsem[1].h, gsem[1].val)
                    t.wait_ge(gsem[2].h, gsem[2].val)

                    def transposes(qg):
                        t.wait_ge(evA.h, evA.val + qg + 1)
                        for g in range(2):
                            gg = 2 * qg + g
                            if gg >= 2:
                                t.wait_ge(dvD.h, dvD.val + gg - 1)
                            for ttl in range(4):
                                for hcj in range(2):
                                    j2 = 2 * hcj + g
                                    inst = t.transpose(
                                        tps[gg % 2][:, 256 * ttl + 128 * hcj:
                                                    256 * ttl + 128 * (hcj + 1)],
                                        fsb[qg % 2][:, 512 * j2 + 128 * ttl:
                                                    512 * j2 + 128 * (ttl + 1)],
                                        cb["ident"][:])
                            inst.then_inc(peT.h, 1)

                    for qg in range(16):
                        HF, tqp, hcp = quads[qg]
                        if qg % 4 == 0:
                            t.wait_ge(wldd.h, wldd.val + 16 * 64 * (HF + 1))
                        if qg >= 1:
                            t.wait_ge(evA.h, evA.val + qg)
                        for icb in range(8):
                            gi = 8 * qg + icb
                            t.wait_ge(hld[gi % 2].h, hld[gi % 2].val + 16 * 16 * (gi // 2 + 1))
                            for k in range(8):
                                ic = 8 * icb + k
                                for hcj in range(2):
                                    hc = 2 * hcp + hcj
                                    for tqj in range(2):
                                        inst = t.matmul(
                                            qd[:, 512 * (2 * hcj + tqj):
                                               512 * (2 * hcj + tqj + 1)],
                                            wdb[HF % 2][:, 512 * ic + 128 * hc:
                                                        512 * ic + 128 * (hc + 1)],
                                            hqb[gi % 2][:, 1024 * k + 512 * tqj:
                                                        1024 * k + 512 * (tqj + 1)],
                                            start=(ic == 0), stop=(ic == 63))
                            inst.then_inc(peB.h, 1)
                        if qg >= 1:
                            transposes(qg - 1)
                    transposes(15)

                @blk.scalar
                def _(sc):
                    for qg in range(16):
                        HF, tqp, hcp = quads[qg]
                        for g in range(2):
                            gg = 2 * qg + g
                            if gg >= 2:
                                sc.wait_ge(dvD.h, dvD.val + gg - 1)
                            for ttl in range(4):
                                tq = 2 * tqp + g
                                row = 2048 * HH + 512 * tq + 128 * ttl
                                sc.dma_start(out=xrb[gg % 2][:, 256 * ttl:256 * (ttl + 1)],
                                             in_=xsrc[row:row + 128,
                                                      512 * HF + 256 * hcp:
                                                      512 * HF + 256 * (hcp + 1)]) \
                                    .then_inc(xrl[gg % 2].h, 16)
                        sc.wait_ge(peB.h, peB.val + 8 * (qg + 1))
                        for j2 in range(4):
                            hcj = j2 // 2
                            hcg = 4 * HF + 2 * hcp + hcj
                            inst = sc.activation(out=fsb[qg % 2][:, 512 * j2:512 * (j2 + 1)],
                                                 in_=qd[:, 512 * j2:512 * (j2 + 1)],
                                                 func=AF.Identity,
                                                 scale=cb["dsc"][:, hcg:hcg + 1])
                        inst.then_inc(evA.h, 1)
                        for g in range(2):
                            gg = 2 * qg + g
                            sc.wait_ge(dvD.h, dvD.val + gg + 1)
                            for ttl in range(4):
                                tq = 2 * tqp + g
                                row = 2048 * HH + 512 * tq + 128 * ttl
                                sc.dma_start(out=xdst[row:row + 128,
                                                      512 * HF + 256 * hcp:
                                                      512 * HF + 256 * (hcp + 1)],
                                             in_=xo4[gg % 2][:, 256 * ttl:256 * (ttl + 1)]) \
                                    .then_inc(stD.h, 16)
                    sc.wait_ge(stD.h, stD.val + 16 * 128)

                @blk.vector
                def _(v):
                    for qg in range(16):
                        for g in range(2):
                            gg = 2 * qg + g
                            v.wait_ge(peT.h, peT.val + gg + 1)
                            v.wait_ge(xrl[gg % 2].h, xrl[gg % 2].val + 16 * 4 * (gg // 2 + 1))
                            if gg >= 2:
                                v.wait_ge(stD.h, stD.val + 64 * (gg - 1))
                            v.tensor_add(xo4[gg % 2][:], tps[gg % 2][:], xrb[gg % 2][:]) \
                                .then_inc(dvD.h, 1)

            wldd.val += 16 * 64 * 4
            hld[0].val += 16 * 16 * 64
            hld[1].val += 16 * 16 * 64
            xrl[0].val += 16 * 4 * 16
            xrl[1].val += 16 * 4 * 16
            peB.val += 128
            evA.val += 16
            peT.val += 32
            dvD.val += 32
            stD.val += 16 * 128
        all_sems += [wldd, hld[0], hld[1], xrl[0], xrl[1], peB, evA, peT, dvD, stD]
    return [(s, s.val) for s in all_sems]


def build_host_tables(h_core):
    kk = np.arange(128)
    qq = np.arange(512)
    t = {}
    t["ident"] = np.eye(128).astype(BF16NP)
    t["onesb"] = np.ones((128, 512), BF16NP)
    t["onesf"] = np.ones((1, 128), np.float32)
    miscf = np.zeros((128, 8), np.float32)
    miscf[:, 0] = EPS
    miscf[:, 1] = -MC
    t["miscf"] = miscf
    ab = np.zeros((128, 7 * 512), np.float32)
    for h in range(7):
        ab[:, 512 * h:512 * (h + 1)] = SLOPES[h] * (kk[:, None] - qq[None, :])
    t["abase"] = ab
    am = np.zeros((128, 4 * 512), np.float32)
    for e in range(4):
        am[:, 512 * e:512 * (e + 1)] = np.where(128 * e + kk[:, None] > qq[None, :],
                                                -1.0e4, 0.0)
    t["amask"] = am
    cols = []
    bv_map, be_map = {}, {}
    for h in sorted(BIASVEC_HEADS):
        for d in range(32):
            bv_map[(h, d)] = len(cols)
            v = SLOPES[h] * (kk - 128.0 * d - 127.0)
            cols.append(np.maximum(v, CLIP) - MC)
    for h in range(H):
        if h in BIASVEC_HEADS:
            continue
        wt = head_window_tiles(h)
        emin = max(-28, 4 - wt)
        for e in range(emin, 4):
            be_map[(h, e)] = len(cols)
            cols.append(np.full(128, SLOPES[h] * 128.0 * e - MC))
    t["abt"] = np.stack(cols, axis=1).astype(np.float32)
    t["colmaps"] = {"bv": bv_map, "be": be_map}
    bm = np.zeros((128, 8 * 512), np.float32)
    s_of = qq // 128
    q128 = qq % 128
    for r in range(8):
        cond = (128 * r + kk[:, None]) > (128 * (h_core + 2 * s_of[None, :]) + q128[None, :])
        bm[:, 512 * r:512 * (r + 1)] = np.where(cond, -1.0e4, 0.0)
    t["bmask"] = bm
    qi = np.zeros((128, 16), np.uint32)
    for j in range(16):
        qi[:, j] = 128 * (2 * j + h_core) + np.arange(128)
    t["qidx"] = qi
    return t


def build_program(upto="full"):
    nc = bass.Bass("TRN2", target_bir_lowering=False)
    dram = {}
    dram["x_in"] = nc.dram_tensor("x_in", [S, D], F32, kind="ExternalInput")
    for n_ in ("aw_q", "aw_k", "aw_v", "aw_o", "wb_q", "wb_k", "wb_v", "wb_o"):
        dram[n_] = nc.dram_tensor(n_, [D, D], F32, kind="ExternalInput")
    dram["aa_b"] = nc.dram_tensor("aa_b", [4, D], F32, kind="ExternalInput")
    dram["ab_b"] = nc.dram_tensor("ab_b", [4, D], F32, kind="ExternalInput")
    dram["gate_w"] = nc.dram_tensor("gate_w", [II, D], I32, kind="ExternalInput")
    dram["up_w"] = nc.dram_tensor("up_w", [II, D], I32, kind="ExternalInput")
    dram["down_w"] = nc.dram_tensor("down_w", [D, II], I32, kind="ExternalInput")
    dram["gsc"] = nc.dram_tensor("gsc", [128, 64], F32, kind="ExternalInput")
    dram["usc"] = nc.dram_tensor("usc", [128, 64], F32, kind="ExternalInput")
    dram["dsc"] = nc.dram_tensor("dsc", [128, 16], F32, kind="ExternalInput")
    for n_, sh, dt_ in (("ident", [128, 128], BF16), ("onesb", [128, 512], BF16),
                        ("onesf", [1, 128], F32), ("miscf", [128, 8], F32),
                        ("abase", [128, 7 * 512], F32), ("amask", [128, 4 * 512], F32),
                        ("abt", [128, None], F32), ("bmask", [128, 8 * 512], F32),
                        ("qidx", [128, 16], U32)):
        pass
    NB = ABT_WIDTH[0]
    dram["ident"] = nc.dram_tensor("ident", [128, 128], BF16, kind="ExternalInput")
    dram["onesb"] = nc.dram_tensor("onesb", [128, 512], BF16, kind="ExternalInput")
    dram["onesf"] = nc.dram_tensor("onesf", [1, 128], F32, kind="ExternalInput")
    dram["miscf"] = nc.dram_tensor("miscf", [128, 8], F32, kind="ExternalInput")
    dram["abase"] = nc.dram_tensor("abase", [128, 7 * 512], F32, kind="ExternalInput")
    dram["amask"] = nc.dram_tensor("amask", [128, 4 * 512], F32, kind="ExternalInput")
    dram["abt"] = nc.dram_tensor("abt", [128, NB], F32, kind="ExternalInput")
    dram["bmask"] = nc.dram_tensor("bmask", [128, 8 * 512], F32, kind="ExternalInput")
    dram["qidx"] = nc.dram_tensor("qidx", [128, 16], U32, kind="ExternalInput")

    out = nc.dram_tensor("out", [2048, D], F32, kind="ExternalOutput")
    dbg = None
    if upto != "full":
        sh = [2048, D] if upto == "x3" else [S, D]
        dbg = nc.dram_tensor("dbg", sh, F32, kind="ExternalOutput")

    scr = {}
    scr["x1"] = nc.dram_tensor("x1", [S, D], F32, kind="Internal")
    scr["x2"] = nc.dram_tensor("x2", [S, D], F32, kind="Internal")
    scr["x3o"] = nc.dram_tensor("x3o", [2048, D], F32, kind="Internal")
    scr["xln_fm"] = nc.dram_tensor("xln_fm", [D, S], BF16, kind="Internal")
    scr["q_tm"] = nc.dram_tensor("q_tm", [S, 256], BF16, kind="Internal")
    scr["o_scr"] = nc.dram_tensor("o_scr", [D, S], BF16, kind="Internal")
    scr["gate_scr"] = nc.dram_tensor("gate_scr", [II, D], BF16, kind="Internal")
    scr["up_scr"] = nc.dram_tensor("up_scr", [II, D], BF16, kind="Internal")
    scr["down_scr"] = nc.dram_tensor("down_scr", [D, II], BF16, kind="Internal")
    scr["wb_scr"] = nc.dram_tensor("wb_scr", [4 * D, D], BF16, kind="Internal")
    scr["h_scr"] = nc.dram_tensor("h_scr", [II, S], BF16, kind="Internal")
    with ExitStack() as ctx:
        p = P(nc, ctx)
        cb = {}
        for n_ in ("ident", "onesb", "onesf", "miscf", "gsc", "usc", "dsc", "qidx"):
            shp = dram[n_].shape
            dt_ = dram[n_].dtype
            cb[n_] = p.sbuf("c_" + n_, list(shp), dt_)
        cld = p.sem("cld")
        with nc.Block("consts") as blk:
            @blk.sync
            def _(sy):
                for n_ in ("ident", "onesb", "onesf", "miscf", "gsc", "usc", "dsc", "qidx"):
                    sy.dma_start(out=cb[n_][:], in_=dram[n_].ap()).then_inc(cld.h, 16)
                    cld.val += 16
                sy.wait_ge(cld.h, cld.val)
        prep = WeightPrep(p, dram, scr, cb)
        done = [(cld, cld.val)]

        done = emit_ln(p, S, dram["x_in"].ap(), scr["xln_fm"].ap(), cb, "L1",
                       ready=done)
        s1scope = p.scope()
        s1scope.__enter__()
        brow_a, dbr = emit_brows(p, dram["aa_b"], "ba")
        done = done + dbr
        with p.scope():
            abufs = {
                "Qh": [p.sbuf(f"aQh{k}", [128, S], BF16) for k in range(2)],
                "Kh": [p.sbuf(f"aKh{k}", [128, S], BF16) for k in range(2)],
                "Vt": p.sbuf("aVt", [128, (S // 128) * 256], BF16),
                "xsb": [p.sbuf(f"axsb{k}", [128, 16 * 512], BF16) for k in range(2)],
                "wqs": p.sbuf("awqs", [128, 16 * 256], BF16),
                "wks": p.sbuf("awks", [128, 16 * 256], BF16),
                "wvs": p.sbuf("awvs", [128, 16 * 256], BF16),
                "brow_a": brow_a, "brow_b": brow_a,
                "abase": p.sbuf("c_abase", [128, 7 * 512], F32),
                "amask": p.sbuf("c_amask", [128, 4 * 512], F32),
                "abt": p.sbuf("c_abt", [128, ABT_WIDTH[0]], F32),
                "colmaps": COLMAPS[0],
            }
            acl = p.sem("acl")
            with nc.Block("aconsts") as blk:
                @blk.scalar
                def _(sc):
                    for n_ in ("abase", "amask", "abt"):
                        sc.dma_start(out=abufs[n_][:], in_=dram[n_].ap()).then_inc(acl.h, 16)
                        acl.val += 16
            done = done + [(acl, acl.val)]
            for g in range(8):
                dq = emit_qkv(p, S, g, "a", dram, None, scr["xln_fm"].ap(), None,
                              abufs, cb, prep, f"aq{g}", ready=done)
                done = emit_attn_inner(p, "a", g, abufs, cb, prep, scr["o_scr"].ap(),
                                       f"ai{g}", ready=dq)
        with p.scope():
            obufs = {"wob": p.sbuf("awob", [128, 16 * 2048], BF16),
                     "brow_a": brow_a, "brow_b": brow_a}
            done = emit_oproj(p, S, "a", dram, None, scr["o_scr"].ap(),
                              dram["x_in"].ap(), scr["x1"].ap(), obufs, cb, prep,
                              None, "ao", ready=done)
        s1scope.__exit__(None, None, None)

        def _finish(src_scr):
            fs = p.sem("fin_s")
            with nc.Block("fin") as blk:
                @blk.sync
                def _(sy):
                    _gate(sy, done)
                    if src_scr is not None:
                        n = dbg.shape[0] // 512
                        for i in range(n):
                            sy.dma_start(out=dbg.ap()[512 * i:512 * (i + 1), :],
                                         in_=src_scr.ap()[512 * i:512 * (i + 1), :]) \
                                .then_inc(fs.h, 16)
                        sy.wait_ge(fs.h, 16 * n)
                @blk.gpsimd
                def _(g_):
                    prep.emit_rest(g_)
                    _gate(g_, done)
                @blk.scalar
                def _(sc):
                    _gate(sc, done)
                @blk.vector
                def _(v):
                    prep.emit_dve_rest(v)
                    _gate(v, done)
                @blk.tensor
                def _(t):
                    _gate(t, done)

        if upto == "x1":
            _finish(scr["x1"])
            return nc

        done = emit_ln(p, S, scr["x1"].ap(), scr["xln_fm"].ap(), cb, "L2", ready=done)
        if upto == "ffh":
            UPTO_FFH[0] = True
        done = emit_ffn(p, S, scr, scr["xln_fm"].ap(), scr["x1"].ap(),
                        scr["x2"].ap(), cb, prep, "F1", ready=done)
        if upto == "ffh":
            UPTO_FFH[0] = False
            sems_ff = [p.sem("FFg" + nm) for nm in ("sh",)]
            done = done + [(s_, s_.val) for s_ in sems_ff]
            _finish2 = p.sem("fin_h")
            with nc.Block("finh") as blk:
                @blk.sync
                def _(sy):
                    _gate(sy, done)
                @blk.gpsimd
                def _(g_):
                    prep.emit_rest(g_)
                    _gate(g_, done)
                    for i in range(8):
                        g_.dma_start(out=dbg.ap()[512 * i:512 * (i + 1), :],
                                     in_=scr["h_scr"].ap()[512 * i:512 * (i + 1), 0:2048]) \
                            .then_inc(_finish2.h, 16)
                    g_.wait_ge(_finish2.h, 16 * 8)
                @blk.scalar
                def _(sc):
                    _gate(sc, done)
                @blk.vector
                def _(v):
                    prep.emit_dve_rest(v)
                    _gate(v, done)
                @blk.tensor
                def _(t):
                    _gate(t, done)
            return nc
        if upto == "x2":
            _finish(scr["x2"])
            return nc

        pf = p.sem("pflush")
        with nc.Block("pflush") as blk:
            @blk.gpsimd
            def _(g_):
                prep.emit_rest(g_)
                g_.wait_ge(prep.st.h, 16 * prep.total)
                g_.sem_inc(pf.h, 1)
            @blk.vector
            def _(v):
                prep.emit_dve_rest(v)
        pf.val += 1
        done = done + [(pf, pf.val)]
        done = emit_ln(p, S, scr["x2"].ap(), scr["xln_fm"].ap(), cb, "L3", ready=done)
        if upto == "ln3":
            _finish(scr["x2"])
            return nc
        s3scope = p.scope()
        s3scope.__enter__()
        brow_b, dbr2 = emit_brows(p, dram["ab_b"], "bb")
        done = done + dbr2
        with p.scope():
            bbufs = {
                "Qh": [p.sbuf(f"bQh{k}", [128, 2048], BF16) for k in range(2)],
                "Kh": [p.sbuf(f"bKh{k}", [128, S], BF16) for k in range(2)],
                "Vt": p.sbuf("bVt", [128, (S // 128) * 256], BF16),
                "xsb": [p.sbuf(f"bxsb{k}", [128, 16 * 512], BF16) for k in range(2)],
                "wqs": p.sbuf("bwqs", [128, 16 * 256], BF16),
                "wks": p.sbuf("bwks", [128, 16 * 256], BF16),
                "wvs": p.sbuf("bwvs", [128, 16 * 256], BF16),
                "brow_a": brow_b, "brow_b": brow_b,
                "bmask": p.sbuf("c_bmask", [128, 8 * 512], F32),
                "colmaps": None,
            }
            bcl = p.sem("bcl")
            with nc.Block("bconsts") as blk:
                @blk.scalar
                def _(sc):
                    sc.dma_start(out=bbufs["bmask"][:], in_=dram["bmask"].ap()) \
                        .then_inc(bcl.h, 16)
                    bcl.val += 16
            done = done + [(bcl, bcl.val)]
            nbg = 1 if upto == "x3g1" else 8
            for g in range(nbg):
                dq = emit_qkv(p, S, g, "b", dram, scr["wb_scr"].ap(),
                              scr["xln_fm"].ap(), scr["q_tm"].ap(), bbufs, cb, prep,
                              f"bq{g}", ready=done)
                qsts = p.sem("QKqs")
                dg = emit_qgather(p, scr["q_tm"].ap(), cb["qidx"], bbufs, cb,
                                  qsts, qsts.val, f"bg{g}", ready=dq)
                done = emit_attn_inner(p, "b", g, bbufs, cb, prep,
                                       scr["o_scr"].ap()[:, 0:2048], f"bi{g}",
                                       ready=dq + dg)
            if upto in ("x3g1", "x3g8"):
                _finish(scr["x2"])
                return nc
        with p.scope():
            obufs = {"wob": p.sbuf("bwob", [128, 16 * 2048], BF16),
                     "brow_a": brow_b, "brow_b": brow_b}
            done = emit_oproj(p, 2048, "b", dram, scr["wb_scr"].ap(),
                              scr["o_scr"].ap()[:, 0:2048], scr["x2"].ap(),
                              scr["x3o"].ap(), obufs, cb, prep, cb["qidx"], "bo",
                              ready=done)
        s3scope.__exit__(None, None, None)
        if upto == "x3":
            _finish(scr["x3o"])
            return nc

        done = emit_ln(p, 2048, scr["x3o"].ap(), scr["xln_fm"].ap(), cb, "L4",
                       ready=done)
        done = emit_ffn(p, 2048, scr, scr["xln_fm"].ap(), scr["x3o"].ap(),
                        out.ap(), cb, prep, "F2", ready=done)

        _finish(None)
    return nc


ABT_WIDTH = [None]
COLMAPS = [None]


def make_core_inputs(inputs, core):
    b, h = core // 2, core % 2
    t = build_host_tables(h)
    fs = II // 128
    im = {
        "x_in": np.ascontiguousarray(inputs["L0_attn_a_x"][b]),
        "aw_q": inputs["L0_attn_a_wq"], "aw_k": inputs["L0_attn_a_wk"],
        "aw_v": inputs["L0_attn_a_wv"], "aw_o": inputs["L0_attn_a_wo"],
        "wb_q": inputs["L0_attn_b_w_q"], "wb_k": inputs["L0_attn_b_w_k"],
        "wb_v": inputs["L0_attn_b_w_v"], "wb_o": inputs["L0_attn_b_w_o"],
        "aa_b": np.ascontiguousarray(np.stack([inputs["L0_attn_a_bq"], inputs["L0_attn_a_bk"],
                                               inputs["L0_attn_a_bv"], inputs["L0_attn_a_bo"]])),
        "ab_b": np.ascontiguousarray(np.stack([inputs["L0_attn_b_b_q"], inputs["L0_attn_b_b_k"],
                                               inputs["L0_attn_b_b_v"], inputs["L0_attn_b_b_o"]])),
        "gate_w": inputs["L0_ffn_gate_w"], "up_w": inputs["L0_ffn_up_w"],
        "down_w": inputs["L0_ffn_down_w"],
        "gsc": np.ascontiguousarray(inputs["L0_ffn_gate_scale"].reshape(fs, 128).T),
        "usc": np.ascontiguousarray(inputs["L0_ffn_up_scale"].reshape(fs, 128).T),
        "dsc": np.ascontiguousarray(inputs["L0_ffn_down_scale"].reshape(16, 128).T),
        "ident": t["ident"], "onesb": t["onesb"], "onesf": t["onesf"],
        "miscf": t["miscf"], "abase": t["abase"], "amask": t["amask"],
        "abt": t["abt"], "bmask": t["bmask"], "qidx": t["qidx"],
    }
    return im


from concourse.bass_utils import run_bass_kernel_spmd

_tables0 = build_host_tables(0)
ABT_WIDTH[0] = _tables0["abt"].shape[1]
COLMAPS[0] = _tables0["colmaps"]


def kernel(**inputs):
    nc = build_program("full")
    in_maps = [make_core_inputs(inputs, c) for c in range(8)]
    res = run_bass_kernel_spmd(nc, in_maps, core_ids=list(range(8)))
    out = np.empty((4, S, D), np.float32)
    for c in range(8):
        b, h = c // 2, c % 2
        o = res.results[c]["out"]
        for j in range(16):
            out[b, 128 * (2 * j + h):128 * (2 * j + h) + 128, :] = o[128 * j:128 * (j + 1), :]
    return out


# version 08
# speedup vs baseline: 1.0925x; 1.0925x over previous
import math
from contextlib import ExitStack
import numpy as np
import ml_dtypes
import concourse.bass as bass
import concourse.mybir as mybir

F32 = mybir.dt.float32
BF16 = mybir.dt.bfloat16
I32 = mybir.dt.int32
U32 = mybir.dt.uint32
AF = mybir.ActivationFunctionType
ALU = mybir.AluOpType
AX = mybir.AxisListType
BF16NP = ml_dtypes.bfloat16

D = 2048
S = 4096
H = 16
HD = 128
II = 8192
NT = S // 128
NDC = D // 128
EPS = 1e-5
MC = 30.0
RSQ_HD = 1.0 / math.sqrt(HD)
CLIP = -120.0

SLOPES = [2.0 ** (-(i + 1) / 2.0) for i in range(H)]
TRUNC = 120.0

def head_window_tiles(h):
    w = TRUNC / SLOPES[h] + 512.0
    return min(int(math.ceil(w / 128.0)), NT)

BIASVEC_HEADS = set(h for h in range(H) if SLOPES[h] * 512.0 <= 32.0)


class Sem:
    __slots__ = ("h", "val")
    def __init__(self, h):
        self.h = h
        self.val = 0


class P:
    def __init__(self, nc, ctx):
        self.nc = nc
        self.ctx = ctx
        self._scopes = [ctx]
        self._sems = {}
    def sem(self, name):
        if name not in self._sems:
            self._sems[name] = Sem(self.ctx.enter_context(self.nc.semaphore(name)))
        return self._sems[name]
    def sbuf(self, name, shape, dtype):
        return self._scopes[-1].enter_context(self.nc.sbuf_tensor(name, shape, dtype))
    def psum(self, name, shape, dtype=F32):
        return self._scopes[-1].enter_context(self.nc.psum_tensor(name, shape, dtype))
    from contextlib import contextmanager as _cm
    @_cm
    def scope(self):
        es = ExitStack()
        self._scopes.append(es)
        try:
            yield es
        finally:
            self._scopes.pop()
            es.close()
    def push_scope(self):
        self._scopes.append(ExitStack())
    def pop_scope(self):
        es = self._scopes.pop()
        es.close()



def _gate(eng, ready):
    for s_, v_ in ready:
        eng.wait_ge(s_.h, v_)


def emit_const_load(p, cb, dram):
    nc = p.nc
    s = p.sem("const_ld")
    with nc.Block("constld") as blk:
        @blk.sync
        def _(sy):
            for name, tile in cb.items():
                sy.dma_start(out=tile[:], in_=dram[name].ap()).then_inc(s.h, 16)
                s.val += 16
            sy.wait_ge(s.h, s.val)


def emit_ln(p, T, src_tm, dst_fm, cb, tag, dst_tm=None, ready=(), stag="LN",
            rs_dst=None):
    nc = p.nc
    p.push_scope()
    n = T // 128
    ld = [p.sem(stag + "ld0"), p.sem(stag + "ld1")]
    acs = p.sem(stag + "ac")
    vvr = p.sem(stag + "vv")
    art = p.sem(stag + "ar")
    vnm = p.sem(stag + "vn")
    vrs = p.sem(stag + "vr")
    ttp = p.sem(stag + "tt")
    vcp = p.sem(stag + "vc")
    sst = p.sem(stag + "ss")
    vsw = p.sem(stag + "sw")

    xb = [p.sbuf(tag + f"xb{k}", [128, 2048], F32) for k in range(4)]
    ln = [p.sbuf(tag + f"ln{k}", [128, 2048], BF16) for k in range(4)]
    dmp = p.sbuf(tag + "dmp", [128, 2048], F32)
    st = p.sbuf(tag + "st", [128, 32], F32)
    rsb = p.sbuf(tag + "rsb", [128, 4], F32)
    fm = [p.sbuf(tag + f"fm{k}", [128, 2048], BF16) for k in range(4)]
    tp = [p.psum(tag + f"tp{k}", [128, 1024], BF16) for k in range(2)]
    ident, miscf = cb["ident"], cb["miscf"]

    n_store = 16 + (1 if dst_tm is not None else 0) + (1 if rs_dst is not None else 0)

    with nc.Block(tag) as blk:
        @blk.sync
        def _(sy):
            _gate(sy, ready)
            for i in range(n):
                if i >= 4:
                    sy.wait_ge(vnm.h, vnm.val + i - 3)
                    sy.wait_ge(acs.h, acs.val + 2 * (i - 3))
                sy.dma_start(out=xb[i % 4][:], in_=src_tm[i * 128:(i + 1) * 128, :]) \
                    .then_inc(ld[i % 2].h, 16)

        @blk.scalar
        def _(sc):
            _gate(sc, ready)
            def sums(i):
                sl = i % 4
                sc.wait_ge(ld[i % 2].h, ld[i % 2].val + 16 * (i // 2 + 1))
                if i >= 4:
                    sc.wait_ge(vvr.h, vvr.val + i - 3)
                sc.activation(out=dmp[:], in_=xb[sl][:], func=AF.Identity,
                              accum_out=st[:, 8 * sl:8 * sl + 1]).then_inc(acs.h, 1)
                sc.activation(out=dmp[:], in_=xb[sl][:], func=AF.Square,
                              accum_out=st[:, 8 * sl + 2:8 * sl + 3]).then_inc(acs.h, 1)
            def rstd(j):
                sl = j % 4
                sc.wait_ge(vvr.h, vvr.val + j + 1)
                sc.activation(out=st[:, 8 * sl + 5:8 * sl + 6], in_=st[:, 8 * sl + 4:8 * sl + 5],
                              func=AF.Ln, bias=miscf[:, 0:1])
                sc.activation(out=st[:, 8 * sl + 6:8 * sl + 7], in_=st[:, 8 * sl + 5:8 * sl + 6],
                              func=AF.Exp, scale=-0.5).then_inc(art.h, 1)
            for i in range(n):
                sums(i)
                if i >= 1:
                    rstd(i - 1)
            rstd(n - 1)

        @blk.vector
        def _(v):
            _gate(v, ready)
            def stats(i):
                sl = i % 4
                v.wait_ge(acs.h, acs.val + 2 * (i + 1))
                if i >= 4:
                    v.wait_ge(art.h, art.val + i - 3)
                    v.wait_ge(vnm.h, vnm.val + i - 3)
                v.tensor_scalar_mul(st[:, 8 * sl + 3:8 * sl + 4], st[:, 8 * sl:8 * sl + 1], 1.0 / D)
                v.tensor_scalar_mul(st[:, 8 * sl + 4:8 * sl + 5], st[:, 8 * sl + 2:8 * sl + 3], 1.0 / D) \
                    .then_inc(vsw.h, 1)
                vsw.val += 1
                v.wait_ge(vsw.h, vsw.val)
                v.tensor_mul(st[:, 8 * sl + 7:8 * sl + 8], st[:, 8 * sl + 3:8 * sl + 4],
                             st[:, 8 * sl + 3:8 * sl + 4]).then_inc(vsw.h, 1)
                vsw.val += 1
                v.wait_ge(vsw.h, vsw.val)
                v.tensor_sub(st[:, 8 * sl + 4:8 * sl + 5], st[:, 8 * sl + 4:8 * sl + 5],
                             st[:, 8 * sl + 7:8 * sl + 8]).then_inc(vvr.h, 1)
            def norm(j):
                sl = j % 4
                v.wait_ge(art.h, art.val + j + 1)
                if j >= 4:
                    v.wait_ge(ttp.h, ttp.val + 16 * (j - 3))
                    v.wait_ge(sst.h, sst.val + 16 * n_store * (j - 3))
                v.tensor_scalar(out=ln[sl][:], in0=xb[sl][:],
                                scalar1=st[:, 8 * sl + 3:8 * sl + 4],
                                scalar2=st[:, 8 * sl + 6:8 * sl + 7],
                                op0=ALU.subtract, op1=ALU.mult).then_inc(vnm.h, 1)
                if rs_dst is not None:
                    v.reduce_sum(rsb[:, sl:sl + 1], ln[sl][:], axis=AX.X).then_inc(vrs.h, 1)
            def copies(j):
                sl = j % 4
                for b in range(2):
                    gb = 2 * j + b
                    v.wait_ge(ttp.h, ttp.val + 8 * (gb + 1))
                    v.tensor_copy(out=fm[sl][:, 1024 * b:1024 * (b + 1)],
                                  in_=tp[gb % 2][:]).then_inc(vcp.h, 1)
            for i in range(n):
                stats(i)
                if i >= 1:
                    norm(i - 1)
                if i >= 3:
                    copies(i - 3)
            norm(n - 1)
            copies(n - 3)
            copies(n - 2)
            copies(n - 1)

        @blk.tensor
        def _(t):
            _gate(t, ready)
            for i in range(n):
                sl = i % 4
                t.wait_ge(vnm.h, vnm.val + i + 1)
                for k in range(16):
                    m = 16 * i + k
                    gb = m // 8
                    if k % 8 == 0 and gb >= 2:
                        t.wait_ge(vcp.h, vcp.val + gb - 1)
                    t.transpose(tp[gb % 2][:, 128 * (k % 8):128 * (k % 8 + 1)],
                                ln[sl][:, 128 * k:128 * (k + 1)], ident[:]) \
                        .then_inc(ttp.h, 1)

        @blk.gpsimd
        def _(g):
            _gate(g, ready)
            for i in range(n):
                sl = i % 4
                g.wait_ge(vcp.h, vcp.val + 2 * (i + 1))
                for k in range(16):
                    g.dma_start(out=dst_fm[128 * k:128 * (k + 1), 128 * i:128 * (i + 1)],
                                in_=fm[sl][:, 128 * k:128 * (k + 1)]).then_inc(sst.h, 16)
                if dst_tm is not None:
                    g.dma_start(out=dst_tm[128 * i:128 * (i + 1), :], in_=ln[sl][:]) \
                        .then_inc(sst.h, 16)
                if rs_dst is not None:
                    g.wait_ge(vrs.h, vrs.val + i + 1)
                    g.dma_start(out=rs_dst[128 * i:128 * (i + 1), 0:1],
                                in_=rsb[:, sl:sl + 1]).then_inc(sst.h, 16)
            g.wait_ge(sst.h, sst.val + 16 * n_store * n)

    p.pop_scope()
    ld[0].val += 16 * ((n + 1) // 2)
    ld[1].val += 16 * (n // 2)
    acs.val += 2 * n
    vvr.val += n
    art.val += n
    vnm.val += n
    if rs_dst is not None:
        vrs.val += n
    ttp.val += 16 * n
    vcp.val += 2 * n
    sst.val += 16 * n_store * n
    sems = [ld[0], ld[1], acs, vvr, art, vnm, vrs, ttp, vcp, sst]
    return [(s, s.val) for s in sems]



def emit_brows(p, dram_t, tag):
    nc = p.nc
    rows = [p.sbuf(tag + f"br{m}", [1, D], BF16) for m in range(4)]
    stg = p.sbuf(tag + "brs", [1, D], F32)
    s1 = p.sem("BRl")
    s2 = p.sem("BRc")
    with nc.Block(tag + "brows") as blk:
        @blk.sync
        def _(sy):
            for m in range(4):
                sy.wait_ge(s2.h, s2.val + m)
                sy.dma_start(out=stg[:], in_=dram_t.ap()[m:m + 1, :]).then_inc(s1.h, 16)
                s1.val += 16
                sy.wait_ge(s1.h, s1.val)
        @blk.vector
        def _(v):
            for m in range(4):
                v.wait_ge(s1.h, s1.val - 16 * (4 - m) + 16 * 1)
                v.tensor_copy(out=rows[m][:], in_=stg[:]).then_inc(s2.h, 1)
    s2.val += 4
    return rows, [(s1, s1.val), (s2, s2.val)]


class WeightPrep:
    def __init__(self, p, dram, scr, cb):
        self.p = p
        self.ld = p.sem("wp_ld")
        self.dq = p.sem("wp_dq")
        self.st = p.sem("wp_st")
        self.raw = [p.sbuf(f"wp_raw{k}", [128, 2048], BF16) for k in range(2)]
        self.deq = [p.sbuf(f"wp_deq{k}", [128, 2048], BF16) for k in range(2)]
        self.chunks = []
        self.done = 0
        self.pending = []
        self.store_q = []
        self.dqn = 0
        for name in ("gate", "up"):
            src = dram[name + "_w"].ap()
            dst = scr[name + "_scr"].ap()
            for r in range(II // 128):
                self.chunks.append(("dq2", src[128 * r:128 * (r + 1), :], None,
                                    dst[128 * r:128 * (r + 1), :]))
        src = dram["down_w"].ap()
        dst = scr["down_scr"].ap()
        for r in range(D // 128):
            for cc in range(4):
                self.chunks.append(("dq2", src[128 * r:128 * (r + 1), 2048 * cc:2048 * (cc + 1)],
                                    None,
                                    dst[128 * r:128 * (r + 1), 2048 * cc:2048 * (cc + 1)]))
        self.n_dq2 = len(self.chunks)
        for mi, name in enumerate(("wb_q", "wb_k", "wb_v", "wb_o")):
            src = dram[name].ap()
            dst = scr["wb_scr"].ap()
            for r in range(D // 128):
                self.chunks.append(("c", src[128 * r:128 * (r + 1), :], None,
                                    dst[2048 * mi + 128 * r:2048 * mi + 128 * (r + 1), :]))
        self.total = len(self.chunks)
        self.n_gateup = 2 * (II // 128)
        self.n_down = self.n_gateup + 4 * (D // 128)
        self.n_wb = self.n_down + 4 * (D // 128)

    def emit(self, g, n):
        for _ in range(n):
            if self.done >= self.total:
                return
            kind, src, scale, dst = self.chunks[self.done]
            k = self.done % 2
            if self.done >= 2:
                if self.done - 2 >= self.n_dq2:
                    g.wait_ge(self.st.h, 16 * (self.done - 1))
                if self.done - 2 < self.n_dq2:
                    g.wait_ge(self.dq.h, min(self.done - 1, self.n_dq2))
            g.dma_start(out=self.raw[k][:], in_=src).then_inc(self.ld.h, 16)
            if kind == "dq2":
                self.pending.append((self.done, dst))
                self.dqn += 1
                self.store_q.append((self.done, dst))
            else:
                g.wait_ge(self.ld.h, 16 * (self.done + 1))
                g.dma_start(out=dst, in_=self.raw[k][:]).then_inc(self.st.h, 16)
            self.done += 1
            while len(self.store_q) > 1:
                j, d2 = self.store_q.pop(0)
                g.wait_ge(self.dq.h, j + 1)
                g.dma_start(out=d2, in_=self.deq[j % 2][:]).then_inc(self.st.h, 16)
        if self.done >= self.total:
            while self.store_q:
                j, d2 = self.store_q.pop(0)
                g.wait_ge(self.dq.h, j + 1)
                g.dma_start(out=d2, in_=self.deq[j % 2][:]).then_inc(self.st.h, 16)
        self.ld.val = 16 * self.done
        self.st.val = 16 * (self.done - len(self.store_q))

    def emit_dve_n(self, v, n):
        for _ in range(n):
            if not self.pending:
                return
            i, dst = self.pending.pop(0)
            k = i % 2
            v.wait_ge(self.ld.h, 16 * (i + 1))
            if i >= 2:
                v.wait_ge(self.st.h, 16 * (i - 1))
            v.tensor_scalar(out=self.deq[k][:], in0=self.raw[k][:],
                            scalar1=128.0, scalar2=1.0,
                            op0=ALU.subtract, op1=ALU.mult).then_inc(self.dq.h, 1)

    def emit_dve_rest(self, v):
        self.emit_dve_n(v, len(self.pending))

    def emit_rest(self, g):
        self.emit(g, self.total - self.done)


def emit_qkv(p, T, g, mode, wdram, wb_scr_ap, xln_fm, q_tm, bufs, cb, prep, tag, ready=(), stag="QK"):
    nc = p.nc
    p.push_scope()
    ntb = T // 512
    Qh, Kh, Vt = bufs["Qh"], bufs["Kh"], bufs["Vt"]
    xsb = bufs["xsb"]
    wqs, wks, wvs = bufs["wqs"], bufs["wks"], bufs["wvs"]
    brows = bufs["brow_a" if mode == "a" else "brow_b"]
    onesb = cb["onesb"]

    wld = p.sem(stag + "wl")
    xld = [p.sem(stag + "xl0"), p.sem(stag + "xl1")]
    peg = p.sem(stag + "pg")
    eva = p.sem(stag + "ea")
    evv = p.sem(stag + "ev")
    qst = p.sem(stag + "qs")

    qps = [p.psum(tag + "qps0", [128, 512 if mode == "a" else 256]),
           p.psum(tag + "qps1", [128, 512 if mode == "a" else 256])]
    kps = [p.psum(tag + "kps0", [128, 512]), p.psum(tag + "kps1", [128, 512])]
    vps = [p.psum(tag + "vps0", [128, 256]), p.psum(tag + "vps1", [128, 256])]
    qsb = [p.sbuf(tag + f"qsb{k}", [128, 256], BF16) for k in range(2)] if mode == "b" else None

    groups = []
    for tb in range(ntb):
        if mode == "a":
            groups += [("qa", tb, hh) for hh in range(2)]
        else:
            groups += [("qb", tb, tt) for tt in range(4)]
        groups += [("k", tb, hh) for hh in range(2)]
        groups += [("v", tb, tt) for tt in range(4)]
    gpt = len(groups) // ntb

    def psum_of(kind, idx):
        if kind in ("qa", "qb"):
            return ("q" + str(idx % 2), qps[idx % 2])
        if kind == "k":
            return ("k" + str(idx % 2), kps[idx % 2])
        return ("v" + str(idx % 2), vps[idx % 2])

    evac_eng = {"qa": "a", "k": "a", "qb": "a", "v": "v"}
    eva_cnt, evv_cnt = [], []
    ca = cv = 0
    for (kind, tb, idx) in groups:
        if evac_eng[kind] == "a":
            ca += 1
        else:
            cv += 1
        eva_cnt.append(ca)
        evv_cnt.append(cv)

    n_w_dma = 48

    with nc.Block(tag) as blk:
        @blk.gpsimd
        def _(g_):
            _gate(g_, ready)
            if mode == "a":
                for mi, wn in enumerate(("aw_q", "aw_k", "aw_v")):
                    dstt = (wqs, wks, wvs)[mi]
                    w = wdram[wn].ap()
                    for dc in range(16):
                        g_.dma_start(out=dstt[:, 256 * dc:256 * (dc + 1)],
                                     in_=w[128 * dc:128 * (dc + 1), 256 * g:256 * (g + 1)]) \
                            .then_inc(wld.h, 16)
            prep.emit(g_, 3)

        @blk.sync
        def _(sy):
            _gate(sy, ready)
            if mode == "b":
                sy.wait_ge(prep.st.h, 16 * prep.n_wb)
                for mi in range(3):
                    dstt = (wqs, wks, wvs)[mi]
                    for dc in range(16):
                        sy.dma_start(out=dstt[:, 256 * dc:256 * (dc + 1)],
                                     in_=wb_scr_ap[2048 * mi + 256 * g:2048 * mi + 256 * (g + 1),
                                                   128 * dc:128 * (dc + 1)],
                                     transpose=True).then_inc(wld.h, 16)
            for tb in range(ntb):
                if tb >= 2:
                    sy.wait_ge(peg.h, peg.val + gpt * (tb - 1))
                for dc in range(16):
                    sy.dma_start(out=xsb[tb % 2][:, 512 * dc:512 * (dc + 1)],
                                 in_=xln_fm[128 * dc:128 * (dc + 1), 512 * tb:512 * (tb + 1)]) \
                        .then_inc(xld[tb % 2].h, 16)

        @blk.tensor
        def _(t):
            _gate(t, ready)
            slot_free = {}
            t.wait_ge(wld.h, wld.val + 16 * n_w_dma)
            for i, (kind, tb, idx) in enumerate(groups):
                sl = tb % 2
                if idx == 0 and kind in ("qa", "qb"):
                    t.wait_ge(xld[sl].h, xld[sl].val + 16 * 16 * (tb // 2 + 1))
                sname, ps = psum_of(kind, idx)
                if sname in slot_free:
                    sem, cnt = slot_free[sname]
                    t.wait_ge(sem.h, cnt)
                if kind == "qa":
                    hh = idx
                    for dc in range(16):
                        t.matmul(ps[:], wqs[:, 256 * dc + 128 * hh:256 * dc + 128 * (hh + 1)],
                                 xsb[sl][:, 512 * dc:512 * (dc + 1)],
                                 start=(dc == 0), stop=False)
                    t.matmul(ps[:], brows[0][0:1, 256 * g + 128 * hh:256 * g + 128 * (hh + 1)],
                             onesb[0:1, 0:512], start=False, stop=True).then_inc(peg.h, 1)
                elif kind == "k":
                    hh = idx
                    for dc in range(16):
                        t.matmul(ps[:], wks[:, 256 * dc + 128 * hh:256 * dc + 128 * (hh + 1)],
                                 xsb[sl][:, 512 * dc:512 * (dc + 1)],
                                 start=(dc == 0), stop=False)
                    t.matmul(ps[:], brows[1][0:1, 256 * g + 128 * hh:256 * g + 128 * (hh + 1)],
                             onesb[0:1, 0:512], start=False, stop=True).then_inc(peg.h, 1)
                elif kind == "qb":
                    tt = idx
                    for dc in range(16):
                        t.matmul(ps[:], xsb[sl][:, 512 * dc + 128 * tt:512 * dc + 128 * (tt + 1)],
                                 wqs[:, 256 * dc:256 * (dc + 1)], start=(dc == 0), stop=False)
                    t.matmul(ps[:], onesb[0:1, 0:128],
                             brows[0][0:1, 256 * g:256 * (g + 1)], start=False, stop=True) \
                        .then_inc(peg.h, 1)
                else:
                    tt = idx
                    for dc in range(16):
                        t.matmul(ps[:], xsb[sl][:, 512 * dc + 128 * tt:512 * dc + 128 * (tt + 1)],
                                 wvs[:, 256 * dc:256 * (dc + 1)], start=(dc == 0), stop=False)
                    t.matmul(ps[:], onesb[0:1, 0:128],
                             brows[2][0:1, 256 * g:256 * (g + 1)], start=False, stop=True) \
                        .then_inc(peg.h, 1)
                if evac_eng[kind] == "a":
                    slot_free[sname] = (eva, eva.val + eva_cnt[i])
                else:
                    slot_free[sname] = (evv, evv.val + evv_cnt[i])

        @blk.scalar
        def _(sc):
            _gate(sc, ready)
            for i, (kind, tb, idx) in enumerate(groups):
                if evac_eng[kind] != "a":
                    continue
                sname, ps = psum_of(kind, idx)
                sc.wait_ge(peg.h, peg.val + i + 1)
                if kind == "qa":
                    sc.activation(out=Qh[idx][:, 512 * tb:512 * (tb + 1)], in_=ps[:],
                                  func=AF.Identity, scale=RSQ_HD).then_inc(eva.h, 1)
                elif kind == "k":
                    sc.activation(out=Kh[idx][:, 512 * tb:512 * (tb + 1)], in_=ps[:],
                                  func=AF.Identity).then_inc(eva.h, 1)
                else:
                    qn = 4 * tb + idx
                    if qn >= 2:
                        sc.wait_ge(qst.h, qst.val + 16 * (qn - 1))
                    sc.activation(out=qsb[qn % 2][:], in_=ps[:],
                                  func=AF.Identity, scale=RSQ_HD).then_inc(eva.h, 1)
                    sc.wait_ge(eva.h, eva.val + eva_cnt[i])
                    sc.dma_start(out=q_tm[128 * qn:128 * (qn + 1), :], in_=qsb[qn % 2][:]) \
                        .then_inc(qst.h, 16)

        @blk.vector
        def _(v):
            _gate(v, ready)
            for i, (kind, tb, idx) in enumerate(groups):
                if evac_eng[kind] != "v":
                    continue
                sname, ps = psum_of(kind, idx)
                v.wait_ge(peg.h, peg.val + i + 1)
                kt = 4 * tb + idx
                v.tensor_copy(out=Vt[:, 256 * kt:256 * (kt + 1)], in_=ps[:]).then_inc(evv.h, 1)

            prep.emit_dve_rest(v)
    p.pop_scope()
    wld.val += 16 * n_w_dma
    for tb in range(ntb):
        xld[tb % 2].val += 16 * 16
    peg.val += len(groups)
    eva.val += eva_cnt[-1]
    evv.val += evv_cnt[-1]
    if mode == "b":
        qst.val += 16 * 4 * ntb
    sems = [wld, xld[0], xld[1], peg, eva, evv, qst]
    return [(s, s.val) for s in sems]


def build_inner_sched(mode, g, colmaps):
    qcs = []
    base_s = base_act = base_a = base_av = 0
    qcidx = 0
    nqc = 8 if mode == "a" else 4
    for hh in range(2):
        h = 2 * g + hh
        for c in range(nqc):
            if mode == "a":
                wt = head_window_tiles(h)
                kstart = max(0, 4 * c + 4 - wt)
                ks = list(range(kstart, 4 * c + 4))
                flags = []
                for kt in ks:
                    diag = kt >= 4 * c
                    use_abase = h not in BIASVEC_HEADS
                    dveadd = use_abase or diag
                    mask_e = (kt - 4 * c) if diag else None
                    if h in BIASVEC_HEADS:
                        bcol = colmaps["bv"][(h, 4 * c - kt + 3)]
                    else:
                        bcol = colmaps["be"][(h, kt - 4 * c)]
                    flags.append((dveadd, use_abase, mask_e, bcol))
            else:
                ks = list(range(0, 8 * c + 8))
                flags = []
                for kt in ks:
                    r = kt - 8 * c
                    if r >= 0:
                        flags.append((True, False, r, None))
                    else:
                        flags.append((False, False, None, None))
            cnt_a = []
            ca = 0
            for f in flags:
                if f[0]:
                    ca += 1
                cnt_a.append(ca)
            qcs.append(dict(hh=hh, h=2 * g + hh, c=c, ks=ks, flags=flags,
                            cnt_a=cnt_a, base_s=base_s, base_act=base_act,
                            base_a=base_a, base_av=base_av, qcidx=qcidx))
            base_s += len(ks)
            base_act += len(ks)
            base_a += ca
            base_av += len(ks)
            qcidx += 1
    return qcs, base_s, base_act, base_a, base_av, qcidx


def emit_attn_inner(p, mode, g, bufs, cb, prep, o_scr, tag, ready=(), stag="AI"):
    nc = p.nc
    p.push_scope()
    Qh, Kh, Vt = bufs["Qh"], bufs["Kh"], bufs["Vt"]
    colmaps = bufs.get("colmaps")
    qcs, tot_s, tot_act, tot_a, tot_av, tot_qc = build_inner_sched(mode, g, colmaps)

    pes = p.sem(stag + "ps")
    actp = p.sem(stag + "ap")
    dvea = p.sem(stag + "da")
    peav = p.sem(stag + "av")
    dverc = p.sem(stag + "rc")
    aisw = p.sem(stag + "sw")
    per = p.sem(stag + "pr")
    dveo = p.sem(stag + "do")
    sto = p.sem(stag + "so")

    sps = [p.psum(tag + f"sps{k}", [128, 512]) for k in range(4)]
    sums = p.psum(tag + "sums", [1, 512])
    ops = [p.psum(tag + "ops0", [128, 512]), p.psum(tag + "ops1", [128, 512])]
    rps = p.psum(tag + "rps", [128, 512])
    Tsb = [p.sbuf(tag + f"tsb{k}", [128, 512], F32) for k in range(4)]
    Psb = [p.sbuf(tag + f"psb{k}", [128, 512], BF16) for k in range(4)]
    rcs = p.sbuf(tag + "rcs", [1, 512], F32)
    Rsb = p.sbuf(tag + "rsb", [128, 512], F32)
    Osb = [p.sbuf(tag + f"osb{k}", [128, 512], BF16) for k in range(2)]
    onesb, onesf, miscf = cb["onesb"], cb["onesf"], cb["miscf"]
    abase = bufs.get("abase")
    amask = bufs.get("amask")
    bmask = bufs.get("bmask")
    abt = bufs.get("abt")

    with nc.Block(tag) as blk:
        @blk.tensor
        def _(t):
            _gate(t, ready)
            slot_free = {}
            for q in qcs:
                hh, c, ks, flags = q["hh"], q["c"], q["ks"], q["flags"]
                n = len(ks)
                qsl = 512 * c
                Qsrc = Qh[hh]
                o_sl = "o" + str(q["qcidx"] % 2)

                def s_mm(j):
                    if j >= 4:
                        t.wait_ge(actp.h, actp.val + q["base_act"] + j - 3)
                    t.matmul(sps[j % 4][:], Kh[hh][:, 128 * ks[j]:128 * (ks[j] + 1)],
                             Qsrc[:, qsl:qsl + 512], start=True, stop=True) \
                        .then_inc(pes.h, 1)

                def sums_av(j):
                    t.wait_ge(actp.h, actp.val + q["base_act"] + j + 1)
                    t.matmul(sums[:], onesb[:, 0:1], Psb[j % 4][:],
                             start=(j == 0), stop=(j == n - 1))
                    if j == 0 and o_sl in slot_free:
                        sem, cnt = slot_free[o_sl]
                        t.wait_ge(sem.h, cnt)
                    t.matmul(ops[q["qcidx"] % 2][:],
                             Vt[:, 256 * ks[j] + 128 * hh:256 * ks[j] + 128 * (hh + 1)],
                             Psb[j % 4][:], start=(j == 0), stop=(j == n - 1)) \
                        .then_inc(peav.h, 1)

                s_mm(0)
                s_mm(1) if n > 1 else None
                s_mm(2) if n > 2 else None
                for j in range(3, n):
                    s_mm(j)
                    sums_av(j - 3)
                for j in range(max(0, n - 3), n):
                    sums_av(j)
                slot_free[o_sl] = (dveo, dveo.val + q["qcidx"] + 1)
                t.wait_ge(dverc.h, dverc.val + q["qcidx"] + 1)
                if "r" in slot_free:
                    sem, cnt = slot_free["r"]
                    t.wait_ge(sem.h, cnt)
                t.matmul(rps[:], onesf[0:1, 0:128], rcs[0:1, :], start=True, stop=True) \
                    .then_inc(per.h, 1)
                slot_free["r"] = (dveo, dveo.val + q["qcidx"] + 1)

        @blk.scalar
        def _(sc):
            _gate(sc, ready)
            for q in qcs:
                n = len(q["ks"])
                for j, kt in enumerate(q["ks"]):
                    dveadd, use_abase, mask_e, bcol = q["flags"][j]
                    sc.wait_ge(pes.h, pes.val + q["base_s"] + j + 1)
                    if dveadd:
                        sc.wait_ge(dvea.h, dvea.val + q["base_a"] + q["cnt_a"][j])
                        src = Tsb[j % 4][:]
                    else:
                        src = sps[j % 4][:]
                    if j >= 4:
                        sc.wait_ge(peav.h, peav.val + q["base_av"] + j - 3)
                    if mode == "a":
                        bias_ap = abt[:, bcol:bcol + 1]
                    else:
                        bias_ap = miscf[:, 1:2]
                    sc.activation(out=Psb[j % 4][:], in_=src, func=AF.Exp,
                                  bias=bias_ap).then_inc(actp.h, 1)

        @blk.gpsimd
        def _(g_):
            _gate(g_, ready)
            for q in qcs:
                prep.emit(g_, 2)

        @blk.vector
        def _(v):
            _gate(v, ready)
            tslot_w = {}
            for q in qcs:
                n = len(q["ks"])
                for j, kt in enumerate(q["ks"]):
                    dveadd, use_abase, mask_e, bcol = q["flags"][j]
                    if not dveadd:
                        continue
                    v.wait_ge(pes.h, pes.val + q["base_s"] + j + 1)
                    sl = j % 4
                    if sl in tslot_w:
                        v.wait_ge(actp.h, tslot_w[sl])
                    ops_list = []
                    if use_abase:
                        ops_list.append(abase[:, 512 * q["h"]:512 * (q["h"] + 1)])
                    if mask_e is not None:
                        ops_list.append((amask if mode == "a" else bmask)
                                        [:, 512 * mask_e:512 * (mask_e + 1)])
                    for oi, opnd in enumerate(ops_list):
                        srcap = sps[sl][:] if oi == 0 else Tsb[sl][:]
                        inst = v.tensor_add(Tsb[sl][:], srcap, opnd)
                        if oi == len(ops_list) - 1:
                            inst.then_inc(dvea.h, 1)
                    tslot_w[sl] = actp.val + q["base_act"] + j + 1
                v.wait_ge(peav.h, peav.val + q["base_av"] + n)
                v.reciprocal(rcs[:], sums[:]).then_inc(dverc.h, 1)
                v.wait_ge(per.h, per.val + q["qcidx"] + 1)
                v.tensor_copy(Rsb[:], rps[:]).then_inc(aisw.h, 1)
                aisw.val += 1
                v.wait_ge(aisw.h, aisw.val)
                if q["qcidx"] >= 2:
                    v.wait_ge(sto.h, sto.val + 16 * (q["qcidx"] - 1))
                v.tensor_tensor(Osb[q["qcidx"] % 2][:], ops[q["qcidx"] % 2][:], Rsb[:],
                                op=ALU.mult).then_inc(dveo.h, 1)
                prep.emit_dve_n(v, 2)

        @blk.sync
        def _(sy):
            _gate(sy, ready)
            for q in qcs:
                sy.wait_ge(dveo.h, dveo.val + q["qcidx"] + 1)
                habs = q["h"]
                sy.dma_start(out=o_scr[128 * habs:128 * (habs + 1), 512 * q["c"]:512 * (q["c"] + 1)],
                             in_=Osb[q["qcidx"] % 2][:]).then_inc(sto.h, 16)


    p.pop_scope()
    pes.val += tot_s
    actp.val += tot_act
    dvea.val += tot_a
    peav.val += tot_av
    dverc.val += tot_qc
    per.val += tot_qc
    dveo.val += tot_qc
    sto.val += 16 * tot_qc
    sems = [pes, actp, dvea, peav, dverc, per, dveo, sto]
    return [(s, s.val) for s in sems]


def emit_qgather(p, q_tm, qidx_sb, bufs, cb, qst_sem, qst_val, tag, ready=(), stag="QG"):
    nc = p.nc
    p.push_scope()
    Qh = bufs["Qh"]
    gq = p.sem(stag + "gq")
    tq = p.sem(stag + "tq")
    cq = p.sem(stag + "cq")
    qg = [p.sbuf(tag + f"qg{k}", [128, 256], BF16) for k in range(2)]
    tpq = [p.psum(tag + f"tpq{k}", [128, 128], BF16) for k in range(2)]
    ident = cb["ident"]

    with nc.Block(tag) as blk:
        @blk.gpsimd
        def _(g_):
            _gate(g_, ready)
            g_.wait_ge(qst_sem.h, qst_val)
            from concourse.bass_core_rust import IndirectOffsetOnAxis
            for j in range(16):
                if j >= 2:
                    g_.wait_ge(tq.h, tq.val + 2 * (j - 1))
                g_.indirect_dma_start(
                    out=qg[j % 2][:, :], out_offset=None, in_=q_tm[:, :],
                    in_offset=IndirectOffsetOnAxis(ap=qidx_sb[:, j:j + 1], axis=0)) \
                    .then_inc(gq.h, 16)

        @blk.tensor
        def _(t):
            _gate(t, ready)
            for j in range(16):
                t.wait_ge(gq.h, gq.val + 16 * (j + 1))
                for u in range(2):
                    m = 2 * j + u
                    if m >= 2:
                        t.wait_ge(cq.h, cq.val + m - 1)
                    t.transpose(tpq[m % 2][:], qg[j % 2][:, 128 * u:128 * (u + 1)], ident[:]) \
                        .then_inc(tq.h, 1)

        @blk.vector
        def _(v):
            _gate(v, ready)
            for j in range(16):
                for u in range(2):
                    m = 2 * j + u
                    v.wait_ge(tq.h, tq.val + m + 1)
                    v.tensor_copy(out=Qh[u][:, 128 * j:128 * (j + 1)], in_=tpq[m % 2][:]) \
                        .then_inc(cq.h, 1)

    p.pop_scope()
    gq.val += 16 * 16
    tq.val += 32
    cq.val += 32
    return [(s, s.val) for s in (gq, tq, cq)]


def emit_oproj(p, Tq, mode, wdram, wb_scr_ap, o_scr, res_ap, x_dst, bufs, cb,
               prep, qidx_sb, tag, ready=(), stag="OP"):
    nc = p.nc
    p.push_scope()
    nt = Tq // 128
    wob = bufs["wob"]
    brows = bufs["brow_a" if mode == "a" else "brow_b"]
    onesb = cb["onesb"]

    wld = p.sem(stag + "wl")
    old_ = [p.sem(stag + "ol0"), p.sem(stag + "ol1")]
    xld = [p.sem(stag + "xl0"), p.sem(stag + "xl1")]
    peo = p.sem(stag + "po")
    dve = p.sem(stag + "dv")
    stx = p.sem(stag + "sx")

    otb = [p.sbuf(tag + f"otb{k}", [128, 2048], BF16) for k in range(2)]
    xrb = [p.sbuf(tag + f"xrb{k}", [128, 2048], F32) for k in range(2)]
    xnb = [p.sbuf(tag + f"xnb{k}", [128, 2048], F32) for k in range(2)]
    pso = [p.psum(tag + f"pso{k}", [128, 512]) for k in range(4)]

    with nc.Block(tag) as blk:
        @blk.gpsimd
        def _(g_):
            _gate(g_, ready)
            if mode == "a":
                w = wdram["aw_o"].ap()
                for dc in range(16):
                    g_.dma_start(out=wob[:, 2048 * dc:2048 * (dc + 1)],
                                 in_=w[128 * dc:128 * (dc + 1), :]).then_inc(wld.h, 16)
            if mode == "b":
                from concourse.bass_core_rust import IndirectOffsetOnAxis
                for i in range(nt):
                    if i >= 2:
                        g_.wait_ge(dve.h, dve.val + 4 * (i - 1))
                    g_.indirect_dma_start(
                        out=xrb[i % 2][:, :], out_offset=None, in_=res_ap[:, :],
                        in_offset=IndirectOffsetOnAxis(ap=qidx_sb[:, i:i + 1], axis=0)) \
                        .then_inc(xld[i % 2].h, 16)
            prep.emit(g_, 4)

        @blk.sync
        def _(sy):
            _gate(sy, ready)
            if mode == "b":
                sy.wait_ge(prep.st.h, 16 * prep.n_wb)
                for doc in range(4):
                    for dc in range(16):
                        sy.dma_start(out=wob[:, 2048 * dc + 512 * doc:2048 * dc + 512 * (doc + 1)],
                                     in_=wb_scr_ap[3 * 2048 + 512 * doc:3 * 2048 + 512 * (doc + 1),
                                                   128 * dc:128 * (dc + 1)],
                                     transpose=True).then_inc(wld.h, 16)
            for i in range(nt):
                if i >= 2:
                    sy.wait_ge(peo.h, peo.val + 4 * (i - 1))
                for dc in range(16):
                    sy.dma_start(out=otb[i % 2][:, 128 * dc:128 * (dc + 1)],
                                 in_=o_scr[128 * dc:128 * (dc + 1), 128 * i:128 * (i + 1)]) \
                        .then_inc(old_[i % 2].h, 16)

        @blk.scalar
        def _(sc):
            _gate(sc, ready)
            for i in range(nt):
                if mode == "a":
                    if i >= 2:
                        sc.wait_ge(dve.h, dve.val + 4 * (i - 1))
                    sc.dma_start(out=xrb[i % 2][:], in_=res_ap[128 * i:128 * (i + 1), :]) \
                        .then_inc(xld[i % 2].h, 16)
                if i >= 1:
                    sc.wait_ge(dve.h, dve.val + 4 * i)
                    sc.dma_start(out=x_dst[128 * (i - 1):128 * i, :], in_=xnb[(i - 1) % 2][:]) \
                        .then_inc(stx.h, 16)
            sc.wait_ge(dve.h, dve.val + 4 * nt)
            sc.dma_start(out=x_dst[128 * (nt - 1):128 * nt, :], in_=xnb[(nt - 1) % 2][:]) \
                .then_inc(stx.h, 16)
            sc.wait_ge(stx.h, stx.val + 16 * nt)

        @blk.tensor
        def _(t):
            _gate(t, ready)
            nw = 64 if mode == "b" else 16
            t.wait_ge(wld.h, wld.val + 16 * nw)
            for i in range(nt):
                sl = i % 2
                t.wait_ge(old_[sl].h, old_[sl].val + 16 * 16 * (i // 2 + 1))
                for doc in range(4):
                    if i >= 1:
                        t.wait_ge(dve.h, dve.val + 4 * (i - 1) + doc + 1)
                    for dc in range(16):
                        t.matmul(pso[doc][:], otb[sl][:, 128 * dc:128 * (dc + 1)],
                                 wob[:, 2048 * dc + 512 * doc:2048 * dc + 512 * (doc + 1)],
                                 start=(dc == 0), stop=False)
                    t.matmul(pso[doc][:], onesb[0:1, 0:128],
                             brows[3][0:1, 512 * doc:512 * (doc + 1)],
                             start=False, stop=True).then_inc(peo.h, 1)

        @blk.vector
        def _(v):
            _gate(v, ready)
            for i in range(nt):
                sl = i % 2
                v.wait_ge(xld[sl].h, xld[sl].val + 16 * (i // 2 + 1))
                if i >= 2:
                    v.wait_ge(stx.h, stx.val + 16 * (i - 1))
                for doc in range(4):
                    v.wait_ge(peo.h, peo.val + 4 * i + doc + 1)
                    v.tensor_add(xnb[sl][:, 512 * doc:512 * (doc + 1)], pso[doc][:],
                                 xrb[sl][:, 512 * doc:512 * (doc + 1)]).then_inc(dve.h, 1)

            prep.emit_dve_rest(v)
    p.pop_scope()
    wld.val += 16 * (64 if mode == "b" else 16)
    for i in range(nt):
        old_[i % 2].val += 16 * 16
        xld[i % 2].val += 16
    peo.val += 4 * nt
    dve.val += 4 * nt
    stx.val += 16 * nt
    sems = [wld, old_[0], old_[1], xld[0], xld[1], peo, dve, stx]
    return [(s, s.val) for s in sems]


UPTO_FFH = [False]


def emit_ffn(p, T, scr, xln_fm, xsrc, xdst, cb, prep, tag, ready=(), stag="FF"):
    nc = p.nc
    nH = T // 2048
    all_sems = []
    gate_scr, up_scr, down_scr, h_scr = (scr["gate_scr"].ap(), scr["up_scr"].ap(),
                                         scr["down_scr"].ap(), scr["h_scr"].ap())
    for HH in range(nH):
        tg = stag + "g"
        xhl = p.sem(tg + "xh")
        wldg = [p.sem(tg + "w0"), p.sem(tg + "w1")]
        peg = p.sem(tg + "pg")
        evg = p.sem(tg + "eg")
        evu = p.sem(tg + "eu")
        evh = p.sem(tg + "eh")
        sth = p.sem(tg + "sh")
        with p.scope():
            xh = p.sbuf(f"{tag}{HH}xh", [128, 16 * 2048], BF16)
            wgb = [p.sbuf(f"{tag}{HH}wg{k}", [128, 16 * 512], BF16) for k in range(2)]
            wub = [p.sbuf(f"{tag}{HH}wu{k}", [128, 16 * 512], BF16) for k in range(2)]
            gsb = [p.sbuf(f"{tag}{HH}gs{k}", [128, 2048], BF16) for k in range(2)]
            usb = p.sbuf(f"{tag}{HH}us", [128, 2048], BF16)
            hsb = [p.sbuf(f"{tag}{HH}hs{k}", [128, 2048], BF16) for k in range(2)]
            gps = p.psum(f"{tag}{HH}gps", [128, 2048])
            ups = p.psum(f"{tag}{HH}ups", [128, 2048])

            with nc.Block(f"{tag}{HH}g") as blk:
                @blk.sync
                def _(sy):
                    _gate(sy, ready)
                    if HH == 1:
                        pd = p.sem(stag + "d" + "sd")
                        dd = p.sem(stag + "d" + "dd")
                        sy.wait_ge(pd.h, pd.val)
                        sy.wait_ge(dd.h, dd.val)
                    if HH == 0:
                        sy.wait_ge(prep.st.h, 16 * prep.n_gateup)
                    for dc in range(16):
                        sy.dma_start(out=xh[:, 2048 * dc:2048 * (dc + 1)],
                                     in_=xln_fm[128 * dc:128 * (dc + 1),
                                                2048 * HH:2048 * (HH + 1)]) \
                            .then_inc(xhl.h, 16)
                    for ir in range(16):
                        if ir >= 2:
                            sy.wait_ge(peg.h, peg.val + 8 * (ir - 1))
                        for dc in range(16):
                            sy.dma_start(out=wgb[ir % 2][:, 512 * dc:512 * (dc + 1)],
                                         in_=gate_scr[512 * ir:512 * (ir + 1),
                                                      128 * dc:128 * (dc + 1)],
                                         transpose=True).then_inc(wldg[ir % 2].h, 16)
                        for dc in range(16):
                            sy.dma_start(out=wub[ir % 2][:, 512 * dc:512 * (dc + 1)],
                                         in_=up_scr[512 * ir:512 * (ir + 1),
                                                    128 * dc:128 * (dc + 1)],
                                         transpose=True).then_inc(wldg[ir % 2].h, 16)

                @blk.tensor
                def _(t):
                    _gate(t, ready)
                    t.wait_ge(xhl.h, xhl.val + 16 * 16)
                    for ic in range(64):
                        ir, u = ic // 4, ic % 4
                        if u == 0:
                            t.wait_ge(wldg[ir % 2].h, wldg[ir % 2].val + 16 * 32 * (ir // 2 + 1))
                        if ic >= 1:
                            t.wait_ge(evg.h, evg.val + ic)
                        for dc in range(16):
                            for tb in range(4):
                                inst = t.matmul(gps[:, 512 * tb:512 * (tb + 1)],
                                                wgb[ir % 2][:, 512 * dc + 128 * u:512 * dc + 128 * (u + 1)],
                                                xh[:, 2048 * dc + 512 * tb:2048 * dc + 512 * (tb + 1)],
                                                start=(dc == 0), stop=(dc == 15))
                        inst.then_inc(peg.h, 1)
                        if ic >= 1:
                            t.wait_ge(evu.h, evu.val + ic)
                        for dc in range(16):
                            for tb in range(4):
                                inst = t.matmul(ups[:, 512 * tb:512 * (tb + 1)],
                                                wub[ir % 2][:, 512 * dc + 128 * u:512 * dc + 128 * (u + 1)],
                                                xh[:, 2048 * dc + 512 * tb:2048 * dc + 512 * (tb + 1)],
                                                start=(dc == 0), stop=(dc == 15))
                        inst.then_inc(peg.h, 1)

                @blk.scalar
                def _(sc):
                    _gate(sc, ready)
                    for ic in range(64):
                        sc.wait_ge(peg.h, peg.val + 2 * ic + 1)
                        if ic >= 2:
                            sc.wait_ge(evh.h, evh.val + ic - 1)
                        sc.activation(out=gsb[ic % 2][:], in_=gps[:], func=AF.Silu,
                                      scale=cb["gsc"][:, ic:ic + 1]).then_inc(evg.h, 1)

                @blk.vector
                def _(v):
                    _gate(v, ready)
                    for ic in range(64):
                        v.wait_ge(peg.h, peg.val + 2 * ic + 2)
                        v.tensor_scalar_mul(usb[:], ups[:], cb["usc"][:, ic:ic + 1]) \
                            .then_inc(evu.h, 1)
                        v.wait_ge(evg.h, evg.val + ic + 1)
                        if ic >= 2:
                            v.wait_ge(sth.h, sth.val + 16 * (ic - 1))
                        v.tensor_mul(hsb[ic % 2][:], gsb[ic % 2][:], usb[:]) \
                            .then_inc(evh.h, 1)

                @blk.gpsimd
                def _(g_):
                    _gate(g_, ready)
                    for ic in range(64):
                        g_.wait_ge(evh.h, evh.val + ic + 1)
                        g_.dma_start(out=h_scr[128 * ic:128 * (ic + 1),
                                               2048 * HH:2048 * (HH + 1)],
                                     in_=hsb[ic % 2][:]).then_inc(sth.h, 16)

            xhl.val += 16 * 16
            for ir in range(16):
                wldg[ir % 2].val += 16 * 32
            peg.val += 128
            evg.val += 64
            evu.val += 64
            evh.val += 64
            sth.val += 16 * 64

        if UPTO_FFH[0] and HH == 0:
            break
        td = stag + "d"
        wldd = p.sem(td + "wl")
        hld = [p.sem(td + "h0"), p.sem(td + "h1")]
        xrl = [p.sem(td + "x0"), p.sem(td + "x1")]
        peB = p.sem(td + "pb")
        evA = p.sem(td + "ea")
        peT = p.sem(td + "pt")
        dvD = p.sem(td + "dd")
        stD = p.sem(td + "sd")
        quads = [(HF, tqp, hcp) for HF in range(4) for tqp in range(2)
                 for hcp in range(2)]
        with p.scope():
            wdb = [p.sbuf(f"{tag}{HH}wd{k}", [128, 64 * 512], BF16) for k in range(2)]
            hqb = [p.sbuf(f"{tag}{HH}hq{k}", [128, 8 * 1024], BF16) for k in range(2)]
            fsb = [p.sbuf(f"{tag}{HH}fs{k}", [128, 2048], BF16) for k in range(2)]
            xrb = [p.sbuf(f"{tag}{HH}xr{k}", [128, 1024], F32) for k in range(2)]
            xo4 = [p.sbuf(f"{tag}{HH}xo{k}", [128, 1024], F32) for k in range(2)]
            qd = p.psum(f"{tag}{HH}qd", [128, 2048])
            tps = [p.psum(f"{tag}{HH}tp{k}", [128, 1024], BF16) for k in range(2)]
            gsem = (p.sem(tg + "pg"), p.sem(tg + "eg"), p.sem(tg + "eu"),
                    p.sem(tg + "eh"), p.sem(tg + "sh"))

            with nc.Block(f"{tag}{HH}dn") as blk:
                @blk.sync
                def _(sy):
                    sy.wait_ge(gsem[0].h, gsem[0].val)
                    sy.wait_ge(gsem[3].h, gsem[3].val)
                    sy.wait_ge(gsem[4].h, gsem[4].val)
                    if HH == 0 and tag == "F1":
                        sy.wait_ge(prep.st.h, 16 * prep.n_down)
                    def wdb_load(HF):
                        if HF >= 2:
                            sy.wait_ge(peB.h, peB.val + 32 * (HF - 1))
                        for ic in range(64):
                            sy.dma_start(out=wdb[HF % 2][:, 512 * ic:512 * (ic + 1)],
                                         in_=down_scr[512 * HF:512 * (HF + 1),
                                                      128 * ic:128 * (ic + 1)],
                                         transpose=True).then_inc(wldd.h, 16)
                    wdb_load(0)
                    wdb_load(1)
                    gi = 0
                    qn = 0
                    for HF in range(4):
                        for tqp in range(2):
                            for hcp in range(2):
                                if qn % 4 == 1 and 2 <= HF + 1 <= 3:
                                    wdb_load(HF + 1)
                                qn += 1
                                for icb in range(8):
                                    if gi >= 2:
                                        sy.wait_ge(peB.h, peB.val + gi - 1)
                                    for k in range(8):
                                        ic = 8 * icb + k
                                        for tqj in range(2):
                                            tq = 2 * tqp + tqj
                                            sy.dma_start(
                                                out=hqb[gi % 2][:, 1024 * k + 512 * tqj:
                                                                1024 * k + 512 * (tqj + 1)],
                                                in_=h_scr[128 * ic:128 * (ic + 1),
                                                          2048 * HH + 512 * tq:
                                                          2048 * HH + 512 * (tq + 1)]) \
                                                .then_inc(hld[gi % 2].h, 16)
                                    gi += 1

                @blk.tensor
                def _(t):
                    t.wait_ge(gsem[1].h, gsem[1].val)
                    t.wait_ge(gsem[2].h, gsem[2].val)

                    def transposes(qg):
                        t.wait_ge(evA.h, evA.val + qg + 1)
                        for g in range(2):
                            gg = 2 * qg + g
                            if gg >= 2:
                                t.wait_ge(dvD.h, dvD.val + gg - 1)
                            for ttl in range(4):
                                for hcj in range(2):
                                    j2 = 2 * hcj + g
                                    inst = t.transpose(
                                        tps[gg % 2][:, 256 * ttl + 128 * hcj:
                                                    256 * ttl + 128 * (hcj + 1)],
                                        fsb[qg % 2][:, 512 * j2 + 128 * ttl:
                                                    512 * j2 + 128 * (ttl + 1)],
                                        cb["ident"][:])
                            inst.then_inc(peT.h, 1)

                    for qg in range(16):
                        HF, tqp, hcp = quads[qg]
                        if qg % 4 == 0:
                            t.wait_ge(wldd.h, wldd.val + 16 * 64 * (HF + 1))
                        if qg >= 1:
                            t.wait_ge(evA.h, evA.val + qg)
                        for icb in range(8):
                            gi = 8 * qg + icb
                            t.wait_ge(hld[gi % 2].h, hld[gi % 2].val + 16 * 16 * (gi // 2 + 1))
                            for k in range(8):
                                ic = 8 * icb + k
                                for hcj in range(2):
                                    hc = 2 * hcp + hcj
                                    for tqj in range(2):
                                        inst = t.matmul(
                                            qd[:, 512 * (2 * hcj + tqj):
                                               512 * (2 * hcj + tqj + 1)],
                                            wdb[HF % 2][:, 512 * ic + 128 * hc:
                                                        512 * ic + 128 * (hc + 1)],
                                            hqb[gi % 2][:, 1024 * k + 512 * tqj:
                                                        1024 * k + 512 * (tqj + 1)],
                                            start=(ic == 0), stop=(ic == 63))
                            inst.then_inc(peB.h, 1)
                        if qg >= 1:
                            transposes(qg - 1)
                    transposes(15)

                @blk.scalar
                def _(sc):
                    for qg in range(16):
                        HF, tqp, hcp = quads[qg]
                        for g in range(2):
                            gg = 2 * qg + g
                            if gg >= 2:
                                sc.wait_ge(dvD.h, dvD.val + gg - 1)
                            for ttl in range(4):
                                tq = 2 * tqp + g
                                row = 2048 * HH + 512 * tq + 128 * ttl
                                sc.dma_start(out=xrb[gg % 2][:, 256 * ttl:256 * (ttl + 1)],
                                             in_=xsrc[row:row + 128,
                                                      512 * HF + 256 * hcp:
                                                      512 * HF + 256 * (hcp + 1)]) \
                                    .then_inc(xrl[gg % 2].h, 16)
                        sc.wait_ge(peB.h, peB.val + 8 * (qg + 1))
                        for j2 in range(4):
                            hcj = j2 // 2
                            hcg = 4 * HF + 2 * hcp + hcj
                            inst = sc.activation(out=fsb[qg % 2][:, 512 * j2:512 * (j2 + 1)],
                                                 in_=qd[:, 512 * j2:512 * (j2 + 1)],
                                                 func=AF.Identity,
                                                 scale=cb["dsc"][:, hcg:hcg + 1])
                        inst.then_inc(evA.h, 1)

                @blk.gpsimd
                def _(g_):
                    for qg in range(16):
                        HF, tqp, hcp = quads[qg]
                        for g in range(2):
                            gg = 2 * qg + g
                            g_.wait_ge(dvD.h, dvD.val + gg + 1)
                            for ttl in range(4):
                                tq = 2 * tqp + g
                                row = 2048 * HH + 512 * tq + 128 * ttl
                                g_.dma_start(out=xdst[row:row + 128,
                                                      512 * HF + 256 * hcp:
                                                      512 * HF + 256 * (hcp + 1)],
                                             in_=xo4[gg % 2][:, 256 * ttl:256 * (ttl + 1)]) \
                                    .then_inc(stD.h, 16)
                    g_.wait_ge(stD.h, stD.val + 16 * 128)

                @blk.vector
                def _(v):
                    for qg in range(16):
                        for g in range(2):
                            gg = 2 * qg + g
                            v.wait_ge(peT.h, peT.val + gg + 1)
                            v.wait_ge(xrl[gg % 2].h, xrl[gg % 2].val + 16 * 4 * (gg // 2 + 1))
                            if gg >= 2:
                                v.wait_ge(stD.h, stD.val + 64 * (gg - 1))
                            v.tensor_add(xo4[gg % 2][:], tps[gg % 2][:], xrb[gg % 2][:]) \
                                .then_inc(dvD.h, 1)

            wldd.val += 16 * 64 * 4
            hld[0].val += 16 * 16 * 64
            hld[1].val += 16 * 16 * 64
            xrl[0].val += 16 * 4 * 16
            xrl[1].val += 16 * 4 * 16
            peB.val += 128
            evA.val += 16
            peT.val += 32
            dvD.val += 32
            stD.val += 16 * 128
        all_sems += [wldd, hld[0], hld[1], xrl[0], xrl[1], peB, evA, peT, dvD, stD]
    return [(s, s.val) for s in all_sems]


def build_host_tables(h_core):
    kk = np.arange(128)
    qq = np.arange(512)
    t = {}
    t["ident"] = np.eye(128).astype(BF16NP)
    t["onesb"] = np.ones((128, 512), BF16NP)
    t["onesf"] = np.ones((1, 128), np.float32)
    miscf = np.zeros((128, 8), np.float32)
    miscf[:, 0] = EPS
    miscf[:, 1] = -MC
    t["miscf"] = miscf
    ab = np.zeros((128, 7 * 512), np.float32)
    for h in range(7):
        ab[:, 512 * h:512 * (h + 1)] = SLOPES[h] * (kk[:, None] - qq[None, :])
    t["abase"] = ab
    am = np.zeros((128, 4 * 512), np.float32)
    for e in range(4):
        am[:, 512 * e:512 * (e + 1)] = np.where(128 * e + kk[:, None] > qq[None, :],
                                                -1.0e4, 0.0)
    t["amask"] = am
    cols = []
    bv_map, be_map = {}, {}
    for h in sorted(BIASVEC_HEADS):
        for d in range(32):
            bv_map[(h, d)] = len(cols)
            v = SLOPES[h] * (kk - 128.0 * d - 127.0)
            cols.append(np.maximum(v, CLIP) - MC)
    for h in range(H):
        if h in BIASVEC_HEADS:
            continue
        wt = head_window_tiles(h)
        emin = max(-28, 4 - wt)
        for e in range(emin, 4):
            be_map[(h, e)] = len(cols)
            cols.append(np.full(128, SLOPES[h] * 128.0 * e - MC))
    t["abt"] = np.stack(cols, axis=1).astype(np.float32)
    t["colmaps"] = {"bv": bv_map, "be": be_map}
    bm = np.zeros((128, 8 * 512), np.float32)
    s_of = qq // 128
    q128 = qq % 128
    for r in range(8):
        cond = (128 * r + kk[:, None]) > (128 * (h_core + 2 * s_of[None, :]) + q128[None, :])
        bm[:, 512 * r:512 * (r + 1)] = np.where(cond, -1.0e4, 0.0)
    t["bmask"] = bm
    qi = np.zeros((128, 16), np.uint32)
    for j in range(16):
        qi[:, j] = 128 * (2 * j + h_core) + np.arange(128)
    t["qidx"] = qi
    return t


def build_program(upto="full"):
    nc = bass.Bass("TRN2", target_bir_lowering=False)
    dram = {}
    dram["x_in"] = nc.dram_tensor("x_in", [S, D], F32, kind="ExternalInput")
    for n_ in ("aw_q", "aw_k", "aw_v", "aw_o", "wb_q", "wb_k", "wb_v", "wb_o"):
        dram[n_] = nc.dram_tensor(n_, [D, D], F32, kind="ExternalInput")
    dram["aa_b"] = nc.dram_tensor("aa_b", [4, D], F32, kind="ExternalInput")
    dram["ab_b"] = nc.dram_tensor("ab_b", [4, D], F32, kind="ExternalInput")
    dram["gate_w"] = nc.dram_tensor("gate_w", [II, D], I32, kind="ExternalInput")
    dram["up_w"] = nc.dram_tensor("up_w", [II, D], I32, kind="ExternalInput")
    dram["down_w"] = nc.dram_tensor("down_w", [D, II], I32, kind="ExternalInput")
    dram["gsc"] = nc.dram_tensor("gsc", [128, 64], F32, kind="ExternalInput")
    dram["usc"] = nc.dram_tensor("usc", [128, 64], F32, kind="ExternalInput")
    dram["dsc"] = nc.dram_tensor("dsc", [128, 16], F32, kind="ExternalInput")
    for n_, sh, dt_ in (("ident", [128, 128], BF16), ("onesb", [128, 512], BF16),
                        ("onesf", [1, 128], F32), ("miscf", [128, 8], F32),
                        ("abase", [128, 7 * 512], F32), ("amask", [128, 4 * 512], F32),
                        ("abt", [128, None], F32), ("bmask", [128, 8 * 512], F32),
                        ("qidx", [128, 16], U32)):
        pass
    NB = ABT_WIDTH[0]
    dram["ident"] = nc.dram_tensor("ident", [128, 128], BF16, kind="ExternalInput")
    dram["onesb"] = nc.dram_tensor("onesb", [128, 512], BF16, kind="ExternalInput")
    dram["onesf"] = nc.dram_tensor("onesf", [1, 128], F32, kind="ExternalInput")
    dram["miscf"] = nc.dram_tensor("miscf", [128, 8], F32, kind="ExternalInput")
    dram["abase"] = nc.dram_tensor("abase", [128, 7 * 512], F32, kind="ExternalInput")
    dram["amask"] = nc.dram_tensor("amask", [128, 4 * 512], F32, kind="ExternalInput")
    dram["abt"] = nc.dram_tensor("abt", [128, NB], F32, kind="ExternalInput")
    dram["bmask"] = nc.dram_tensor("bmask", [128, 8 * 512], F32, kind="ExternalInput")
    dram["qidx"] = nc.dram_tensor("qidx", [128, 16], U32, kind="ExternalInput")

    out = nc.dram_tensor("out", [2048, D], F32, kind="ExternalOutput")
    dbg = None
    if upto != "full":
        sh = [2048, D] if upto == "x3" else [S, D]
        dbg = nc.dram_tensor("dbg", sh, F32, kind="ExternalOutput")

    scr = {}
    scr["x1"] = nc.dram_tensor("x1", [S, D], F32, kind="Internal")
    scr["x2"] = nc.dram_tensor("x2", [S, D], F32, kind="Internal")
    scr["x3o"] = nc.dram_tensor("x3o", [2048, D], F32, kind="Internal")
    scr["xln_fm"] = nc.dram_tensor("xln_fm", [D, S], BF16, kind="Internal")
    scr["q_tm"] = nc.dram_tensor("q_tm", [S, 256], BF16, kind="Internal")
    scr["o_scr"] = nc.dram_tensor("o_scr", [D, S], BF16, kind="Internal")
    scr["gate_scr"] = nc.dram_tensor("gate_scr", [II, D], BF16, kind="Internal")
    scr["up_scr"] = nc.dram_tensor("up_scr", [II, D], BF16, kind="Internal")
    scr["down_scr"] = nc.dram_tensor("down_scr", [D, II], BF16, kind="Internal")
    scr["wb_scr"] = nc.dram_tensor("wb_scr", [4 * D, D], BF16, kind="Internal")
    scr["h_scr"] = nc.dram_tensor("h_scr", [II, S], BF16, kind="Internal")
    with ExitStack() as ctx:
        p = P(nc, ctx)
        cb = {}
        for n_ in ("ident", "onesb", "onesf", "miscf", "gsc", "usc", "dsc", "qidx"):
            shp = dram[n_].shape
            dt_ = dram[n_].dtype
            cb[n_] = p.sbuf("c_" + n_, list(shp), dt_)
        cld = p.sem("cld")
        with nc.Block("consts") as blk:
            @blk.sync
            def _(sy):
                for n_ in ("ident", "onesb", "onesf", "miscf", "gsc", "usc", "dsc", "qidx"):
                    sy.dma_start(out=cb[n_][:], in_=dram[n_].ap()).then_inc(cld.h, 16)
                    cld.val += 16
                sy.wait_ge(cld.h, cld.val)
        prep = WeightPrep(p, dram, scr, cb)
        done = [(cld, cld.val)]

        done = emit_ln(p, S, dram["x_in"].ap(), scr["xln_fm"].ap(), cb, "L1",
                       ready=done)
        s1scope = p.scope()
        s1scope.__enter__()
        brow_a, dbr = emit_brows(p, dram["aa_b"], "ba")
        done = done + dbr
        with p.scope():
            abufs = {
                "Qh": [p.sbuf(f"aQh{k}", [128, S], BF16) for k in range(2)],
                "Kh": [p.sbuf(f"aKh{k}", [128, S], BF16) for k in range(2)],
                "Vt": p.sbuf("aVt", [128, (S // 128) * 256], BF16),
                "xsb": [p.sbuf(f"axsb{k}", [128, 16 * 512], BF16) for k in range(2)],
                "wqs": p.sbuf("awqs", [128, 16 * 256], BF16),
                "wks": p.sbuf("awks", [128, 16 * 256], BF16),
                "wvs": p.sbuf("awvs", [128, 16 * 256], BF16),
                "brow_a": brow_a, "brow_b": brow_a,
                "abase": p.sbuf("c_abase", [128, 7 * 512], F32),
                "amask": p.sbuf("c_amask", [128, 4 * 512], F32),
                "abt": p.sbuf("c_abt", [128, ABT_WIDTH[0]], F32),
                "colmaps": COLMAPS[0],
            }
            acl = p.sem("acl")
            with nc.Block("aconsts") as blk:
                @blk.scalar
                def _(sc):
                    for n_ in ("abase", "amask", "abt"):
                        sc.dma_start(out=abufs[n_][:], in_=dram[n_].ap()).then_inc(acl.h, 16)
                        acl.val += 16
            done = done + [(acl, acl.val)]
            for g in range(8):
                dq = emit_qkv(p, S, g, "a", dram, None, scr["xln_fm"].ap(), None,
                              abufs, cb, prep, f"aq{g}", ready=done)
                done = emit_attn_inner(p, "a", g, abufs, cb, prep, scr["o_scr"].ap(),
                                       f"ai{g}", ready=dq)
        with p.scope():
            obufs = {"wob": p.sbuf("awob", [128, 16 * 2048], BF16),
                     "brow_a": brow_a, "brow_b": brow_a}
            done = emit_oproj(p, S, "a", dram, None, scr["o_scr"].ap(),
                              dram["x_in"].ap(), scr["x1"].ap(), obufs, cb, prep,
                              None, "ao", ready=done)
        s1scope.__exit__(None, None, None)

        def _finish(src_scr):
            fs = p.sem("fin_s")
            with nc.Block("fin") as blk:
                @blk.sync
                def _(sy):
                    _gate(sy, done)
                    if src_scr is not None:
                        n = dbg.shape[0] // 512
                        for i in range(n):
                            sy.dma_start(out=dbg.ap()[512 * i:512 * (i + 1), :],
                                         in_=src_scr.ap()[512 * i:512 * (i + 1), :]) \
                                .then_inc(fs.h, 16)
                        sy.wait_ge(fs.h, 16 * n)
                @blk.gpsimd
                def _(g_):
                    prep.emit_rest(g_)
                    _gate(g_, done)
                @blk.scalar
                def _(sc):
                    _gate(sc, done)
                @blk.vector
                def _(v):
                    prep.emit_dve_rest(v)
                    _gate(v, done)
                @blk.tensor
                def _(t):
                    _gate(t, done)

        if upto == "x1":
            _finish(scr["x1"])
            return nc

        done = emit_ln(p, S, scr["x1"].ap(), scr["xln_fm"].ap(), cb, "L2", ready=done)
        if upto == "ffh":
            UPTO_FFH[0] = True
        done = emit_ffn(p, S, scr, scr["xln_fm"].ap(), scr["x1"].ap(),
                        scr["x2"].ap(), cb, prep, "F1", ready=done)
        if upto == "ffh":
            UPTO_FFH[0] = False
            sems_ff = [p.sem("FFg" + nm) for nm in ("sh",)]
            done = done + [(s_, s_.val) for s_ in sems_ff]
            _finish2 = p.sem("fin_h")
            with nc.Block("finh") as blk:
                @blk.sync
                def _(sy):
                    _gate(sy, done)
                @blk.gpsimd
                def _(g_):
                    prep.emit_rest(g_)
                    _gate(g_, done)
                    for i in range(8):
                        g_.dma_start(out=dbg.ap()[512 * i:512 * (i + 1), :],
                                     in_=scr["h_scr"].ap()[512 * i:512 * (i + 1), 0:2048]) \
                            .then_inc(_finish2.h, 16)
                    g_.wait_ge(_finish2.h, 16 * 8)
                @blk.scalar
                def _(sc):
                    _gate(sc, done)
                @blk.vector
                def _(v):
                    prep.emit_dve_rest(v)
                    _gate(v, done)
                @blk.tensor
                def _(t):
                    _gate(t, done)
            return nc
        if upto == "x2":
            _finish(scr["x2"])
            return nc

        pf = p.sem("pflush")
        with nc.Block("pflush") as blk:
            @blk.gpsimd
            def _(g_):
                prep.emit_rest(g_)
                g_.wait_ge(prep.st.h, 16 * prep.total)
                g_.sem_inc(pf.h, 1)
            @blk.vector
            def _(v):
                prep.emit_dve_rest(v)
        pf.val += 1
        done = done + [(pf, pf.val)]
        done = emit_ln(p, S, scr["x2"].ap(), scr["xln_fm"].ap(), cb, "L3", ready=done)
        if upto == "ln3":
            _finish(scr["x2"])
            return nc
        s3scope = p.scope()
        s3scope.__enter__()
        brow_b, dbr2 = emit_brows(p, dram["ab_b"], "bb")
        done = done + dbr2
        with p.scope():
            bbufs = {
                "Qh": [p.sbuf(f"bQh{k}", [128, 2048], BF16) for k in range(2)],
                "Kh": [p.sbuf(f"bKh{k}", [128, S], BF16) for k in range(2)],
                "Vt": p.sbuf("bVt", [128, (S // 128) * 256], BF16),
                "xsb": [p.sbuf(f"bxsb{k}", [128, 16 * 512], BF16) for k in range(2)],
                "wqs": p.sbuf("bwqs", [128, 16 * 256], BF16),
                "wks": p.sbuf("bwks", [128, 16 * 256], BF16),
                "wvs": p.sbuf("bwvs", [128, 16 * 256], BF16),
                "brow_a": brow_b, "brow_b": brow_b,
                "bmask": p.sbuf("c_bmask", [128, 8 * 512], F32),
                "colmaps": None,
            }
            bcl = p.sem("bcl")
            with nc.Block("bconsts") as blk:
                @blk.scalar
                def _(sc):
                    sc.dma_start(out=bbufs["bmask"][:], in_=dram["bmask"].ap()) \
                        .then_inc(bcl.h, 16)
                    bcl.val += 16
            done = done + [(bcl, bcl.val)]
            nbg = 1 if upto == "x3g1" else 8
            for g in range(nbg):
                dq = emit_qkv(p, S, g, "b", dram, scr["wb_scr"].ap(),
                              scr["xln_fm"].ap(), scr["q_tm"].ap(), bbufs, cb, prep,
                              f"bq{g}", ready=done)
                qsts = p.sem("QKqs")
                dg = emit_qgather(p, scr["q_tm"].ap(), cb["qidx"], bbufs, cb,
                                  qsts, qsts.val, f"bg{g}", ready=dq)
                done = emit_attn_inner(p, "b", g, bbufs, cb, prep,
                                       scr["o_scr"].ap()[:, 0:2048], f"bi{g}",
                                       ready=dq + dg)
            if upto in ("x3g1", "x3g8"):
                _finish(scr["x2"])
                return nc
        with p.scope():
            obufs = {"wob": p.sbuf("bwob", [128, 16 * 2048], BF16),
                     "brow_a": brow_b, "brow_b": brow_b}
            done = emit_oproj(p, 2048, "b", dram, scr["wb_scr"].ap(),
                              scr["o_scr"].ap()[:, 0:2048], scr["x2"].ap(),
                              scr["x3o"].ap(), obufs, cb, prep, cb["qidx"], "bo",
                              ready=done)
        s3scope.__exit__(None, None, None)
        if upto == "x3":
            _finish(scr["x3o"])
            return nc

        done = emit_ln(p, 2048, scr["x3o"].ap(), scr["xln_fm"].ap(), cb, "L4",
                       ready=done)
        done = emit_ffn(p, 2048, scr, scr["xln_fm"].ap(), scr["x3o"].ap(),
                        out.ap(), cb, prep, "F2", ready=done)

        _finish(None)
    return nc


ABT_WIDTH = [None]
COLMAPS = [None]


def make_core_inputs(inputs, core):
    b, h = core // 2, core % 2
    t = build_host_tables(h)
    fs = II // 128
    im = {
        "x_in": np.ascontiguousarray(inputs["L0_attn_a_x"][b]),
        "aw_q": inputs["L0_attn_a_wq"], "aw_k": inputs["L0_attn_a_wk"],
        "aw_v": inputs["L0_attn_a_wv"], "aw_o": inputs["L0_attn_a_wo"],
        "wb_q": inputs["L0_attn_b_w_q"], "wb_k": inputs["L0_attn_b_w_k"],
        "wb_v": inputs["L0_attn_b_w_v"], "wb_o": inputs["L0_attn_b_w_o"],
        "aa_b": np.ascontiguousarray(np.stack([inputs["L0_attn_a_bq"], inputs["L0_attn_a_bk"],
                                               inputs["L0_attn_a_bv"], inputs["L0_attn_a_bo"]])),
        "ab_b": np.ascontiguousarray(np.stack([inputs["L0_attn_b_b_q"], inputs["L0_attn_b_b_k"],
                                               inputs["L0_attn_b_b_v"], inputs["L0_attn_b_b_o"]])),
        "gate_w": inputs["L0_ffn_gate_w"], "up_w": inputs["L0_ffn_up_w"],
        "down_w": inputs["L0_ffn_down_w"],
        "gsc": np.ascontiguousarray(inputs["L0_ffn_gate_scale"].reshape(fs, 128).T),
        "usc": np.ascontiguousarray(inputs["L0_ffn_up_scale"].reshape(fs, 128).T),
        "dsc": np.ascontiguousarray(inputs["L0_ffn_down_scale"].reshape(16, 128).T),
        "ident": t["ident"], "onesb": t["onesb"], "onesf": t["onesf"],
        "miscf": t["miscf"], "abase": t["abase"], "amask": t["amask"],
        "abt": t["abt"], "bmask": t["bmask"], "qidx": t["qidx"],
    }
    return im


from concourse.bass_utils import run_bass_kernel_spmd

_tables0 = build_host_tables(0)
ABT_WIDTH[0] = _tables0["abt"].shape[1]
COLMAPS[0] = _tables0["colmaps"]


def kernel(**inputs):
    nc = build_program("full")
    in_maps = [make_core_inputs(inputs, c) for c in range(8)]
    res = run_bass_kernel_spmd(nc, in_maps, core_ids=list(range(8)))
    out = np.empty((4, S, D), np.float32)
    for c in range(8):
        b, h = c // 2, c % 2
        o = res.results[c]["out"]
        for j in range(16):
            out[b, 128 * (2 * j + h):128 * (2 * j + h) + 128, :] = o[128 * j:128 * (j + 1), :]
    return out


# version 09
# speedup vs baseline: 1.1533x; 1.0557x over previous
import math
from contextlib import ExitStack
import numpy as np
import ml_dtypes
import concourse.bass as bass
import concourse.mybir as mybir

F32 = mybir.dt.float32
BF16 = mybir.dt.bfloat16
I32 = mybir.dt.int32
U32 = mybir.dt.uint32
AF = mybir.ActivationFunctionType
ALU = mybir.AluOpType
AX = mybir.AxisListType
BF16NP = ml_dtypes.bfloat16

D = 2048
S = 4096
H = 16
HD = 128
II = 8192
NT = S // 128
NDC = D // 128
EPS = 1e-5
MC = 30.0
RSQ_HD = 1.0 / math.sqrt(HD)
CLIP = -120.0

SLOPES = [2.0 ** (-(i + 1) / 2.0) for i in range(H)]
TRUNC = 120.0

def head_window_tiles(h):
    w = TRUNC / SLOPES[h] + 512.0
    return min(int(math.ceil(w / 128.0)), NT)

BIASVEC_HEADS = set(h for h in range(H) if SLOPES[h] * 512.0 <= 32.0)


class Sem:
    __slots__ = ("h", "val")
    def __init__(self, h):
        self.h = h
        self.val = 0


class P:
    def __init__(self, nc, ctx):
        self.nc = nc
        self.ctx = ctx
        self._scopes = [ctx]
        self._sems = {}
    def sem(self, name):
        if name not in self._sems:
            self._sems[name] = Sem(self.ctx.enter_context(self.nc.semaphore(name)))
        return self._sems[name]
    def sbuf(self, name, shape, dtype):
        return self._scopes[-1].enter_context(self.nc.sbuf_tensor(name, shape, dtype))
    def psum(self, name, shape, dtype=F32):
        return self._scopes[-1].enter_context(self.nc.psum_tensor(name, shape, dtype))
    from contextlib import contextmanager as _cm
    @_cm
    def scope(self):
        es = ExitStack()
        self._scopes.append(es)
        try:
            yield es
        finally:
            self._scopes.pop()
            es.close()
    def push_scope(self):
        self._scopes.append(ExitStack())
    def pop_scope(self):
        es = self._scopes.pop()
        es.close()



def _gate(eng, ready):
    for s_, v_ in ready:
        eng.wait_ge(s_.h, v_)


def emit_const_load(p, cb, dram):
    nc = p.nc
    s = p.sem("const_ld")
    with nc.Block("constld") as blk:
        @blk.sync
        def _(sy):
            for name, tile in cb.items():
                sy.dma_start(out=tile[:], in_=dram[name].ap()).then_inc(s.h, 16)
                s.val += 16
            sy.wait_ge(s.h, s.val)


def emit_ln(p, T, src_tm, dst_fm, cb, tag, dst_tm=None, ready=(), stag="LN",
            rs_dst=None):
    nc = p.nc
    p.push_scope()
    n = T // 128
    ld = [p.sem(stag + "ld0"), p.sem(stag + "ld1")]
    acs = p.sem(stag + "ac")
    vvr = p.sem(stag + "vv")
    art = p.sem(stag + "ar")
    vnm = p.sem(stag + "vn")
    vrs = p.sem(stag + "vr")
    ttp = p.sem(stag + "tt")
    vcp = p.sem(stag + "vc")
    sst = p.sem(stag + "ss")
    vsw = p.sem(stag + "sw")

    xb = [p.sbuf(tag + f"xb{k}", [128, 2048], F32) for k in range(4)]
    ln = [p.sbuf(tag + f"ln{k}", [128, 2048], BF16) for k in range(4)]
    dmp = p.sbuf(tag + "dmp", [128, 2048], F32)
    st = p.sbuf(tag + "st", [128, 32], F32)
    rsb = p.sbuf(tag + "rsb", [128, 4], F32)
    fm = [p.sbuf(tag + f"fm{k}", [128, 2048], BF16) for k in range(4)]
    tp = [p.psum(tag + f"tp{k}", [128, 1024], BF16) for k in range(2)]
    ident, miscf = cb["ident"], cb["miscf"]

    n_store = 16 + (1 if dst_tm is not None else 0) + (1 if rs_dst is not None else 0)

    with nc.Block(tag) as blk:
        @blk.sync
        def _(sy):
            _gate(sy, ready)
            for i in range(n):
                if i >= 4:
                    sy.wait_ge(vnm.h, vnm.val + i - 3)
                    sy.wait_ge(acs.h, acs.val + 2 * (i - 3))
                sy.dma_start(out=xb[i % 4][:], in_=src_tm[i * 128:(i + 1) * 128, :]) \
                    .then_inc(ld[i % 2].h, 16)

        @blk.scalar
        def _(sc):
            _gate(sc, ready)
            def sums(i):
                sl = i % 4
                sc.wait_ge(ld[i % 2].h, ld[i % 2].val + 16 * (i // 2 + 1))
                if i >= 4:
                    sc.wait_ge(vvr.h, vvr.val + i - 3)
                sc.activation(out=dmp[:], in_=xb[sl][:], func=AF.Identity,
                              accum_out=st[:, 8 * sl:8 * sl + 1]).then_inc(acs.h, 1)
                sc.activation(out=dmp[:], in_=xb[sl][:], func=AF.Square,
                              accum_out=st[:, 8 * sl + 2:8 * sl + 3]).then_inc(acs.h, 1)
            def rstd(j):
                sl = j % 4
                sc.wait_ge(vvr.h, vvr.val + j + 1)
                sc.activation(out=st[:, 8 * sl + 5:8 * sl + 6], in_=st[:, 8 * sl + 4:8 * sl + 5],
                              func=AF.Ln, bias=miscf[:, 0:1])
                sc.activation(out=st[:, 8 * sl + 6:8 * sl + 7], in_=st[:, 8 * sl + 5:8 * sl + 6],
                              func=AF.Exp, scale=-0.5).then_inc(art.h, 1)
            for i in range(n):
                sums(i)
                if i >= 1:
                    rstd(i - 1)
            rstd(n - 1)

        @blk.vector
        def _(v):
            _gate(v, ready)
            def stats(i):
                sl = i % 4
                v.wait_ge(acs.h, acs.val + 2 * (i + 1))
                if i >= 4:
                    v.wait_ge(art.h, art.val + i - 3)
                    v.wait_ge(vnm.h, vnm.val + i - 3)
                v.tensor_scalar_mul(st[:, 8 * sl + 3:8 * sl + 4], st[:, 8 * sl:8 * sl + 1], 1.0 / D)
                v.tensor_scalar_mul(st[:, 8 * sl + 4:8 * sl + 5], st[:, 8 * sl + 2:8 * sl + 3], 1.0 / D) \
                    .then_inc(vsw.h, 1)
                vsw.val += 1
                v.wait_ge(vsw.h, vsw.val)
                v.tensor_mul(st[:, 8 * sl + 7:8 * sl + 8], st[:, 8 * sl + 3:8 * sl + 4],
                             st[:, 8 * sl + 3:8 * sl + 4]).then_inc(vsw.h, 1)
                vsw.val += 1
                v.wait_ge(vsw.h, vsw.val)
                v.tensor_sub(st[:, 8 * sl + 4:8 * sl + 5], st[:, 8 * sl + 4:8 * sl + 5],
                             st[:, 8 * sl + 7:8 * sl + 8]).then_inc(vvr.h, 1)
            def norm(j):
                sl = j % 4
                v.wait_ge(art.h, art.val + j + 1)
                if j >= 4:
                    v.wait_ge(ttp.h, ttp.val + 16 * (j - 3))
                    v.wait_ge(sst.h, sst.val + 16 * n_store * (j - 3))
                v.tensor_scalar(out=ln[sl][:], in0=xb[sl][:],
                                scalar1=st[:, 8 * sl + 3:8 * sl + 4],
                                scalar2=st[:, 8 * sl + 6:8 * sl + 7],
                                op0=ALU.subtract, op1=ALU.mult).then_inc(vnm.h, 1)
                if rs_dst is not None:
                    v.reduce_sum(rsb[:, sl:sl + 1], ln[sl][:], axis=AX.X).then_inc(vrs.h, 1)
            def copies(j):
                sl = j % 4
                for b in range(2):
                    gb = 2 * j + b
                    v.wait_ge(ttp.h, ttp.val + 8 * (gb + 1))
                    v.tensor_copy(out=fm[sl][:, 1024 * b:1024 * (b + 1)],
                                  in_=tp[gb % 2][:]).then_inc(vcp.h, 1)
            for i in range(n):
                stats(i)
                if i >= 1:
                    norm(i - 1)
                if i >= 3:
                    copies(i - 3)
            norm(n - 1)
            copies(n - 3)
            copies(n - 2)
            copies(n - 1)

        @blk.tensor
        def _(t):
            _gate(t, ready)
            for i in range(n):
                sl = i % 4
                t.wait_ge(vnm.h, vnm.val + i + 1)
                for k in range(16):
                    m = 16 * i + k
                    gb = m // 8
                    if k % 8 == 0 and gb >= 2:
                        t.wait_ge(vcp.h, vcp.val + gb - 1)
                    t.transpose(tp[gb % 2][:, 128 * (k % 8):128 * (k % 8 + 1)],
                                ln[sl][:, 128 * k:128 * (k + 1)], ident[:]) \
                        .then_inc(ttp.h, 1)

        @blk.gpsimd
        def _(g):
            _gate(g, ready)
            for i in range(n):
                sl = i % 4
                g.wait_ge(vcp.h, vcp.val + 2 * (i + 1))
                for k in range(16):
                    g.dma_start(out=dst_fm[128 * k:128 * (k + 1), 128 * i:128 * (i + 1)],
                                in_=fm[sl][:, 128 * k:128 * (k + 1)]).then_inc(sst.h, 16)
                if dst_tm is not None:
                    g.dma_start(out=dst_tm[128 * i:128 * (i + 1), :], in_=ln[sl][:]) \
                        .then_inc(sst.h, 16)
                if rs_dst is not None:
                    g.wait_ge(vrs.h, vrs.val + i + 1)
                    g.dma_start(out=rs_dst[128 * i:128 * (i + 1), 0:1],
                                in_=rsb[:, sl:sl + 1]).then_inc(sst.h, 16)
            g.wait_ge(sst.h, sst.val + 16 * n_store * n)

    p.pop_scope()
    ld[0].val += 16 * ((n + 1) // 2)
    ld[1].val += 16 * (n // 2)
    acs.val += 2 * n
    vvr.val += n
    art.val += n
    vnm.val += n
    if rs_dst is not None:
        vrs.val += n
    ttp.val += 16 * n
    vcp.val += 2 * n
    sst.val += 16 * n_store * n
    sems = [ld[0], ld[1], acs, vvr, art, vnm, vrs, ttp, vcp, sst]
    return [(s, s.val) for s in sems]



def emit_brows(p, dram_t, tag):
    nc = p.nc
    rows = [p.sbuf(tag + f"br{m}", [1, D], BF16) for m in range(4)]
    stg = p.sbuf(tag + "brs", [1, D], F32)
    s1 = p.sem("BRl")
    s2 = p.sem("BRc")
    with nc.Block(tag + "brows") as blk:
        @blk.sync
        def _(sy):
            for m in range(4):
                sy.wait_ge(s2.h, s2.val + m)
                sy.dma_start(out=stg[:], in_=dram_t.ap()[m:m + 1, :]).then_inc(s1.h, 16)
                s1.val += 16
                sy.wait_ge(s1.h, s1.val)
        @blk.vector
        def _(v):
            for m in range(4):
                v.wait_ge(s1.h, s1.val - 16 * (4 - m) + 16 * 1)
                v.tensor_copy(out=rows[m][:], in_=stg[:]).then_inc(s2.h, 1)
    s2.val += 4
    return rows, [(s1, s1.val), (s2, s2.val)]


class WeightPrep:
    def __init__(self, p, dram, scr, cb):
        self.p = p
        self.ld = p.sem("wp_ld")
        self.dq = p.sem("wp_dq")
        self.st = p.sem("wp_st")
        self.raw = [p.sbuf(f"wp_raw{k}", [128, 2048], BF16) for k in range(2)]
        self.deq = [p.sbuf(f"wp_deq{k}", [128, 2048], BF16) for k in range(2)]
        self.chunks = []
        self.done = 0
        self.pending = []
        self.store_q = []
        self.dqn = 0
        for name in ("gate", "up"):
            src = dram[name + "_w"].ap()
            dst = scr[name + "_scr"].ap()
            for r in range(II // 128):
                self.chunks.append(("dq2", src[128 * r:128 * (r + 1), :], None,
                                    dst[128 * r:128 * (r + 1), :]))
        src = dram["down_w"].ap()
        dst = scr["down_scr"].ap()
        for r in range(D // 128):
            for cc in range(4):
                self.chunks.append(("dq2", src[128 * r:128 * (r + 1), 2048 * cc:2048 * (cc + 1)],
                                    None,
                                    dst[128 * r:128 * (r + 1), 2048 * cc:2048 * (cc + 1)]))
        self.n_dq2 = len(self.chunks)
        for mi, name in enumerate(("wb_q", "wb_k", "wb_v", "wb_o")):
            src = dram[name].ap()
            dst = scr["wb_scr"].ap()
            for r in range(D // 128):
                self.chunks.append(("c", src[128 * r:128 * (r + 1), :], None,
                                    dst[2048 * mi + 128 * r:2048 * mi + 128 * (r + 1), :]))
        self.total = len(self.chunks)
        self.n_gateup = 2 * (II // 128)
        self.n_down = self.n_gateup + 4 * (D // 128)
        self.n_wb = self.n_down + 4 * (D // 128)

    def emit(self, g, n):
        for _ in range(n):
            if self.done >= self.total:
                return
            kind, src, scale, dst = self.chunks[self.done]
            k = self.done % 2
            if self.done >= 2:
                if self.done - 2 >= self.n_dq2:
                    g.wait_ge(self.st.h, 16 * (self.done - 1))
                if self.done - 2 < self.n_dq2:
                    g.wait_ge(self.dq.h, min(self.done - 1, self.n_dq2))
            g.dma_start(out=self.raw[k][:], in_=src).then_inc(self.ld.h, 16)
            if kind == "dq2":
                self.pending.append((self.done, dst))
                self.dqn += 1
                self.store_q.append((self.done, dst))
            else:
                g.wait_ge(self.ld.h, 16 * (self.done + 1))
                g.dma_start(out=dst, in_=self.raw[k][:]).then_inc(self.st.h, 16)
            self.done += 1
            while len(self.store_q) > 1:
                j, d2 = self.store_q.pop(0)
                g.wait_ge(self.dq.h, j + 1)
                g.dma_start(out=d2, in_=self.deq[j % 2][:]).then_inc(self.st.h, 16)
        if self.done >= self.total:
            while self.store_q:
                j, d2 = self.store_q.pop(0)
                g.wait_ge(self.dq.h, j + 1)
                g.dma_start(out=d2, in_=self.deq[j % 2][:]).then_inc(self.st.h, 16)
        self.ld.val = 16 * self.done
        self.st.val = 16 * (self.done - len(self.store_q))

    def emit_dve_n(self, v, n):
        for _ in range(n):
            if not self.pending:
                return
            i, dst = self.pending.pop(0)
            k = i % 2
            v.wait_ge(self.ld.h, 16 * (i + 1))
            if i >= 2:
                v.wait_ge(self.st.h, 16 * (i - 1))
            v.tensor_scalar(out=self.deq[k][:], in0=self.raw[k][:],
                            scalar1=128.0, scalar2=1.0,
                            op0=ALU.subtract, op1=ALU.mult).then_inc(self.dq.h, 1)

    def emit_dve_rest(self, v):
        self.emit_dve_n(v, len(self.pending))

    def emit_rest(self, g):
        self.emit(g, self.total - self.done)


def emit_qkv(p, T, g, mode, wdram, wb_scr_ap, xln_fm, q_tm, bufs, cb, prep, tag, ready=(), stag="QK"):
    nc = p.nc
    p.push_scope()
    ntb = T // 512
    Qh, Kh, Vt = bufs["Qh"], bufs["Kh"], bufs["Vt"]
    xsb = bufs["xsb"]
    wqs, wks, wvs = bufs["wqs"], bufs["wks"], bufs["wvs"]
    brows = bufs["brow_a" if mode == "a" else "brow_b"]
    onesb = cb["onesb"]

    wld = p.sem(stag + "wl")
    xld = [p.sem(stag + "xl0"), p.sem(stag + "xl1")]
    peg = p.sem(stag + "pg")
    eva = p.sem(stag + "ea")
    evv = p.sem(stag + "ev")
    qst = p.sem(stag + "qs")

    qps = [p.psum(tag + "qps0", [128, 512 if mode == "a" else 256]),
           p.psum(tag + "qps1", [128, 512 if mode == "a" else 256])]
    kps = [p.psum(tag + "kps0", [128, 512]), p.psum(tag + "kps1", [128, 512])]
    vps = [p.psum(tag + "vps0", [128, 256]), p.psum(tag + "vps1", [128, 256])]
    qsb = [p.sbuf(tag + f"qsb{k}", [128, 256], BF16) for k in range(2)] if mode == "b" else None

    groups = []
    for tb in range(ntb):
        if mode == "a":
            groups += [("qa", tb, hh) for hh in range(2)]
        else:
            groups += [("qb", tb, tt) for tt in range(4)]
        groups += [("k", tb, hh) for hh in range(2)]
        groups += [("v", tb, tt) for tt in range(4)]
    gpt = len(groups) // ntb

    def psum_of(kind, idx):
        if kind in ("qa", "qb"):
            return ("q" + str(idx % 2), qps[idx % 2])
        if kind == "k":
            return ("k" + str(idx % 2), kps[idx % 2])
        return ("v" + str(idx % 2), vps[idx % 2])

    evac_eng = {"qa": "a", "k": "a", "qb": "a", "v": "v"}
    eva_cnt, evv_cnt = [], []
    ca = cv = 0
    for (kind, tb, idx) in groups:
        if evac_eng[kind] == "a":
            ca += 1
        else:
            cv += 1
        eva_cnt.append(ca)
        evv_cnt.append(cv)

    n_w_dma = 48

    with nc.Block(tag) as blk:
        @blk.gpsimd
        def _(g_):
            _gate(g_, ready)
            if mode == "a":
                for mi, wn in enumerate(("aw_q", "aw_k", "aw_v")):
                    dstt = (wqs, wks, wvs)[mi]
                    w = wdram[wn].ap()
                    for dc in range(16):
                        g_.dma_start(out=dstt[:, 256 * dc:256 * (dc + 1)],
                                     in_=w[128 * dc:128 * (dc + 1), 256 * g:256 * (g + 1)]) \
                            .then_inc(wld.h, 16)
            prep.emit(g_, 3)

        @blk.sync
        def _(sy):
            _gate(sy, ready)
            if mode == "b":
                sy.wait_ge(prep.st.h, 16 * prep.n_wb)
                for mi in range(3):
                    dstt = (wqs, wks, wvs)[mi]
                    for dc in range(16):
                        sy.dma_start(out=dstt[:, 256 * dc:256 * (dc + 1)],
                                     in_=wb_scr_ap[2048 * mi + 256 * g:2048 * mi + 256 * (g + 1),
                                                   128 * dc:128 * (dc + 1)],
                                     transpose=True).then_inc(wld.h, 16)
            for tb in range(ntb):
                if tb >= 2:
                    sy.wait_ge(peg.h, peg.val + gpt * (tb - 1))
                for dc in range(16):
                    sy.dma_start(out=xsb[tb % 2][:, 512 * dc:512 * (dc + 1)],
                                 in_=xln_fm[128 * dc:128 * (dc + 1), 512 * tb:512 * (tb + 1)]) \
                        .then_inc(xld[tb % 2].h, 16)

        @blk.tensor
        def _(t):
            _gate(t, ready)
            slot_free = {}
            t.wait_ge(wld.h, wld.val + 16 * n_w_dma)
            for i, (kind, tb, idx) in enumerate(groups):
                sl = tb % 2
                if idx == 0 and kind in ("qa", "qb"):
                    t.wait_ge(xld[sl].h, xld[sl].val + 16 * 16 * (tb // 2 + 1))
                sname, ps = psum_of(kind, idx)
                if sname in slot_free:
                    sem, cnt = slot_free[sname]
                    t.wait_ge(sem.h, cnt)
                if kind == "qa":
                    hh = idx
                    for dc in range(16):
                        t.matmul(ps[:], wqs[:, 256 * dc + 128 * hh:256 * dc + 128 * (hh + 1)],
                                 xsb[sl][:, 512 * dc:512 * (dc + 1)],
                                 start=(dc == 0), stop=False)
                    t.matmul(ps[:], brows[0][0:1, 256 * g + 128 * hh:256 * g + 128 * (hh + 1)],
                             onesb[0:1, 0:512], start=False, stop=True).then_inc(peg.h, 1)
                elif kind == "k":
                    hh = idx
                    for dc in range(16):
                        t.matmul(ps[:], wks[:, 256 * dc + 128 * hh:256 * dc + 128 * (hh + 1)],
                                 xsb[sl][:, 512 * dc:512 * (dc + 1)],
                                 start=(dc == 0), stop=False)
                    t.matmul(ps[:], brows[1][0:1, 256 * g + 128 * hh:256 * g + 128 * (hh + 1)],
                             onesb[0:1, 0:512], start=False, stop=True).then_inc(peg.h, 1)
                elif kind == "qb":
                    tt = idx
                    for dc in range(16):
                        t.matmul(ps[:], xsb[sl][:, 512 * dc + 128 * tt:512 * dc + 128 * (tt + 1)],
                                 wqs[:, 256 * dc:256 * (dc + 1)], start=(dc == 0), stop=False)
                    t.matmul(ps[:], onesb[0:1, 0:128],
                             brows[0][0:1, 256 * g:256 * (g + 1)], start=False, stop=True) \
                        .then_inc(peg.h, 1)
                else:
                    tt = idx
                    for dc in range(16):
                        t.matmul(ps[:], xsb[sl][:, 512 * dc + 128 * tt:512 * dc + 128 * (tt + 1)],
                                 wvs[:, 256 * dc:256 * (dc + 1)], start=(dc == 0), stop=False)
                    t.matmul(ps[:], onesb[0:1, 0:128],
                             brows[2][0:1, 256 * g:256 * (g + 1)], start=False, stop=True) \
                        .then_inc(peg.h, 1)
                if evac_eng[kind] == "a":
                    slot_free[sname] = (eva, eva.val + eva_cnt[i])
                else:
                    slot_free[sname] = (evv, evv.val + evv_cnt[i])

        @blk.scalar
        def _(sc):
            _gate(sc, ready)
            for i, (kind, tb, idx) in enumerate(groups):
                if evac_eng[kind] != "a":
                    continue
                sname, ps = psum_of(kind, idx)
                sc.wait_ge(peg.h, peg.val + i + 1)
                if kind == "qa":
                    sc.activation(out=Qh[idx][:, 512 * tb:512 * (tb + 1)], in_=ps[:],
                                  func=AF.Identity, scale=RSQ_HD).then_inc(eva.h, 1)
                elif kind == "k":
                    sc.activation(out=Kh[idx][:, 512 * tb:512 * (tb + 1)], in_=ps[:],
                                  func=AF.Identity).then_inc(eva.h, 1)
                else:
                    qn = 4 * tb + idx
                    if qn >= 2:
                        sc.wait_ge(qst.h, qst.val + 16 * (qn - 1))
                    sc.activation(out=qsb[qn % 2][:], in_=ps[:],
                                  func=AF.Identity, scale=RSQ_HD).then_inc(eva.h, 1)
                    sc.wait_ge(eva.h, eva.val + eva_cnt[i])
                    sc.dma_start(out=q_tm[128 * qn:128 * (qn + 1), :], in_=qsb[qn % 2][:]) \
                        .then_inc(qst.h, 16)

        @blk.vector
        def _(v):
            _gate(v, ready)
            for i, (kind, tb, idx) in enumerate(groups):
                if evac_eng[kind] != "v":
                    continue
                sname, ps = psum_of(kind, idx)
                v.wait_ge(peg.h, peg.val + i + 1)
                kt = 4 * tb + idx
                v.tensor_copy(out=Vt[:, 256 * kt:256 * (kt + 1)], in_=ps[:]).then_inc(evv.h, 1)

            prep.emit_dve_rest(v)
    p.pop_scope()
    wld.val += 16 * n_w_dma
    for tb in range(ntb):
        xld[tb % 2].val += 16 * 16
    peg.val += len(groups)
    eva.val += eva_cnt[-1]
    evv.val += evv_cnt[-1]
    if mode == "b":
        qst.val += 16 * 4 * ntb
    sems = [wld, xld[0], xld[1], peg, eva, evv, qst]
    return [(s, s.val) for s in sems]


def build_inner_sched(mode, g, colmaps):
    qcs = []
    base_s = base_act = base_a = base_av = 0
    qcidx = 0
    nqc = 8 if mode == "a" else 4
    for hh in range(2):
        h = 2 * g + hh
        for c in range(nqc):
            if mode == "a":
                wt = head_window_tiles(h)
                kstart = max(0, 4 * c + 4 - wt)
                ks = list(range(kstart, 4 * c + 4))
                flags = []
                for kt in ks:
                    diag = kt >= 4 * c
                    use_abase = h not in BIASVEC_HEADS
                    dveadd = use_abase or diag
                    mask_e = (kt - 4 * c) if diag else None
                    if h in BIASVEC_HEADS:
                        bcol = colmaps["bv"][(h, 4 * c - kt + 3)]
                    else:
                        bcol = colmaps["be"][(h, kt - 4 * c)]
                    flags.append((dveadd, use_abase, mask_e, bcol))
            else:
                ks = list(range(0, 8 * c + 8))
                flags = []
                for kt in ks:
                    r = kt - 8 * c
                    if r >= 0:
                        flags.append((True, False, r, None))
                    else:
                        flags.append((False, False, None, None))
            cnt_a = []
            ca = 0
            for f in flags:
                if f[0]:
                    ca += 1
                cnt_a.append(ca)
            qcs.append(dict(hh=hh, h=2 * g + hh, c=c, ks=ks, flags=flags,
                            cnt_a=cnt_a, base_s=base_s, base_act=base_act,
                            base_a=base_a, base_av=base_av, qcidx=qcidx))
            base_s += len(ks)
            base_act += len(ks)
            base_a += ca
            base_av += len(ks)
            qcidx += 1
    return qcs, base_s, base_act, base_a, base_av, qcidx


def emit_attn_inner(p, mode, g, bufs, cb, prep, o_scr, tag, ready=(), stag="AI"):
    nc = p.nc
    p.push_scope()
    Qh, Kh, Vt = bufs["Qh"], bufs["Kh"], bufs["Vt"]
    colmaps = bufs.get("colmaps")
    qcs, tot_s, tot_act, tot_a, tot_av, tot_qc = build_inner_sched(mode, g, colmaps)

    pes = p.sem(stag + "ps")
    actp = p.sem(stag + "ap")
    dvea = p.sem(stag + "da")
    peav = p.sem(stag + "av")
    dverc = p.sem(stag + "rc")
    aisw = p.sem(stag + "sw")
    per = p.sem(stag + "pr")
    dveo = p.sem(stag + "do")
    sto = p.sem(stag + "so")

    sps = [p.psum(tag + f"sps{k}", [128, 512]) for k in range(4)]
    sums = p.psum(tag + "sums", [1, 512])
    ops = [p.psum(tag + "ops0", [128, 512]), p.psum(tag + "ops1", [128, 512])]
    rps = p.psum(tag + "rps", [128, 512])
    Tsb = [p.sbuf(tag + f"tsb{k}", [128, 512], F32) for k in range(4)]
    Psb = [p.sbuf(tag + f"psb{k}", [128, 512], BF16) for k in range(4)]
    rcs = p.sbuf(tag + "rcs", [1, 512], F32)
    Rsb = p.sbuf(tag + "rsb", [128, 512], F32)
    Osb = [p.sbuf(tag + f"osb{k}", [128, 512], BF16) for k in range(2)]
    onesb, onesf, miscf = cb["onesb"], cb["onesf"], cb["miscf"]
    abase = bufs.get("abase")
    amask = bufs.get("amask")
    bmask = bufs.get("bmask")
    abt = bufs.get("abt")

    with nc.Block(tag) as blk:
        @blk.tensor
        def _(t):
            _gate(t, ready)
            slot_free = {}
            for q in qcs:
                hh, c, ks, flags = q["hh"], q["c"], q["ks"], q["flags"]
                n = len(ks)
                qsl = 512 * c
                Qsrc = Qh[hh]
                o_sl = "o" + str(q["qcidx"] % 2)

                def s_mm(j):
                    if j >= 4:
                        t.wait_ge(actp.h, actp.val + q["base_act"] + j - 3)
                    t.matmul(sps[j % 4][:], Kh[hh][:, 128 * ks[j]:128 * (ks[j] + 1)],
                             Qsrc[:, qsl:qsl + 512], start=True, stop=True) \
                        .then_inc(pes.h, 1)

                def sums_av(j):
                    t.wait_ge(actp.h, actp.val + q["base_act"] + j + 1)
                    t.matmul(sums[:], onesb[:, 0:1], Psb[j % 4][:],
                             start=(j == 0), stop=(j == n - 1))
                    if j == 0 and o_sl in slot_free:
                        sem, cnt = slot_free[o_sl]
                        t.wait_ge(sem.h, cnt)
                    t.matmul(ops[q["qcidx"] % 2][:],
                             Vt[:, 256 * ks[j] + 128 * hh:256 * ks[j] + 128 * (hh + 1)],
                             Psb[j % 4][:], start=(j == 0), stop=(j == n - 1)) \
                        .then_inc(peav.h, 1)

                s_mm(0)
                s_mm(1) if n > 1 else None
                s_mm(2) if n > 2 else None
                for j in range(3, n):
                    s_mm(j)
                    sums_av(j - 3)
                for j in range(max(0, n - 3), n):
                    sums_av(j)
                slot_free[o_sl] = (dveo, dveo.val + q["qcidx"] + 1)
                t.wait_ge(dverc.h, dverc.val + q["qcidx"] + 1)
                if "r" in slot_free:
                    sem, cnt = slot_free["r"]
                    t.wait_ge(sem.h, cnt)
                t.matmul(rps[:], onesf[0:1, 0:128], rcs[0:1, :], start=True, stop=True) \
                    .then_inc(per.h, 1)
                slot_free["r"] = (dveo, dveo.val + q["qcidx"] + 1)

        @blk.scalar
        def _(sc):
            _gate(sc, ready)
            for q in qcs:
                n = len(q["ks"])
                for j, kt in enumerate(q["ks"]):
                    dveadd, use_abase, mask_e, bcol = q["flags"][j]
                    sc.wait_ge(pes.h, pes.val + q["base_s"] + j + 1)
                    if dveadd:
                        sc.wait_ge(dvea.h, dvea.val + q["base_a"] + q["cnt_a"][j])
                        src = Tsb[j % 4][:]
                    else:
                        src = sps[j % 4][:]
                    if j >= 4:
                        sc.wait_ge(peav.h, peav.val + q["base_av"] + j - 3)
                    if mode == "a":
                        bias_ap = abt[:, bcol:bcol + 1]
                    else:
                        bias_ap = miscf[:, 1:2]
                    sc.activation(out=Psb[j % 4][:], in_=src, func=AF.Exp,
                                  bias=bias_ap).then_inc(actp.h, 1)

        @blk.gpsimd
        def _(g_):
            _gate(g_, ready)
            for q in qcs:
                prep.emit(g_, 2)

        @blk.vector
        def _(v):
            _gate(v, ready)
            tslot_w = {}
            for q in qcs:
                n = len(q["ks"])
                for j, kt in enumerate(q["ks"]):
                    dveadd, use_abase, mask_e, bcol = q["flags"][j]
                    if not dveadd:
                        continue
                    v.wait_ge(pes.h, pes.val + q["base_s"] + j + 1)
                    sl = j % 4
                    if sl in tslot_w:
                        v.wait_ge(actp.h, tslot_w[sl])
                    ops_list = []
                    if use_abase:
                        ops_list.append(abase[:, 512 * q["h"]:512 * (q["h"] + 1)])
                    if mask_e is not None:
                        ops_list.append((amask if mode == "a" else bmask)
                                        [:, 512 * mask_e:512 * (mask_e + 1)])
                    for oi, opnd in enumerate(ops_list):
                        srcap = sps[sl][:] if oi == 0 else Tsb[sl][:]
                        inst = v.tensor_add(Tsb[sl][:], srcap, opnd)
                        if oi == len(ops_list) - 1:
                            inst.then_inc(dvea.h, 1)
                    tslot_w[sl] = actp.val + q["base_act"] + j + 1
                v.wait_ge(peav.h, peav.val + q["base_av"] + n)
                v.reciprocal(rcs[:], sums[:]).then_inc(dverc.h, 1)
                v.wait_ge(per.h, per.val + q["qcidx"] + 1)
                v.tensor_copy(Rsb[:], rps[:]).then_inc(aisw.h, 1)
                aisw.val += 1
                v.wait_ge(aisw.h, aisw.val)
                if q["qcidx"] >= 2:
                    v.wait_ge(sto.h, sto.val + 16 * (q["qcidx"] - 1))
                v.tensor_tensor(Osb[q["qcidx"] % 2][:], ops[q["qcidx"] % 2][:], Rsb[:],
                                op=ALU.mult).then_inc(dveo.h, 1)
                prep.emit_dve_n(v, 2)

        @blk.sync
        def _(sy):
            _gate(sy, ready)
            for q in qcs:
                sy.wait_ge(dveo.h, dveo.val + q["qcidx"] + 1)
                habs = q["h"]
                sy.dma_start(out=o_scr[128 * habs:128 * (habs + 1), 512 * q["c"]:512 * (q["c"] + 1)],
                             in_=Osb[q["qcidx"] % 2][:]).then_inc(sto.h, 16)


    p.pop_scope()
    pes.val += tot_s
    actp.val += tot_act
    dvea.val += tot_a
    peav.val += tot_av
    dverc.val += tot_qc
    per.val += tot_qc
    dveo.val += tot_qc
    sto.val += 16 * tot_qc
    sems = [pes, actp, dvea, peav, dverc, per, dveo, sto]
    return [(s, s.val) for s in sems]


def emit_qgather(p, q_tm, qidx_sb, bufs, cb, qst_sem, qst_val, tag, ready=(), stag="QG"):
    nc = p.nc
    p.push_scope()
    Qh = bufs["Qh"]
    gq = p.sem(stag + "gq")
    tq = p.sem(stag + "tq")
    cq = p.sem(stag + "cq")
    qg = [p.sbuf(tag + f"qg{k}", [128, 256], BF16) for k in range(2)]
    tpq = [p.psum(tag + f"tpq{k}", [128, 128], BF16) for k in range(2)]
    ident = cb["ident"]

    with nc.Block(tag) as blk:
        @blk.gpsimd
        def _(g_):
            _gate(g_, ready)
            g_.wait_ge(qst_sem.h, qst_val)
            from concourse.bass_core_rust import IndirectOffsetOnAxis
            for j in range(16):
                if j >= 2:
                    g_.wait_ge(tq.h, tq.val + 2 * (j - 1))
                g_.indirect_dma_start(
                    out=qg[j % 2][:, :], out_offset=None, in_=q_tm[:, :],
                    in_offset=IndirectOffsetOnAxis(ap=qidx_sb[:, j:j + 1], axis=0)) \
                    .then_inc(gq.h, 16)

        @blk.tensor
        def _(t):
            _gate(t, ready)
            for j in range(16):
                t.wait_ge(gq.h, gq.val + 16 * (j + 1))
                for u in range(2):
                    m = 2 * j + u
                    if m >= 2:
                        t.wait_ge(cq.h, cq.val + m - 1)
                    t.transpose(tpq[m % 2][:], qg[j % 2][:, 128 * u:128 * (u + 1)], ident[:]) \
                        .then_inc(tq.h, 1)

        @blk.vector
        def _(v):
            _gate(v, ready)
            for j in range(16):
                for u in range(2):
                    m = 2 * j + u
                    v.wait_ge(tq.h, tq.val + m + 1)
                    v.tensor_copy(out=Qh[u][:, 128 * j:128 * (j + 1)], in_=tpq[m % 2][:]) \
                        .then_inc(cq.h, 1)

    p.pop_scope()
    gq.val += 16 * 16
    tq.val += 32
    cq.val += 32
    return [(s, s.val) for s in (gq, tq, cq)]


def emit_oproj(p, Tq, mode, wdram, wb_scr_ap, o_scr, res_ap, x_dst, bufs, cb,
               prep, qidx_sb, tag, ready=(), stag="OP"):
    nc = p.nc
    p.push_scope()
    nt = Tq // 128
    wob = bufs["wob"]
    brows = bufs["brow_a" if mode == "a" else "brow_b"]
    onesb = cb["onesb"]

    wld = p.sem(stag + "wl")
    old_ = [p.sem(stag + "ol0"), p.sem(stag + "ol1")]
    xld = [p.sem(stag + "xl0"), p.sem(stag + "xl1")]
    peo = p.sem(stag + "po")
    dve = p.sem(stag + "dv")
    stx = p.sem(stag + "sx")

    otb = [p.sbuf(tag + f"otb{k}", [128, 2048], BF16) for k in range(2)]
    xrb = [p.sbuf(tag + f"xrb{k}", [128, 2048], F32) for k in range(2)]
    xnb = [p.sbuf(tag + f"xnb{k}", [128, 2048], F32) for k in range(2)]
    pso = [p.psum(tag + f"pso{k}", [128, 512]) for k in range(4)]

    with nc.Block(tag) as blk:
        @blk.gpsimd
        def _(g_):
            _gate(g_, ready)
            if mode == "a":
                w = wdram["aw_o"].ap()
                for dc in range(16):
                    g_.dma_start(out=wob[:, 2048 * dc:2048 * (dc + 1)],
                                 in_=w[128 * dc:128 * (dc + 1), :]).then_inc(wld.h, 16)
            if mode == "b":
                from concourse.bass_core_rust import IndirectOffsetOnAxis
                for i in range(nt):
                    if i >= 2:
                        g_.wait_ge(dve.h, dve.val + 4 * (i - 1))
                    g_.indirect_dma_start(
                        out=xrb[i % 2][:, :], out_offset=None, in_=res_ap[:, :],
                        in_offset=IndirectOffsetOnAxis(ap=qidx_sb[:, i:i + 1], axis=0)) \
                        .then_inc(xld[i % 2].h, 16)
            prep.emit(g_, 4)

        @blk.sync
        def _(sy):
            _gate(sy, ready)
            if mode == "b":
                sy.wait_ge(prep.st.h, 16 * prep.n_wb)
                for doc in range(4):
                    for dc in range(16):
                        sy.dma_start(out=wob[:, 2048 * dc + 512 * doc:2048 * dc + 512 * (doc + 1)],
                                     in_=wb_scr_ap[3 * 2048 + 512 * doc:3 * 2048 + 512 * (doc + 1),
                                                   128 * dc:128 * (dc + 1)],
                                     transpose=True).then_inc(wld.h, 16)
            for i in range(nt):
                if i >= 2:
                    sy.wait_ge(peo.h, peo.val + 4 * (i - 1))
                for dc in range(16):
                    sy.dma_start(out=otb[i % 2][:, 128 * dc:128 * (dc + 1)],
                                 in_=o_scr[128 * dc:128 * (dc + 1), 128 * i:128 * (i + 1)]) \
                        .then_inc(old_[i % 2].h, 16)

        @blk.scalar
        def _(sc):
            _gate(sc, ready)
            for i in range(nt):
                if mode == "a":
                    if i >= 2:
                        sc.wait_ge(dve.h, dve.val + 4 * (i - 1))
                    sc.dma_start(out=xrb[i % 2][:], in_=res_ap[128 * i:128 * (i + 1), :]) \
                        .then_inc(xld[i % 2].h, 16)
                if i >= 1:
                    sc.wait_ge(dve.h, dve.val + 4 * i)
                    sc.dma_start(out=x_dst[128 * (i - 1):128 * i, :], in_=xnb[(i - 1) % 2][:]) \
                        .then_inc(stx.h, 16)
            sc.wait_ge(dve.h, dve.val + 4 * nt)
            sc.dma_start(out=x_dst[128 * (nt - 1):128 * nt, :], in_=xnb[(nt - 1) % 2][:]) \
                .then_inc(stx.h, 16)
            sc.wait_ge(stx.h, stx.val + 16 * nt)

        @blk.tensor
        def _(t):
            _gate(t, ready)
            nw = 64 if mode == "b" else 16
            t.wait_ge(wld.h, wld.val + 16 * nw)
            for i in range(nt):
                sl = i % 2
                t.wait_ge(old_[sl].h, old_[sl].val + 16 * 16 * (i // 2 + 1))
                for doc in range(4):
                    if i >= 1:
                        t.wait_ge(dve.h, dve.val + 4 * (i - 1) + doc + 1)
                    for dc in range(16):
                        t.matmul(pso[doc][:], otb[sl][:, 128 * dc:128 * (dc + 1)],
                                 wob[:, 2048 * dc + 512 * doc:2048 * dc + 512 * (doc + 1)],
                                 start=(dc == 0), stop=False)
                    t.matmul(pso[doc][:], onesb[0:1, 0:128],
                             brows[3][0:1, 512 * doc:512 * (doc + 1)],
                             start=False, stop=True).then_inc(peo.h, 1)

        @blk.vector
        def _(v):
            _gate(v, ready)
            for i in range(nt):
                sl = i % 2
                v.wait_ge(xld[sl].h, xld[sl].val + 16 * (i // 2 + 1))
                if i >= 2:
                    v.wait_ge(stx.h, stx.val + 16 * (i - 1))
                for doc in range(4):
                    v.wait_ge(peo.h, peo.val + 4 * i + doc + 1)
                    v.tensor_add(xnb[sl][:, 512 * doc:512 * (doc + 1)], pso[doc][:],
                                 xrb[sl][:, 512 * doc:512 * (doc + 1)]).then_inc(dve.h, 1)

            prep.emit_dve_rest(v)
    p.pop_scope()
    wld.val += 16 * (64 if mode == "b" else 16)
    for i in range(nt):
        old_[i % 2].val += 16 * 16
        xld[i % 2].val += 16
    peo.val += 4 * nt
    dve.val += 4 * nt
    stx.val += 16 * nt
    sems = [wld, old_[0], old_[1], xld[0], xld[1], peo, dve, stx]
    return [(s, s.val) for s in sems]


UPTO_FFH = [False]


def emit_ffn(p, T, scr, xln_fm, xsrc, xdst, cb, prep, tag, ready=(), stag="FF"):
    nc = p.nc
    nH = T // 2048
    all_sems = []
    gate_scr, up_scr, down_scr, h_scr = (scr["gate_scr"].ap(), scr["up_scr"].ap(),
                                         scr["down_scr"].ap(), scr["h_scr"].ap())
    for HH in range(nH):
        tg = stag + "g"
        xhl = p.sem(tg + "xh")
        wldg = [p.sem(tg + "w0"), p.sem(tg + "w1")]
        peg = p.sem(tg + "pg")
        evg = p.sem(tg + "eg")
        evu = p.sem(tg + "eu")
        evh = p.sem(tg + "eh")
        sth = p.sem(tg + "sh")
        with p.scope():
            xh = p.sbuf(f"{tag}{HH}xh", [128, 16 * 2048], BF16)
            wgb = [p.sbuf(f"{tag}{HH}wg{k}", [128, 16 * 512], BF16) for k in range(2)]
            wub = [p.sbuf(f"{tag}{HH}wu{k}", [128, 16 * 512], BF16) for k in range(2)]
            gsb = [p.sbuf(f"{tag}{HH}gs{k}", [128, 2048], BF16) for k in range(2)]
            usb = p.sbuf(f"{tag}{HH}us", [128, 2048], BF16)
            hsb = [p.sbuf(f"{tag}{HH}hs{k}", [128, 2048], BF16) for k in range(2)]
            gps = p.psum(f"{tag}{HH}gps", [128, 2048])
            ups = p.psum(f"{tag}{HH}ups", [128, 2048])

            with nc.Block(f"{tag}{HH}g") as blk:
                @blk.sync
                def _(sy):
                    _gate(sy, ready)
                    if HH == 1:
                        pd = p.sem(stag + "d" + "sd")
                        dd = p.sem(stag + "d" + "dd")
                        sy.wait_ge(pd.h, pd.val)
                        sy.wait_ge(dd.h, dd.val)
                    if HH == 0:
                        sy.wait_ge(prep.st.h, 16 * prep.n_gateup)
                    for dc in range(16):
                        sy.dma_start(out=xh[:, 2048 * dc:2048 * (dc + 1)],
                                     in_=xln_fm[128 * dc:128 * (dc + 1),
                                                2048 * HH:2048 * (HH + 1)]) \
                            .then_inc(xhl.h, 16)
                    for ir in range(16):
                        if ir >= 2:
                            sy.wait_ge(peg.h, peg.val + 8 * (ir - 1))
                        for dc in range(16):
                            sy.dma_start(out=wgb[ir % 2][:, 512 * dc:512 * (dc + 1)],
                                         in_=gate_scr[512 * ir:512 * (ir + 1),
                                                      128 * dc:128 * (dc + 1)],
                                         transpose=True).then_inc(wldg[ir % 2].h, 16)
                        for dc in range(16):
                            sy.dma_start(out=wub[ir % 2][:, 512 * dc:512 * (dc + 1)],
                                         in_=up_scr[512 * ir:512 * (ir + 1),
                                                    128 * dc:128 * (dc + 1)],
                                         transpose=True).then_inc(wldg[ir % 2].h, 16)

                @blk.tensor
                def _(t):
                    _gate(t, ready)
                    t.wait_ge(xhl.h, xhl.val + 16 * 16)
                    for ic in range(64):
                        ir, u = ic // 4, ic % 4
                        if u == 0:
                            t.wait_ge(wldg[ir % 2].h, wldg[ir % 2].val + 16 * 32 * (ir // 2 + 1))
                        if ic >= 1:
                            t.wait_ge(evg.h, evg.val + ic)
                        for dc in range(16):
                            for tb in range(4):
                                inst = t.matmul(gps[:, 512 * tb:512 * (tb + 1)],
                                                wgb[ir % 2][:, 512 * dc + 128 * u:512 * dc + 128 * (u + 1)],
                                                xh[:, 2048 * dc + 512 * tb:2048 * dc + 512 * (tb + 1)],
                                                start=(dc == 0), stop=(dc == 15))
                        inst.then_inc(peg.h, 1)
                        if ic >= 1:
                            t.wait_ge(evu.h, evu.val + ic)
                        for dc in range(16):
                            for tb in range(4):
                                inst = t.matmul(ups[:, 512 * tb:512 * (tb + 1)],
                                                wub[ir % 2][:, 512 * dc + 128 * u:512 * dc + 128 * (u + 1)],
                                                xh[:, 2048 * dc + 512 * tb:2048 * dc + 512 * (tb + 1)],
                                                start=(dc == 0), stop=(dc == 15))
                        inst.then_inc(peg.h, 1)

                @blk.scalar
                def _(sc):
                    _gate(sc, ready)
                    for ic in range(64):
                        sc.wait_ge(peg.h, peg.val + 2 * ic + 1)
                        if ic >= 2:
                            sc.wait_ge(evh.h, evh.val + ic - 1)
                        sc.activation(out=gsb[ic % 2][:], in_=gps[:], func=AF.Silu,
                                      scale=cb["gsc"][:, ic:ic + 1]).then_inc(evg.h, 1)

                @blk.vector
                def _(v):
                    _gate(v, ready)
                    for ic in range(64):
                        v.wait_ge(peg.h, peg.val + 2 * ic + 2)
                        v.tensor_scalar_mul(usb[:], ups[:], cb["usc"][:, ic:ic + 1]) \
                            .then_inc(evu.h, 1)
                        v.wait_ge(evg.h, evg.val + ic + 1)
                        if ic >= 2:
                            v.wait_ge(sth.h, sth.val + 16 * (ic - 1))
                        v.tensor_mul(hsb[ic % 2][:], gsb[ic % 2][:], usb[:]) \
                            .then_inc(evh.h, 1)

                @blk.gpsimd
                def _(g_):
                    _gate(g_, ready)
                    for ic in range(64):
                        g_.wait_ge(evh.h, evh.val + ic + 1)
                        g_.dma_start(out=h_scr[128 * ic:128 * (ic + 1),
                                               2048 * HH:2048 * (HH + 1)],
                                     in_=hsb[ic % 2][:]).then_inc(sth.h, 16)

            xhl.val += 16 * 16
            for ir in range(16):
                wldg[ir % 2].val += 16 * 32
            peg.val += 128
            evg.val += 64
            evu.val += 64
            evh.val += 64
            sth.val += 16 * 64

        if UPTO_FFH[0] and HH == 0:
            break
        td = stag + "d"
        wldd = p.sem(td + "wl")
        hld = [p.sem(td + "h0"), p.sem(td + "h1")]
        xrl = [p.sem(td + "x0"), p.sem(td + "x1")]
        peB = p.sem(td + "pb")
        evA = p.sem(td + "ea")
        peT = p.sem(td + "pt")
        dvD = p.sem(td + "dd")
        stD = p.sem(td + "sd")
        passes = [(HF, tq) for HF in range(4) for tq in range(4)]
        with p.scope():
            wdb = [p.sbuf(f"{tag}{HH}wd{k}", [128, 64 * 512], BF16) for k in range(2)]
            hqb = [p.sbuf(f"{tag}{HH}hq{k}", [128, 8 * 512], BF16) for k in range(2)]
            fsb = [p.sbuf(f"{tag}{HH}fs{k}", [128, 2048], BF16) for k in range(2)]
            xrb = [p.sbuf(f"{tag}{HH}xr{k}", [128, 1024], F32) for k in range(2)]
            xo4 = [p.sbuf(f"{tag}{HH}xo{k}", [128, 1024], F32) for k in range(2)]
            qd = p.psum(f"{tag}{HH}qd", [128, 2048])
            tps = [p.psum(f"{tag}{HH}tp{k}", [128, 1024], BF16) for k in range(2)]
            gsem = (p.sem(tg + "pg"), p.sem(tg + "eg"), p.sem(tg + "eu"),
                    p.sem(tg + "eh"), p.sem(tg + "sh"))

            with nc.Block(f"{tag}{HH}dn") as blk:
                @blk.sync
                def _(sy):
                    sy.wait_ge(gsem[0].h, gsem[0].val)
                    sy.wait_ge(gsem[3].h, gsem[3].val)
                    sy.wait_ge(gsem[4].h, gsem[4].val)
                    if HH == 0 and tag == "F1":
                        sy.wait_ge(prep.st.h, 16 * prep.n_down)
                    gi = 0
                    for HF, tq in passes:
                        for icb in range(8):
                            if gi >= 2:
                                sy.wait_ge(peB.h, peB.val + gi - 1)
                            for k in range(8):
                                ic = 8 * icb + k
                                sy.dma_start(
                                    out=hqb[gi % 2][:, 512 * k:512 * (k + 1)],
                                    in_=h_scr[128 * ic:128 * (ic + 1),
                                              2048 * HH + 512 * tq:
                                              2048 * HH + 512 * (tq + 1)]) \
                                    .then_inc(hld[gi % 2].h, 16)
                            gi += 1

                @blk.tensor
                def _(t):
                    t.wait_ge(gsem[1].h, gsem[1].val)
                    t.wait_ge(gsem[2].h, gsem[2].val)

                    def transposes(pp):
                        HF, tq = passes[pp]
                        t.wait_ge(evA.h, evA.val + pp + 1)
                        for g in range(2):
                            gg = 2 * pp + g
                            if gg >= 2:
                                t.wait_ge(dvD.h, dvD.val + gg - 1)
                            for ttl in range(2):
                                for hc in range(4):
                                    inst = t.transpose(
                                        tps[gg % 2][:, 512 * ttl + 128 * hc:
                                                    512 * ttl + 128 * (hc + 1)],
                                        fsb[pp % 2][:, 512 * hc + 128 * (2 * g + ttl):
                                                    512 * hc + 128 * (2 * g + ttl) + 128],
                                        cb["ident"][:])
                            inst.then_inc(peT.h, 1)

                    for pp in range(16):
                        HF, tq = passes[pp]
                        if pp % 4 == 0:
                            t.wait_ge(wldd.h, wldd.val + 16 * 64 * (HF + 1))
                        if pp >= 1:
                            t.wait_ge(evA.h, evA.val + pp)
                        for icb in range(8):
                            gi = 8 * pp + icb
                            t.wait_ge(hld[gi % 2].h, hld[gi % 2].val + 16 * 8 * (gi // 2 + 1))
                            for k in range(8):
                                ic = 8 * icb + k
                                for hc in range(4):
                                    inst = t.matmul(
                                        qd[:, 512 * hc:512 * (hc + 1)],
                                        wdb[HF % 2][:, 512 * ic + 128 * hc:
                                                    512 * ic + 128 * (hc + 1)],
                                        hqb[gi % 2][:, 512 * k:512 * (k + 1)],
                                        start=(ic == 0), stop=(ic == 63))
                            inst.then_inc(peB.h, 1)
                        if pp >= 1:
                            transposes(pp - 1)
                    transposes(15)

                @blk.scalar
                def _(sc):
                    def wdb_load(HF):
                        if HF >= 2:
                            sc.wait_ge(peB.h, peB.val + 32 * (HF - 1))
                        for ic in range(64):
                            sc.dma_start(out=wdb[HF % 2][:, 512 * ic:512 * (ic + 1)],
                                         in_=down_scr[512 * HF:512 * (HF + 1),
                                                      128 * ic:128 * (ic + 1)],
                                         transpose=True).then_inc(wldd.h, 16)
                    wdb_load(0)
                    wdb_load(1)
                    for pp in range(16):
                        HF, tq = passes[pp]
                        if pp % 4 == 1 and 2 <= HF + 1 <= 3:
                            wdb_load(HF + 1)
                        for g in range(2):
                            gg = 2 * pp + g
                            if gg >= 2:
                                sc.wait_ge(dvD.h, dvD.val + gg - 1)
                            for ttl in range(2):
                                ttg = 4 * tq + 2 * g + ttl
                                row = 2048 * HH + 128 * ttg
                                sc.dma_start(out=xrb[gg % 2][:, 512 * ttl:512 * (ttl + 1)],
                                             in_=xsrc[row:row + 128,
                                                      512 * HF:512 * (HF + 1)]) \
                                    .then_inc(xrl[gg % 2].h, 16)
                        sc.wait_ge(peB.h, peB.val + 8 * (pp + 1))
                        for hc in range(4):
                            hcg = 4 * HF + hc
                            inst = sc.activation(out=fsb[pp % 2][:, 512 * hc:512 * (hc + 1)],
                                                 in_=qd[:, 512 * hc:512 * (hc + 1)],
                                                 func=AF.Identity,
                                                 scale=cb["dsc"][:, hcg:hcg + 1])
                        inst.then_inc(evA.h, 1)

                @blk.gpsimd
                def _(g_):
                    for pp in range(16):
                        HF, tq = passes[pp]
                        for g in range(2):
                            gg = 2 * pp + g
                            g_.wait_ge(dvD.h, dvD.val + gg + 1)
                            for ttl in range(2):
                                ttg = 4 * tq + 2 * g + ttl
                                row = 2048 * HH + 128 * ttg
                                g_.dma_start(out=xdst[row:row + 128,
                                                      512 * HF:512 * (HF + 1)],
                                             in_=xo4[gg % 2][:, 512 * ttl:512 * (ttl + 1)]) \
                                    .then_inc(stD.h, 16)
                    g_.wait_ge(stD.h, stD.val + 16 * 64)

                @blk.vector
                def _(v):
                    for pp in range(16):
                        for g in range(2):
                            gg = 2 * pp + g
                            v.wait_ge(peT.h, peT.val + gg + 1)
                            v.wait_ge(xrl[gg % 2].h, xrl[gg % 2].val + 16 * 2 * (gg // 2 + 1))
                            if gg >= 2:
                                v.wait_ge(stD.h, stD.val + 32 * (gg - 1))
                            v.tensor_add(xo4[gg % 2][:], tps[gg % 2][:], xrb[gg % 2][:]) \
                                .then_inc(dvD.h, 1)

            wldd.val += 16 * 64 * 4
            hld[0].val += 16 * 8 * 64
            hld[1].val += 16 * 8 * 64
            xrl[0].val += 16 * 2 * 16
            xrl[1].val += 16 * 2 * 16
            peB.val += 128
            evA.val += 16
            peT.val += 32
            dvD.val += 32
            stD.val += 16 * 64
        all_sems += [wldd, hld[0], hld[1], xrl[0], xrl[1], peB, evA, peT, dvD, stD]
    return [(s, s.val) for s in all_sems]


def build_host_tables(h_core):
    kk = np.arange(128)
    qq = np.arange(512)
    t = {}
    t["ident"] = np.eye(128).astype(BF16NP)
    t["onesb"] = np.ones((128, 512), BF16NP)
    t["onesf"] = np.ones((1, 128), np.float32)
    miscf = np.zeros((128, 8), np.float32)
    miscf[:, 0] = EPS
    miscf[:, 1] = -MC
    t["miscf"] = miscf
    ab = np.zeros((128, 7 * 512), np.float32)
    for h in range(7):
        ab[:, 512 * h:512 * (h + 1)] = SLOPES[h] * (kk[:, None] - qq[None, :])
    t["abase"] = ab
    am = np.zeros((128, 4 * 512), np.float32)
    for e in range(4):
        am[:, 512 * e:512 * (e + 1)] = np.where(128 * e + kk[:, None] > qq[None, :],
                                                -1.0e4, 0.0)
    t["amask"] = am
    cols = []
    bv_map, be_map = {}, {}
    for h in sorted(BIASVEC_HEADS):
        for d in range(32):
            bv_map[(h, d)] = len(cols)
            v = SLOPES[h] * (kk - 128.0 * d - 127.0)
            cols.append(np.maximum(v, CLIP) - MC)
    for h in range(H):
        if h in BIASVEC_HEADS:
            continue
        wt = head_window_tiles(h)
        emin = max(-28, 4 - wt)
        for e in range(emin, 4):
            be_map[(h, e)] = len(cols)
            cols.append(np.full(128, SLOPES[h] * 128.0 * e - MC))
    t["abt"] = np.stack(cols, axis=1).astype(np.float32)
    t["colmaps"] = {"bv": bv_map, "be": be_map}
    bm = np.zeros((128, 8 * 512), np.float32)
    s_of = qq // 128
    q128 = qq % 128
    for r in range(8):
        cond = (128 * r + kk[:, None]) > (128 * (h_core + 2 * s_of[None, :]) + q128[None, :])
        bm[:, 512 * r:512 * (r + 1)] = np.where(cond, -1.0e4, 0.0)
    t["bmask"] = bm
    qi = np.zeros((128, 16), np.uint32)
    for j in range(16):
        qi[:, j] = 128 * (2 * j + h_core) + np.arange(128)
    t["qidx"] = qi
    return t


def build_program(upto="full"):
    nc = bass.Bass("TRN2", target_bir_lowering=False)
    dram = {}
    dram["x_in"] = nc.dram_tensor("x_in", [S, D], F32, kind="ExternalInput")
    for n_ in ("aw_q", "aw_k", "aw_v", "aw_o", "wb_q", "wb_k", "wb_v", "wb_o"):
        dram[n_] = nc.dram_tensor(n_, [D, D], F32, kind="ExternalInput")
    dram["aa_b"] = nc.dram_tensor("aa_b", [4, D], F32, kind="ExternalInput")
    dram["ab_b"] = nc.dram_tensor("ab_b", [4, D], F32, kind="ExternalInput")
    dram["gate_w"] = nc.dram_tensor("gate_w", [II, D], I32, kind="ExternalInput")
    dram["up_w"] = nc.dram_tensor("up_w", [II, D], I32, kind="ExternalInput")
    dram["down_w"] = nc.dram_tensor("down_w", [D, II], I32, kind="ExternalInput")
    dram["gsc"] = nc.dram_tensor("gsc", [128, 64], F32, kind="ExternalInput")
    dram["usc"] = nc.dram_tensor("usc", [128, 64], F32, kind="ExternalInput")
    dram["dsc"] = nc.dram_tensor("dsc", [128, 16], F32, kind="ExternalInput")
    for n_, sh, dt_ in (("ident", [128, 128], BF16), ("onesb", [128, 512], BF16),
                        ("onesf", [1, 128], F32), ("miscf", [128, 8], F32),
                        ("abase", [128, 7 * 512], F32), ("amask", [128, 4 * 512], F32),
                        ("abt", [128, None], F32), ("bmask", [128, 8 * 512], F32),
                        ("qidx", [128, 16], U32)):
        pass
    NB = ABT_WIDTH[0]
    dram["ident"] = nc.dram_tensor("ident", [128, 128], BF16, kind="ExternalInput")
    dram["onesb"] = nc.dram_tensor("onesb", [128, 512], BF16, kind="ExternalInput")
    dram["onesf"] = nc.dram_tensor("onesf", [1, 128], F32, kind="ExternalInput")
    dram["miscf"] = nc.dram_tensor("miscf", [128, 8], F32, kind="ExternalInput")
    dram["abase"] = nc.dram_tensor("abase", [128, 7 * 512], F32, kind="ExternalInput")
    dram["amask"] = nc.dram_tensor("amask", [128, 4 * 512], F32, kind="ExternalInput")
    dram["abt"] = nc.dram_tensor("abt", [128, NB], F32, kind="ExternalInput")
    dram["bmask"] = nc.dram_tensor("bmask", [128, 8 * 512], F32, kind="ExternalInput")
    dram["qidx"] = nc.dram_tensor("qidx", [128, 16], U32, kind="ExternalInput")

    out = nc.dram_tensor("out", [2048, D], F32, kind="ExternalOutput")
    dbg = None
    if upto != "full":
        sh = [2048, D] if upto == "x3" else [S, D]
        dbg = nc.dram_tensor("dbg", sh, F32, kind="ExternalOutput")

    scr = {}
    scr["x1"] = nc.dram_tensor("x1", [S, D], F32, kind="Internal")
    scr["x2"] = nc.dram_tensor("x2", [S, D], F32, kind="Internal")
    scr["x3o"] = nc.dram_tensor("x3o", [2048, D], F32, kind="Internal")
    scr["xln_fm"] = nc.dram_tensor("xln_fm", [D, S], BF16, kind="Internal")
    scr["q_tm"] = nc.dram_tensor("q_tm", [S, 256], BF16, kind="Internal")
    scr["o_scr"] = nc.dram_tensor("o_scr", [D, S], BF16, kind="Internal")
    scr["gate_scr"] = nc.dram_tensor("gate_scr", [II, D], BF16, kind="Internal")
    scr["up_scr"] = nc.dram_tensor("up_scr", [II, D], BF16, kind="Internal")
    scr["down_scr"] = nc.dram_tensor("down_scr", [D, II], BF16, kind="Internal")
    scr["wb_scr"] = nc.dram_tensor("wb_scr", [4 * D, D], BF16, kind="Internal")
    scr["h_scr"] = nc.dram_tensor("h_scr", [II, S], BF16, kind="Internal")
    with ExitStack() as ctx:
        p = P(nc, ctx)
        cb = {}
        for n_ in ("ident", "onesb", "onesf", "miscf", "gsc", "usc", "dsc", "qidx"):
            shp = dram[n_].shape
            dt_ = dram[n_].dtype
            cb[n_] = p.sbuf("c_" + n_, list(shp), dt_)
        cld = p.sem("cld")
        with nc.Block("consts") as blk:
            @blk.sync
            def _(sy):
                for n_ in ("ident", "onesb", "onesf", "miscf", "gsc", "usc", "dsc", "qidx"):
                    sy.dma_start(out=cb[n_][:], in_=dram[n_].ap()).then_inc(cld.h, 16)
                    cld.val += 16
                sy.wait_ge(cld.h, cld.val)
        prep = WeightPrep(p, dram, scr, cb)
        done = [(cld, cld.val)]

        done = emit_ln(p, S, dram["x_in"].ap(), scr["xln_fm"].ap(), cb, "L1",
                       ready=done)
        s1scope = p.scope()
        s1scope.__enter__()
        brow_a, dbr = emit_brows(p, dram["aa_b"], "ba")
        done = done + dbr
        with p.scope():
            abufs = {
                "Qh": [p.sbuf(f"aQh{k}", [128, S], BF16) for k in range(2)],
                "Kh": [p.sbuf(f"aKh{k}", [128, S], BF16) for k in range(2)],
                "Vt": p.sbuf("aVt", [128, (S // 128) * 256], BF16),
                "xsb": [p.sbuf(f"axsb{k}", [128, 16 * 512], BF16) for k in range(2)],
                "wqs": p.sbuf("awqs", [128, 16 * 256], BF16),
                "wks": p.sbuf("awks", [128, 16 * 256], BF16),
                "wvs": p.sbuf("awvs", [128, 16 * 256], BF16),
                "brow_a": brow_a, "brow_b": brow_a,
                "abase": p.sbuf("c_abase", [128, 7 * 512], F32),
                "amask": p.sbuf("c_amask", [128, 4 * 512], F32),
                "abt": p.sbuf("c_abt", [128, ABT_WIDTH[0]], F32),
                "colmaps": COLMAPS[0],
            }
            acl = p.sem("acl")
            with nc.Block("aconsts") as blk:
                @blk.scalar
                def _(sc):
                    for n_ in ("abase", "amask", "abt"):
                        sc.dma_start(out=abufs[n_][:], in_=dram[n_].ap()).then_inc(acl.h, 16)
                        acl.val += 16
            done = done + [(acl, acl.val)]
            for g in range(8):
                dq = emit_qkv(p, S, g, "a", dram, None, scr["xln_fm"].ap(), None,
                              abufs, cb, prep, f"aq{g}", ready=done)
                done = emit_attn_inner(p, "a", g, abufs, cb, prep, scr["o_scr"].ap(),
                                       f"ai{g}", ready=dq)
        with p.scope():
            obufs = {"wob": p.sbuf("awob", [128, 16 * 2048], BF16),
                     "brow_a": brow_a, "brow_b": brow_a}
            done = emit_oproj(p, S, "a", dram, None, scr["o_scr"].ap(),
                              dram["x_in"].ap(), scr["x1"].ap(), obufs, cb, prep,
                              None, "ao", ready=done)
        s1scope.__exit__(None, None, None)

        def _finish(src_scr):
            fs = p.sem("fin_s")
            with nc.Block("fin") as blk:
                @blk.sync
                def _(sy):
                    _gate(sy, done)
                    if src_scr is not None:
                        n = dbg.shape[0] // 512
                        for i in range(n):
                            sy.dma_start(out=dbg.ap()[512 * i:512 * (i + 1), :],
                                         in_=src_scr.ap()[512 * i:512 * (i + 1), :]) \
                                .then_inc(fs.h, 16)
                        sy.wait_ge(fs.h, 16 * n)
                @blk.gpsimd
                def _(g_):
                    prep.emit_rest(g_)
                    _gate(g_, done)
                @blk.scalar
                def _(sc):
                    _gate(sc, done)
                @blk.vector
                def _(v):
                    prep.emit_dve_rest(v)
                    _gate(v, done)
                @blk.tensor
                def _(t):
                    _gate(t, done)

        if upto == "x1":
            _finish(scr["x1"])
            return nc

        done = emit_ln(p, S, scr["x1"].ap(), scr["xln_fm"].ap(), cb, "L2", ready=done)
        if upto == "ffh":
            UPTO_FFH[0] = True
        done = emit_ffn(p, S, scr, scr["xln_fm"].ap(), scr["x1"].ap(),
                        scr["x2"].ap(), cb, prep, "F1", ready=done)
        if upto == "ffh":
            UPTO_FFH[0] = False
            sems_ff = [p.sem("FFg" + nm) for nm in ("sh",)]
            done = done + [(s_, s_.val) for s_ in sems_ff]
            _finish2 = p.sem("fin_h")
            with nc.Block("finh") as blk:
                @blk.sync
                def _(sy):
                    _gate(sy, done)
                @blk.gpsimd
                def _(g_):
                    prep.emit_rest(g_)
                    _gate(g_, done)
                    for i in range(8):
                        g_.dma_start(out=dbg.ap()[512 * i:512 * (i + 1), :],
                                     in_=scr["h_scr"].ap()[512 * i:512 * (i + 1), 0:2048]) \
                            .then_inc(_finish2.h, 16)
                    g_.wait_ge(_finish2.h, 16 * 8)
                @blk.scalar
                def _(sc):
                    _gate(sc, done)
                @blk.vector
                def _(v):
                    prep.emit_dve_rest(v)
                    _gate(v, done)
                @blk.tensor
                def _(t):
                    _gate(t, done)
            return nc
        if upto == "x2":
            _finish(scr["x2"])
            return nc

        pf = p.sem("pflush")
        with nc.Block("pflush") as blk:
            @blk.gpsimd
            def _(g_):
                prep.emit_rest(g_)
                g_.wait_ge(prep.st.h, 16 * prep.total)
                g_.sem_inc(pf.h, 1)
            @blk.vector
            def _(v):
                prep.emit_dve_rest(v)
        pf.val += 1
        done = done + [(pf, pf.val)]
        done = emit_ln(p, S, scr["x2"].ap(), scr["xln_fm"].ap(), cb, "L3", ready=done)
        if upto == "ln3":
            _finish(scr["x2"])
            return nc
        s3scope = p.scope()
        s3scope.__enter__()
        brow_b, dbr2 = emit_brows(p, dram["ab_b"], "bb")
        done = done + dbr2
        with p.scope():
            bbufs = {
                "Qh": [p.sbuf(f"bQh{k}", [128, 2048], BF16) for k in range(2)],
                "Kh": [p.sbuf(f"bKh{k}", [128, S], BF16) for k in range(2)],
                "Vt": p.sbuf("bVt", [128, (S // 128) * 256], BF16),
                "xsb": [p.sbuf(f"bxsb{k}", [128, 16 * 512], BF16) for k in range(2)],
                "wqs": p.sbuf("bwqs", [128, 16 * 256], BF16),
                "wks": p.sbuf("bwks", [128, 16 * 256], BF16),
                "wvs": p.sbuf("bwvs", [128, 16 * 256], BF16),
                "brow_a": brow_b, "brow_b": brow_b,
                "bmask": p.sbuf("c_bmask", [128, 8 * 512], F32),
                "colmaps": None,
            }
            bcl = p.sem("bcl")
            with nc.Block("bconsts") as blk:
                @blk.scalar
                def _(sc):
                    sc.dma_start(out=bbufs["bmask"][:], in_=dram["bmask"].ap()) \
                        .then_inc(bcl.h, 16)
                    bcl.val += 16
            done = done + [(bcl, bcl.val)]
            nbg = 1 if upto == "x3g1" else 8
            for g in range(nbg):
                dq = emit_qkv(p, S, g, "b", dram, scr["wb_scr"].ap(),
                              scr["xln_fm"].ap(), scr["q_tm"].ap(), bbufs, cb, prep,
                              f"bq{g}", ready=done)
                qsts = p.sem("QKqs")
                dg = emit_qgather(p, scr["q_tm"].ap(), cb["qidx"], bbufs, cb,
                                  qsts, qsts.val, f"bg{g}", ready=dq)
                done = emit_attn_inner(p, "b", g, bbufs, cb, prep,
                                       scr["o_scr"].ap()[:, 0:2048], f"bi{g}",
                                       ready=dq + dg)
            if upto in ("x3g1", "x3g8"):
                _finish(scr["x2"])
                return nc
        with p.scope():
            obufs = {"wob": p.sbuf("bwob", [128, 16 * 2048], BF16),
                     "brow_a": brow_b, "brow_b": brow_b}
            done = emit_oproj(p, 2048, "b", dram, scr["wb_scr"].ap(),
                              scr["o_scr"].ap()[:, 0:2048], scr["x2"].ap(),
                              scr["x3o"].ap(), obufs, cb, prep, cb["qidx"], "bo",
                              ready=done)
        s3scope.__exit__(None, None, None)
        if upto == "x3":
            _finish(scr["x3o"])
            return nc

        done = emit_ln(p, 2048, scr["x3o"].ap(), scr["xln_fm"].ap(), cb, "L4",
                       ready=done)
        done = emit_ffn(p, 2048, scr, scr["xln_fm"].ap(), scr["x3o"].ap(),
                        out.ap(), cb, prep, "F2", ready=done)

        _finish(None)
    return nc


ABT_WIDTH = [None]
COLMAPS = [None]


def make_core_inputs(inputs, core):
    b, h = core // 2, core % 2
    t = build_host_tables(h)
    fs = II // 128
    im = {
        "x_in": np.ascontiguousarray(inputs["L0_attn_a_x"][b]),
        "aw_q": inputs["L0_attn_a_wq"], "aw_k": inputs["L0_attn_a_wk"],
        "aw_v": inputs["L0_attn_a_wv"], "aw_o": inputs["L0_attn_a_wo"],
        "wb_q": inputs["L0_attn_b_w_q"], "wb_k": inputs["L0_attn_b_w_k"],
        "wb_v": inputs["L0_attn_b_w_v"], "wb_o": inputs["L0_attn_b_w_o"],
        "aa_b": np.ascontiguousarray(np.stack([inputs["L0_attn_a_bq"], inputs["L0_attn_a_bk"],
                                               inputs["L0_attn_a_bv"], inputs["L0_attn_a_bo"]])),
        "ab_b": np.ascontiguousarray(np.stack([inputs["L0_attn_b_b_q"], inputs["L0_attn_b_b_k"],
                                               inputs["L0_attn_b_b_v"], inputs["L0_attn_b_b_o"]])),
        "gate_w": inputs["L0_ffn_gate_w"], "up_w": inputs["L0_ffn_up_w"],
        "down_w": inputs["L0_ffn_down_w"],
        "gsc": np.ascontiguousarray(inputs["L0_ffn_gate_scale"].reshape(fs, 128).T),
        "usc": np.ascontiguousarray(inputs["L0_ffn_up_scale"].reshape(fs, 128).T),
        "dsc": np.ascontiguousarray(inputs["L0_ffn_down_scale"].reshape(16, 128).T),
        "ident": t["ident"], "onesb": t["onesb"], "onesf": t["onesf"],
        "miscf": t["miscf"], "abase": t["abase"], "amask": t["amask"],
        "abt": t["abt"], "bmask": t["bmask"], "qidx": t["qidx"],
    }
    return im


from concourse.bass_utils import run_bass_kernel_spmd

_tables0 = build_host_tables(0)
ABT_WIDTH[0] = _tables0["abt"].shape[1]
COLMAPS[0] = _tables0["colmaps"]


def kernel(**inputs):
    nc = build_program("full")
    in_maps = [make_core_inputs(inputs, c) for c in range(8)]
    res = run_bass_kernel_spmd(nc, in_maps, core_ids=list(range(8)))
    out = np.empty((4, S, D), np.float32)
    for c in range(8):
        b, h = c // 2, c % 2
        o = res.results[c]["out"]
        for j in range(16):
            out[b, 128 * (2 * j + h):128 * (2 * j + h) + 128, :] = o[128 * j:128 * (j + 1), :]
    return out
